# version 06
# baseline (speedup 1.0000x reference)
import numpy as np
import ml_dtypes
from contextlib import ExitStack
import concourse.bass as bass
import concourse.mybir as mybir
from concourse.bass_utils import run_bass_kernel_spmd

F32 = mybir.dt.float32
BF16 = mybir.dt.bfloat16
AF = mybir.ActivationFunctionType
ALU = mybir.AluOpType
AXX = mybir.AxisListType.X



class Buf:
    __slots__ = ("name", "w", "rs")

    def __init__(self, name=""):
        self.name = name
        self.w = []
        self.rs = []


class Stream:
    def __init__(self, name):
        self.name = name
        self.records = []
        self.max_waited = {}


class Q:
    def __init__(self, stream, sem, step=1, name=""):
        self.stream = stream
        self.sem = sem
        self.step = step
        self.n = 0
        self.name = name
        self.pend_w = []
        self.pend_r = []

    def emit(self, build, reads=(), writes=(), inc=True):
        need = {}
        if self.step == 16 and self.n > 0:
            need[self] = self.n
        for b in reads:
            for (q, v) in b.w:
                if need.get(q, -1) < v:
                    need[q] = v
        for b in writes:
            for (q, v) in b.w:
                if need.get(q, -1) < v:
                    need[q] = v
            for (q, v) in b.rs:
                if need.get(q, -1) < v:
                    need[q] = v
        waits = []
        st = self.stream
        for q, v in need.items():
            assert v <= q.n, f"wait on unproduced token {q.name}:{v}"
            if st.max_waited.get(q, -1) < v:
                waits.append((q.sem, v))
                st.max_waited[q] = v
        st.records.append((waits, build, (self.sem, self.step) if inc else None))
        for b in writes:
            self.pend_w.append(b)
        for b in reads:
            self.pend_r.append(b)
        if inc:
            self.n += self.step
            tok = (self, self.n)
            for b in self.pend_w:
                b.w.append(tok)
                b.rs = []
            for b in self.pend_r:
                b.rs.append(tok)
            self.pend_w = []
            self.pend_r = []

    def seal(self):
        if self.pend_w or self.pend_r:
            raise RuntimeError(f"{self.name}: pending accesses without an inc")

    def token(self):
        self.seal()
        return (self, self.n)


class DmaQ:
    def __init__(self, stream, sems, name):
        self.subqs = [Q(stream, s, 16, f"{name}{i}") for i, s in enumerate(sems)]
        self.i = 0
        self.name = name

    def emit(self, build, reads=(), writes=()):
        q = self.subqs[self.i]
        self.i = (self.i + 1) % len(self.subqs)
        return q.emit(build, reads, writes, inc=True)

    def qs(self):
        return list(self.subqs)


class Prog:
    def __init__(self, nc, ctx, n_dsync=12, n_dact=10, n_dpool=6):
        self.nc = nc
        self.streams = {n: Stream(n) for n in
                        ("tensor", "vector", "scalar", "gpsimd", "sync")}

        def sem(name):
            return ctx.enter_context(nc.semaphore(name))
        self.pe = Q(self.streams["tensor"], sem("q_pe"), 1, "pe")
        self.dve = Q(self.streams["vector"], sem("q_dve"), 1, "dve")
        self.act = Q(self.streams["scalar"], sem("q_act"), 1, "act")
        self.pool = Q(self.streams["gpsimd"], sem("q_pool"), 1, "pool")
        self.dsync = DmaQ(self.streams["sync"],
                          [sem(f"q_dsync{i}") for i in range(n_dsync)], "dsync")
        self.dact = DmaQ(self.streams["scalar"],
                         [sem(f"q_dact{i}") for i in range(n_dact)], "dact")
        self.dpool = DmaQ(self.streams["gpsimd"],
                          [sem(f"q_dpool{i}") for i in range(n_dpool)], "dpool")
        self.cc = Q(self.streams["gpsimd"], sem("q_cc"), 1, "cc")
        self.all_qs = ([self.pe, self.dve, self.act, self.pool, self.cc]
                       + self.dsync.qs() + self.dact.qs() + self.dpool.qs())

    def barrier(self, qs=None):
        qs = qs or self.all_qs
        toks = [q.token() for q in qs]
        for q in qs:
            st = q.stream
            waits = []
            for (q2, v) in toks:
                if v > 0 and st.max_waited.get(q2, -1) < v:
                    waits.append((q2.sem, v))
                    st.max_waited[q2] = v
            if waits:
                st.records.append((waits, None, None))

    def compute_fence(self):
        self.barrier([self.pe, self.dve, self.act])

    def replay(self, block):
        eng_of = {"tensor": block.tensor, "vector": block.vector,
                  "scalar": block.scalar, "gpsimd": block.gpsimd,
                  "sync": block.sync}

        def mk(stream):
            def body(eng):
                for waits, build, inc in stream.records:
                    if build is None:
                        for (sem, val) in waits:
                            eng.wait_ge(sem, val)
                        continue
                    for (sem, val) in waits[:-1]:
                        eng.wait_ge(sem, val)
                    inst = build(eng)
                    if waits:
                        inst._wait_ge(waits[-1][0], waits[-1][1])
                    if inc is not None:
                        inst.then_inc(inc[0], inc[1])
            return body
        for name, stream in self.streams.items():
            if stream.records:
                eng_of[name](mk(stream))


TPC = 2048
HALF = 1024
NSUB = 4
NFF = 256
NDC = 8
NEXP = 16
SEQ = 8192
NKT = 64
NQT = 16
EU = 2048
LN_EPS = 1e-5
INV_L = 1.0 / 8192.0


def build_nc(stop_after=None):
    nc = bass.Bass("TRN2", target_bir_lowering=False)
    dt = nc.dram_tensor
    x_d = dt("x", [1024, TPC], F32, kind="ExternalInput").ap()
    expw_d = dt("expw", [NEXP, 128, 12288], F32, kind="ExternalInput").ap()
    winU_d = dt("winU", [8, 128, 2048], F32, kind="ExternalInput").ap()
    winV_d = dt("winV", [8, 128, 2048], F32, kind="ExternalInput").ap()
    winZ_d = dt("winZ", [8, 128, 128], F32, kind="ExternalInput").ap()
    wout_d = dt("wout", [128, 16384], F32, kind="ExternalInput").ap()
    wr_d = dt("wr", [8, 128, 16], F32, kind="ExternalInput").ap()
    bu_d = dt("bu", [1, 2048], F32, kind="ExternalInput").ap()
    bv_d = dt("bv", [1, 2048], F32, kind="ExternalInput").ap()
    bz_d = dt("bz", [128, 1], F32, kind="ExternalInput").ap()
    lng_d = dt("lng", [128, 8], F32, kind="ExternalInput").ap()
    lnb_d = dt("lnb", [128, 8], F32, kind="ExternalInput").ap()
    bout_d = dt("bout", [128, 8], F32, kind="ExternalInput").ap()
    gq_d = dt("gq", [128, 1], F32, kind="ExternalInput").ap()
    bq_d = dt("bq", [128, 1], F32, kind="ExternalInput").ap()
    gk_d = dt("gk", [128, 1], F32, kind="ExternalInput").ap()
    bk_d = dt("bk", [128, 1], F32, kind="ExternalInput").ap()
    rb_d = dt("rb", [1, 16], F32, kind="ExternalInput").ap()
    ident_d = dt("ident", [128, 128], BF16, kind="ExternalInput").ap()
    onesc_d = dt("onesc", [128, 1], BF16, kind="ExternalInput").ap()
    ones1b_d = dt("ones1b", [1, 128], BF16, kind="ExternalInput").ap()
    ones1f_d = dt("ones1f", [1, 128], F32, kind="ExternalInput").ap()
    ones11_d = dt("ones11", [1, 1], F32, kind="ExternalInput").ap()
    out_d = dt("out", [1024, TPC], F32, kind="ExternalOutput").ap()
    Xd = dt("Xd", [1024, TPC], F32).ap()
    u_dram = dt("u_dram", [TPC, EU], BF16).ap()
    vown_d = dt("vown", [TPC, EU], BF16).ap()
    vall = [dt(f"vall{i}", [4 * HALF, EU], BF16).ap() for i in range(2)]
    kown_d = dt("kown", [128, TPC], BF16).ap()
    kall_d = dt("kall", [4 * 128, TPC], BF16).ap()
    cb_dram = dt("cb_dram", [NEXP, TPC], BF16).ap()
    rows_d = dt("rows_d", [2, TPC], BF16).ap()
    rowsb_d = dt("rowsb_d", [2, HALF], F32).ap()

    Xd_B = Buf("Xd")
    out_B = Buf("outd")
    ud_B = Buf("u_dram")
    vown_B = [Buf("vown0"), Buf("vown1")]
    kown_B = Buf("kown")
    kall_B = Buf("kall")
    vall_B = [Buf("vall0"), Buf("vall1")]
    cb_B = Buf("cb_dram")
    rowsd_B = Buf("rows_d")
    rowsb_B = Buf("rowsb_d")

    ctx = ExitStack()
    with ctx:
        p = Prog(nc, ctx)
        PE, V, A, PL = p.pe, p.dve, p.act, p.pool

        def sb_in(scope, name, shape, dtype):
            t = scope.enter_context(nc.sbuf_tensor(name, shape, dtype))
            return t, Buf(name)

        sb = lambda nm, sh, dtt: sb_in(ctx, nm, sh, dtt)

        ident, ident_b = sb("c_ident", [128, 128], BF16)
        onesc, onesc_b = sb("c_onesc", [128, 1], BF16)
        ones1b, ones1b_b = sb("c_ones1b", [1, 128], BF16)
        ones1f, ones1f_b = sb("c_ones1f", [1, 128], F32)
        ones11, ones11_b = sb("c_ones11", [1, 1], F32)
        bz, bz_b = sb("c_bz", [128, 1], F32)
        lng, lng_b = sb("c_lng", [128, 8], F32)
        lnb, lnb_b = sb("c_lnb", [128, 8], F32)
        bout, bout_b = sb("c_bout", [128, 8], F32)
        gqv, gqv_b = sb("gqv", [128, 1], F32)
        bqv, bqv_b = sb("bqv", [128, 1], F32)
        gkv, gkv_b = sb("gkv", [128, 1], F32)
        bkv, bkv_b = sb("bkv", [128, 1], F32)
        rb_bc, rb_bc_b = sb("c_rb_bc", [128, 16], F32)
        cw_bc, cw_bc_b = sb("c_cw_bc", [128, 16], F32)
        zq_sb, zq_sb_b = sb("c_zq_sb", [128, TPC], BF16)
        wr_sb, wr_sb_b = sb("c_wr_sb", [128, 8 * 16], F32)
        eps_t, eps_b = sb("c_eps", [1, 1], F32)
        V.emit(lambda e: e.memset(eps_t[:], LN_EPS), [], [eps_b])


        for dst, dst_b, src in [
                (ident, ident_b, ident_d), (onesc, onesc_b, onesc_d),
                (ones1b, ones1b_b, ones1b_d), (ones1f, ones1f_b, ones1f_d),
                (ones11, ones11_b, ones11_d), (bz, bz_b, bz_d),
                (lng, lng_b, lng_d), (lnb, lnb_b, lnb_d), (bout, bout_b, bout_d),
                (gqv, gqv_b, gq_d), (bqv, bqv_b, bq_d), (gkv, gkv_b, gk_d),
                (bkv, bkv_b, bk_d)]:
            p.dsync.emit(lambda e, dst=dst, src=src: e.dma_start(out=dst[:], in_=src),
                         [], [dst_b])
        p.dsync.emit(lambda e: e.dma_start(
            out=wr_sb[:].rearrange("p (dc j) -> p dc j", dc=8),
            in_=wr_d.rearrange("dc p j -> p dc j")), [], [wr_sb_b])
        p.dsync.emit(lambda e: e.dma_start(out=rb_bc[:], in_=rb_d.partition_broadcast(128)),
                     [], [rb_bc_b])

        with ExitStack() as cws:
            cw_ps = cws.enter_context(nc.psum_tensor("cw_ps", [1, 16], F32))
            cw_ps_b = Buf("cw_ps")
            onesf_t, onesf_b = sb_in(cws, "onesf128", [128, 1], F32)
            V.emit(lambda e: e.memset(onesf_t[:], 1.0), [], [onesf_b])
            for dc in range(8):
                PE.emit(lambda e, dc=dc: e.matmul(
                    cw_ps[:], lhsT=onesf_t[:], rhs=wr_sb[:, dc * 16:(dc + 1) * 16],
                    start=(dc == 0), stop=(dc == 7)),
                    [onesf_b, wr_sb_b], [cw_ps_b], inc=(dc == 7))
            cw_row, cw_row_b = sb_in(cws, "cw_row", [1, 16], F32)
            V.emit(lambda e: e.tensor_copy(out=cw_row[:], in_=cw_ps[:]),
                   [cw_ps_b], [cw_row_b])
            cwb_ps = cws.enter_context(nc.psum_tensor("cwb_ps", [128, 16], F32))
            cwb_b = Buf("cwb_ps")
            PE.emit(lambda e: e.matmul(cwb_ps[:], lhsT=ones1f[:], rhs=cw_row[:],
                                       start=True, stop=True),
                    [ones1f_b, cw_row_b], [cwb_b])
            V.emit(lambda e: e.tensor_copy(out=cw_bc[:], in_=cwb_ps[:]),
                   [cwb_b], [cw_bc_b])
        p.barrier()

        def ln_stats_rows(tag, rows, X, X_b, X_cells=None):
            (m_row, m_row_b) = rows["m"]
            (var_row, var_row_b) = rows["var"]
            (rstd, rstd_b) = rows["rstd"]
            (scr, scr_b) = rows["scr"]
            scr16 = rows["scr16"]
            with ExitStack() as ps_scope:
                stats_ps = ps_scope.enter_context(
                    nc.psum_tensor(tag + "stats", [1, 2 * HALF], F32))
                stats_b = Buf(tag + "stats")
                for dc in range(NDC):
                    tb, tb_b = scr16[dc % 2]
                    ts, ts_b = scr16[2 + dc % 2]
                    xsl = X[:, dc * HALF:(dc + 1) * HALF]
                    xdep = [X_cells[dc]] if X_cells else [X_b]
                    V.emit(lambda e, tb=tb, xsl=xsl: e.tensor_copy(out=tb[:], in_=xsl),
                           xdep, [tb_b])
                    A.emit(lambda e, ts=ts, xsl=xsl: e.activation(
                        out=ts[:], in_=xsl, func=AF.Square), xdep, [ts_b])
                    for ns in range(2):
                        PE.emit(lambda e, tb=tb, ns=ns, dc=dc: e.matmul(
                            stats_ps[0:1, ns * 512:(ns + 1) * 512],
                            lhsT=onesc[:], rhs=tb[:, ns * 512:(ns + 1) * 512],
                            start=(dc == 0), stop=(dc == NDC - 1)),
                            [tb_b, onesc_b], [stats_b], inc=False)
                        PE.emit(lambda e, ts=ts, ns=ns, dc=dc: e.matmul(
                            stats_ps[0:1, HALF + ns * 512:HALF + (ns + 1) * 512],
                            lhsT=onesc[:], rhs=ts[:, ns * 512:(ns + 1) * 512],
                            start=(dc == 0), stop=(dc == NDC - 1)),
                            [ts_b, onesc_b], [stats_b],
                            inc=(ns == 1))
                A.emit(lambda e: e.mul(m_row[:], stats_ps[0:1, 0:HALF], 1.0 / 1024.0),
                       [stats_b], [m_row_b])
                A.emit(lambda e: e.activation(out=scr[:], in_=m_row[:], func=AF.Square),
                       [m_row_b], [scr_b])
                V.emit(lambda e: e.scalar_tensor_tensor(
                    out=var_row[:], in0=stats_ps[0:1, HALF:2 * HALF],
                    scalar=1.0 / 1024.0, in1=scr[:],
                    op0=ALU.mult, op1=ALU.subtract), [stats_b, scr_b], [var_row_b])
                A.emit(lambda e: e.activation(out=scr[:], in_=var_row[:], func=AF.Ln,
                                              bias=eps_t[0:1, 0:1]),
                       [var_row_b, eps_b], [scr_b])
                A.emit(lambda e: e.activation(out=rstd[:], in_=scr[:], func=AF.Exp,
                                              scale=-0.5), [scr_b], [rstd_b])
            p.compute_fence()

        def ffn_pass(pass_tag, src, src_B, dst, dst_B):
            src_r = src.rearrange("(dc p) t -> p dc t", p=128)
            dst_r = dst.rearrange("(dc p) t -> p dc t", p=128)
            with ExitStack() as fs:
                sbl = lambda nm, sh, dtt: sb_in(fs, pass_tag + nm, sh, dtt)
                X, X_b = sbl("X", [128, NDC * HALF], F32)
                X_cells = [Buf(f"{pass_tag}Xc{dc}") for dc in range(NDC)]
                xn, _xn_b = sbl("xn", [128, NDC * HALF], BF16)
                xn_bs = [Buf(f"{pass_tag}xn{dc}") for dc in range(NDC)]
                wslot = [sbl(f"w{i}", [128, 12288], BF16) for i in range(3)]
                cbslot = [sbl(f"cb{i}", [128, HALF], BF16) for i in range(3)]
                gsil = [sbl(f"gs{i}", [128, 4 * NFF], BF16) for i in range(2)]
                hbuf = [sbl(f"h{i}", [128, 4 * NFF], BF16) for i in range(2)]
                rows = {
                    "m": sbl("m_row", [1, HALF], F32),
                    "var": sbl("var_row", [1, HALF], F32),
                    "rstd": sbl("rstd", [1, HALF], F32),
                    "scr": sbl("scrrow", [1, HALF], F32),
                    "scr16": [sbl(f"scr16_{i}", [128, HALF], BF16) for i in range(4)],
                }
                negm, negm_b = sbl("negm", [1, HALF], F32)
                tokv, tokv_b = sbl("tokv", [128, 16], F32)
                xt = [sbl(f"xt{i}", [128, HALF], F32) for i in range(2)]
                rbc_sb, rbc_b = sbl("rbc", [128, HALF], F32)
                nbc_sb, nbc_b = sbl("nbc", [128, HALF], F32)
                rt = {nm: sbl(nm, [128, 128], F32) for nm in
                      ("lgr", "logit", "score", "sch", "emask", "mk0", "mkw",
                       "eqm", "sel", "wraw")}
                pr, pr_b = sbl("pr", [128, 32 * 6], F32)
                gsum, gsum_b = sbl("gsum", [128, 32], F32)
                gmin, gmin_b = sbl("gmin", [128, 8 * 6], F32)
                thr2, thr2_b = sbl("thr2", [128, 8], F32)
                gmask, gmask_b = sbl("gmask", [128, 32], F32)
                mx, mx_b = sbl("mx", [128, 8], F32)
                wsum, wsum_b = sbl("wsum", [128, 8], F32)
                winv, winv_b = sbl("winv", [128, 8], F32)
                cbt, cbt_b = sbl("cbt", [128, 128], BF16)
                cbst, cbst_b = sbl("cbst", [16, 1024], BF16)

                def load_w(e_i):
                    wt, wt_b = wslot[e_i % 3]
                    p.dpool.emit(lambda e, wt=wt, e_i=e_i: e.dma_start(
                        out=wt[:], in_=expw_d[e_i]), [], [wt_b])

                def do_half(half):
                    tag = f"{pass_tag}h{half}_"
                    for dc in range(NDC):
                        p.dsync.emit(lambda e, half=half, dc=dc: e.dma_start(
                            out=X[:, dc * HALF:(dc + 1) * HALF],
                            in_=src_r[:, dc, half * HALF:(half + 1) * HALF]),
                            [src_B] if src_B else [], [X_cells[dc], X_b])
                    load_w(0)
                    load_w(1)
                    (m_row, m_row_b) = rows["m"]
                    (rstd, rstd_b) = rows["rstd"]
                    ln_stats_rows(tag, rows, X, X_b, X_cells)
                    V.emit(lambda e: e.scalar_tensor_tensor(
                        out=negm[:], in0=m_row[:], scalar=-1.0, in1=rstd[:],
                        op0=ALU.mult, op1=ALU.mult), [m_row_b, rstd_b], [negm_b])

                    p.dact.emit(lambda e: e.dma_start(out=rowsb_d[0:1, :], in_=rstd[:]),
                                [rstd_b], [rowsb_B])
                    p.dact.emit(lambda e: e.dma_start(out=rowsb_d[1:2, :], in_=negm[:]),
                                [negm_b], [rowsb_B])
                    p.dsync.emit(lambda e: e.dma_start(
                        out=rbc_sb[:], in_=rowsb_d[0:1, :].partition_broadcast(128)),
                        [rowsb_B], [rbc_b])
                    p.dsync.emit(lambda e: e.dma_start(
                        out=nbc_sb[:], in_=rowsb_d[1:2, :].partition_broadcast(128)),
                        [rowsb_B], [nbc_b])
                    for dc in range(NDC):
                        t, t_b = xt[dc % 2]
                        V.emit(lambda e, t=t, dc=dc: e.tensor_mul(
                            out=t[:], in0=X[:, dc * HALF:(dc + 1) * HALF],
                            in1=rbc_sb[:]), [X_cells[dc], rbc_b], [t_b])
                        V.emit(lambda e, t=t, dc=dc: e.tensor_add(
                            out=xn[:, dc * HALF:(dc + 1) * HALF], in0=t[:],
                            in1=nbc_sb[:]), [t_b, nbc_b], [xn_bs[dc]])

                    with ExitStack() as ps2:
                        tok_ps = ps2.enter_context(
                            nc.psum_tensor(tag + "tokps", [128, 16], F32))
                        tok_ps_b = Buf(tag + "tokps")
                        for tt in range(8):
                            PE.emit(lambda e, tt=tt: e.matmul(
                                tok_ps[:, tt:tt + 1],
                                lhsT=rstd[0:1, tt * 128:(tt + 1) * 128],
                                rhs=ones11[:], start=True, stop=True),
                                [rstd_b, ones11_b], [tok_ps_b], inc=False)
                            PE.emit(lambda e, tt=tt: e.matmul(
                                tok_ps[:, 8 + tt:9 + tt],
                                lhsT=negm[0:1, tt * 128:(tt + 1) * 128],
                                rhs=ones11[:], start=True, stop=True),
                                [negm_b, ones11_b], [tok_ps_b], inc=(tt == 7))
                        V.emit(lambda e: e.tensor_copy(out=tokv[:], in_=tok_ps[:]),
                               [tok_ps_b], [tokv_b])

                        lg_ps = ps2.enter_context(
                            nc.psum_tensor(tag + "lg", [128, 8 * 16], F32))
                        lg_b = Buf(tag + "lg")
                        for tt in range(8):
                            for dc in range(NDC):
                                PE.emit(lambda e, tt=tt, dc=dc: e.matmul(
                                    lg_ps[:, tt * 16:(tt + 1) * 16],
                                    lhsT=X[:, dc * HALF + tt * 128:
                                           dc * HALF + (tt + 1) * 128],
                                    rhs=wr_sb[:, dc * 16:(dc + 1) * 16],
                                    start=(dc == 0), stop=(dc == NDC - 1)),
                                    [X_cells[dc], wr_sb_b], [lg_b],
                                    inc=(dc == NDC - 1 and tt == 7))

                        r3 = lambda t: t[:].rearrange("p (tt j) -> p tt j", tt=8)
                        lgr, lgr_b = rt["lgr"]
                        logit, logit_b = rt["logit"]
                        score, score_b = rt["score"]
                        sch, sch_b = rt["sch"]
                        emask, emask_b = rt["emask"]
                        mk0, mk0_b = rt["mk0"]
                        mkw, mkw_b = rt["mkw"]
                        eqm, eqm_b = rt["eqm"]
                        sel, sel_b = rt["sel"]
                        wraw, wraw_b = rt["wraw"]
                        V.emit(lambda e: e.tensor_tensor(
                            out=r3(lgr),
                            in0=lg_ps[:].rearrange("p (tt j) -> p tt j", tt=8),
                            in1=tokv[:, 0:8].unsqueeze(2).to_broadcast([128, 8, 16]),
                            op=ALU.mult), [lg_b, tokv_b], [lgr_b])
                        V.emit(lambda e: e.tensor_tensor(
                            out=r3(logit),
                            in0=cw_bc[:].unsqueeze(1).to_broadcast([128, 8, 16]),
                            in1=tokv[:, 8:16].unsqueeze(2).to_broadcast([128, 8, 16]),
                            op=ALU.mult), [cw_bc_b, tokv_b], [logit_b])
                        V.emit(lambda e: e.tensor_add(out=logit[:], in0=logit[:],
                                                      in1=lgr[:]),
                               [lgr_b, logit_b], [logit_b])
                        A.emit(lambda e: e.activation(out=score[:], in_=logit[:],
                                                      func=AF.Sigmoid),
                               [logit_b], [score_b])
                        V.emit(lambda e: e.tensor_tensor(
                            out=r3(sch), in0=r3(score),
                            in1=rb_bc[:].unsqueeze(1).to_broadcast([128, 8, 16]),
                            op=ALU.add), [score_b, rb_bc_b], [sch_b])
                        schr = r3(sch)
                        el = [schr[:, :, j:16:4] for j in range(4)]
                        prk = lambda k: pr[:, k * 32:(k + 1) * 32].rearrange(
                            "p (tt g) -> p tt g", tt=8)
                        pairs = [(0, 1), (0, 2), (0, 3), (1, 2), (1, 3), (2, 3)]
                        for k, (i, j) in enumerate(pairs):
                            V.emit(lambda e, k=k, i=i, j=j: e.tensor_add(
                                out=prk(k), in0=el[i], in1=el[j]), [sch_b], [pr_b])
                        gsr = gsum[:].rearrange("p (tt g) -> p tt g", tt=8)
                        V.emit(lambda e: e.tensor_max(out=gsr, in0=prk(0), in1=prk(1)),
                               [pr_b], [gsum_b])
                        for k in range(2, 6):
                            V.emit(lambda e, k=k: e.tensor_max(
                                out=gsr, in0=gsr, in1=prk(k)),
                                [pr_b, gsum_b], [gsum_b])
                        for k, (i, j) in enumerate(pairs):
                            V.emit(lambda e, k=k, i=i, j=j: e.tensor_tensor(
                                out=gmin[:, k * 8:(k + 1) * 8].unsqueeze(2),
                                in0=gsr[:, :, i:i + 1], in1=gsr[:, :, j:j + 1],
                                op=ALU.min), [gsum_b], [gmin_b])
                        V.emit(lambda e: e.tensor_max(
                            out=thr2[:], in0=gmin[:, 0:8], in1=gmin[:, 8:16]),
                            [gmin_b], [thr2_b])
                        for k in range(2, 6):
                            V.emit(lambda e, k=k: e.tensor_max(
                                out=thr2[:], in0=thr2[:],
                                in1=gmin[:, k * 8:(k + 1) * 8]),
                                [gmin_b, thr2_b], [thr2_b])
                        V.emit(lambda e: e.tensor_tensor(
                            out=gmask[:].rearrange("p (tt g) -> p tt g", tt=8),
                            in0=gsr,
                            in1=thr2[:].unsqueeze(2).to_broadcast([128, 8, 4]),
                            op=ALU.is_ge), [gsum_b, thr2_b], [gmask_b])
                        emr = r3(emask)
                        for j in range(4):
                            V.emit(lambda e, j=j: e.tensor_copy(
                                out=emr[:, :, j:16:4],
                                in_=gmask[:].rearrange("p (tt g) -> p tt g", tt=8)),
                                [gmask_b], [emask_b])
                        V.emit(lambda e: e.tensor_scalar(
                            out=eqm[:], in0=emask[:], scalar1=-1.0, scalar2=1e9,
                            op0=ALU.add, op1=ALU.mult), [emask_b], [eqm_b])
                        V.emit(lambda e: e.tensor_add(
                            out=mk0[:], in0=sch[:], in1=eqm[:]),
                            [sch_b, eqm_b], [mk0_b])
                        V.emit(lambda e: e.tensor_copy(out=mkw[:], in_=mk0[:]),
                               [mk0_b], [mkw_b])
                        for it in range(3):
                            V.emit(lambda e: e.tensor_reduce(
                                out=mx[:].unsqueeze(2), in_=r3(mkw), axis=AXX,
                                op=ALU.max), [mkw_b], [mx_b])
                            V.emit(lambda e: e.tensor_tensor(
                                out=r3(eqm), in0=r3(mkw),
                                in1=mx[:].unsqueeze(2).to_broadcast([128, 8, 16]),
                                op=ALU.is_ge), [mkw_b, mx_b], [eqm_b])
                            V.emit(lambda e: e.scalar_tensor_tensor(
                                out=mkw[:], in0=eqm[:], scalar=-2e9, in1=mkw[:],
                                op0=ALU.mult, op1=ALU.add), [eqm_b, mkw_b], [mkw_b])
                        V.emit(lambda e: e.tensor_reduce(
                            out=mx[:].unsqueeze(2), in_=r3(mkw), axis=AXX, op=ALU.max),
                            [mkw_b], [mx_b])
                        V.emit(lambda e: e.tensor_tensor(
                            out=r3(sel), in0=r3(mk0),
                            in1=mx[:].unsqueeze(2).to_broadcast([128, 8, 16]),
                            op=ALU.is_ge), [mk0_b, mx_b], [sel_b])
                        V.emit(lambda e: e.tensor_mul(out=wraw[:], in0=sel[:],
                                                      in1=score[:]),
                               [sel_b, score_b], [wraw_b])
                        V.emit(lambda e: e.tensor_reduce(
                            out=wsum[:].unsqueeze(2), in_=r3(wraw), axis=AXX,
                            op=ALU.add), [wraw_b], [wsum_b])
                        V.emit(lambda e: e.tensor_scalar_add(
                            out=wsum[:], in0=wsum[:], scalar1=1e-20),
                            [wsum_b], [wsum_b])
                        V.emit(lambda e: e.reciprocal(out=winv[:], in_=wsum[:]),
                               [wsum_b], [winv_b])
                        V.emit(lambda e: e.tensor_scalar_mul(
                            out=winv[:], in0=winv[:], scalar1=2.5),
                            [winv_b], [winv_b])
                        V.emit(lambda e: e.tensor_tensor(
                            out=r3(cbt), in0=r3(wraw),
                            in1=winv[:].unsqueeze(2).to_broadcast([128, 8, 16]),
                            op=ALU.mult), [wraw_b, winv_b], [cbt_b])
                        cbtr_ps = ps2.enter_context(
                            nc.psum_tensor(tag + "cbtr", [16, 8 * 128], BF16))
                        cbtr_b = Buf(tag + "cbtr")
                        for tt in range(8):
                            PE.emit(lambda e, tt=tt: e.transpose(
                                cbtr_ps[:, tt * 128:(tt + 1) * 128],
                                in_=cbt[:, tt * 16:(tt + 1) * 16],
                                identity=ident[:]),
                                [cbt_b, ident_b], [cbtr_b], inc=(tt == 7))
                        V.emit(lambda e: e.tensor_copy(out=cbst[:], in_=cbtr_ps[:]),
                               [cbtr_b], [cbst_b])
                        cb_done = Buf(tag + "cbd")
                        p.dact.emit(lambda e, half=half: e.dma_start(
                            out=cb_dram[:, half * HALF:(half + 1) * HALF],
                            in_=cbst[:]), [cbst_b], [cb_done, cb_B])
                    p.compute_fence()

                    with ExitStack() as ps4:
                        gate_ps = ps4.enter_context(
                            nc.psum_tensor(tag + "gate", [128, 4 * NFF], F32))
                        up_ps = ps4.enter_context(
                            nc.psum_tensor(tag + "up", [128, 4 * NFF], F32))
                        down_ps = ps4.enter_context(
                            nc.psum_tensor(tag + "down", [128, 8 * NFF], F32))
                        gate_b, up_b, down_b = (Buf(tag + "g"), Buf(tag + "u"),
                                                Buf(tag + "d"))

                        def load_cb(e_i):
                            cbs, cbs_b = cbslot[e_i % 3]
                            p.dpool.emit(lambda e, cbs=cbs, e_i=e_i, half=half: e.dma_start(
                                out=cbs[:],
                                in_=cb_dram[e_i, half * HALF:(half + 1) * HALF]
                                .unsqueeze(0).partition_broadcast(128)),
                                [cb_done], [cbs_b])

                        load_cb(0)
                        load_cb(1)
                        for c in range(NEXP // 2):
                            if c + 1 < NEXP // 2:
                                load_w(2 * c + 2)
                                load_cb(2 * c + 2)
                            for sub in range(NSUB):
                                t0 = sub * NFF
                                xsl = [xn[:, dc * HALF + t0: dc * HALF + t0 + NFF]
                                       for dc in range(NDC)]
                                ws = [(wslot[(2 * c + ei) % 3],
                                       cbslot[(2 * c + ei) % 3]) for ei in range(2)]
                                for ei in range(2):
                                    (wt, wt_b), (cbs, cbs_b) = ws[ei]
                                    gs, gs_b = gsil[ei]
                                    hh, hh_b = hbuf[ei]
                                    for off, psb, pb in ((0, gate_ps, gate_b),
                                                         (4096, up_ps, up_b)):
                                        for ic in range(4):
                                            for dc in range(NDC):
                                                PE.emit(
                                                    lambda e, ic=ic, dc=dc, wt=wt,
                                                    off=off, psb=psb, xsl=xsl: e.matmul(
                                                        psb[:, ic * NFF:(ic + 1) * NFF],
                                                        lhsT=wt[:, off + dc * 512 + ic * 128:
                                                                off + dc * 512 + (ic + 1) * 128],
                                                        rhs=xsl[dc],
                                                        start=(dc == 0 and ic % 2 == 0),
                                                        stop=(dc == 7 and ic % 2 == 1)),
                                                    [wt_b, xn_bs[dc]], [pb],
                                                    inc=(dc == 7 and ic == 3))
                                    A.emit(lambda e, gs=gs: e.activation(
                                        out=gs[:], in_=gate_ps[:], func=AF.Silu),
                                        [gate_b], [gs_b])
                                    V.emit(lambda e, gs=gs, hh=hh: e.tensor_mul(
                                        out=hh[:], in0=gs[:], in1=up_ps[:]),
                                        [gs_b, up_b], [hh_b])
                                    PL.emit(lambda e, cbs=cbs, hh=hh, t0=t0: e.tensor_tensor(
                                        out=hh[:].rearrange("p (i t) -> p i t", i=4),
                                        in0=hh[:].rearrange("p (i t) -> p i t", i=4),
                                        in1=cbs[:, t0:t0 + NFF].unsqueeze(1)
                                        .to_broadcast([128, 4, NFF]),
                                        op=ALU.mult), [hh_b, cbs_b], [hh_b])
                                for ei in range(2):
                                    (wt, wt_b), _cb = ws[ei]
                                    hh, hh_b = hbuf[ei]
                                    for dc in range(NDC):
                                        for ic in range(4):
                                            PE.emit(lambda e, ic=ic, dc=dc, ei=ei, wt=wt,
                                                    hh=hh: e.matmul(
                                                down_ps[:, dc * NFF:(dc + 1) * NFF],
                                                lhsT=wt[:, 8192 + ic * 1024 + dc * 128:
                                                        8192 + ic * 1024 + (dc + 1) * 128],
                                                rhs=hh[:, ic * NFF:(ic + 1) * NFF],
                                                start=(ei == 0 and dc % 2 == 0 and ic == 0),
                                                stop=(ei == 1 and dc % 2 == 1 and ic == 3)),
                                                [wt_b, hh_b], [down_b],
                                                inc=(dc == 7 and ic == 3))
                                Xr = X[:].rearrange("p (dc t) -> p dc t", dc=NDC)
                                Xsl = Xr[:, :, t0:t0 + NFF]
                                V.emit(lambda e, Xsl=Xsl: e.scalar_tensor_tensor(
                                    out=Xsl,
                                    in0=down_ps[:].rearrange("p (dc t) -> p dc t", dc=NDC),
                                    scalar=0.5, in1=Xsl, op0=ALU.mult, op1=ALU.add),
                                    [down_b], [X_b] + X_cells)
                                if c == NEXP // 2 - 1:
                                    p.dact.emit(lambda e, half=half, t0=t0: e.dma_start(
                                        out=dst_r[:, :, half * HALF + t0:
                                                  half * HALF + t0 + NFF],
                                        in_=X[:].rearrange(
                                            "p (dc t) -> p dc t",
                                            dc=NDC)[:, :, t0:t0 + NFF]),
                                        [X_b], [dst_B])
                            if c + 1 < NEXP // 2:
                                load_w(2 * c + 3)
                                load_cb(2 * c + 3)
                    p.compute_fence()

                for half in range(2):
                    do_half(half)
            p.barrier()

        if stop_after == "ffn1":
            ffn_pass("f1", x_d, None, out_d, out_B)
        else:
            ffn_pass("f1", x_d, None, Xd, Xd_B)

        if stop_after == "ffn1":
            p.barrier()
            block = ctx.enter_context(nc.Block())
            p.replay(block)
            return nc
        cc_k_done = Buf("cc_k")
        cc_v_done = [Buf("cc_v0"), Buf("cc_v1")]
        with ExitStack() as p2:
            sbl = lambda nm, sh, dtt: sb_in(p2, "p2" + nm, sh, dtt)
            bu_bf, bu_bf_b = sbl("bu_bf", [1, 2048], BF16)
            bv_bf, bv_bf_b = sbl("bv_bf", [1, 2048], BF16)
            p.dpool.emit(lambda e: e.dma_start(out=bu_bf[:], in_=bu_d), [], [bu_bf_b])
            p.dpool.emit(lambda e: e.dma_start(out=bv_bf[:], in_=bv_d), [], [bv_bf_b])
            xn2, xn2_b = sbl("xn2", [128, NDC * TPC], BF16)
            X, X_b = sbl("X", [128, NDC * HALF], F32)
            rows = {
                "m": sbl("m_row", [1, HALF], F32),
                "var": sbl("var_row", [1, HALF], F32),
                "rstd": sbl("rstd", [1, HALF], F32),
                "scr": sbl("scrrow", [1, HALF], F32),
                "scr16": [sbl(f"scr16_{i}", [128, HALF], BF16) for i in range(4)],
            }
            v2, v2_b = sbl("v2row", [1, HALF], F32)
            rfin, rfin_b = sbl("rfin", [1, HALF], F32)
            negmf, negmf_b = sbl("negmf", [1, HALF], F32)
            xt = [sbl(f"xt{i}", [128, HALF], F32) for i in range(2)]
            Xsrc_r = Xd.rearrange("(dc p) t -> p dc t", p=128)
            def p2_half(half):
                tag = f"p2h{half}_"
                p.dsync.emit(lambda e, half=half: e.dma_start(
                    out=X[:].rearrange("p (dc t) -> p dc t", dc=NDC),
                    in_=Xsrc_r[:, :, half * HALF:(half + 1) * HALF]),
                    [Xd_B], [X_b])
                (m_row, m_row_b) = rows["m"]
                (var_row, var_row_b) = rows["var"]
                (rstd, rstd_b) = rows["rstd"]
                (scr, scr_b) = rows["scr"]
                ln_stats_rows(tag, rows, X, X_b)
                p.dpool.emit(lambda e, half=half: e.dma_start(
                    out=rows_d[0:1, half * HALF:(half + 1) * HALF], in_=rstd[:]),
                    [rstd_b], [rowsd_B])
                V.emit(lambda e: e.scalar_tensor_tensor(
                    out=scr[:], in0=m_row[:],
                    scalar=-1.0, in1=rstd[:], op0=ALU.mult, op1=ALU.mult),
                    [m_row_b, rstd_b], [scr_b])
                p.dpool.emit(lambda e, half=half: e.dma_start(
                    out=rows_d[1:2, half * HALF:(half + 1) * HALF], in_=scr[:]),
                    [scr_b], [rowsd_B])
                V.emit(lambda e: e.tensor_mul(out=v2[:], in0=var_row[:], in1=rstd[:]),
                       [var_row_b, rstd_b], [v2_b])
                V.emit(lambda e: e.tensor_mul(out=v2[:], in0=v2[:], in1=rstd[:]),
                       [v2_b, rstd_b], [v2_b])
                A.emit(lambda e: e.activation(out=scr[:], in_=v2[:], func=AF.Ln,
                                              bias=eps_t[0:1, 0:1]),
                       [v2_b, eps_b], [scr_b])
                A.emit(lambda e: e.activation(out=v2[:], in_=scr[:], func=AF.Exp,
                                              scale=-0.5), [scr_b], [v2_b])
                V.emit(lambda e: e.tensor_mul(out=rfin[:], in0=rstd[:], in1=v2[:]),
                       [rstd_b, v2_b], [rfin_b])
                V.emit(lambda e: e.scalar_tensor_tensor(
                    out=negmf[:], in0=m_row[:], scalar=-1.0, in1=rfin[:],
                    op0=ALU.mult, op1=ALU.mult), [m_row_b, rfin_b], [negmf_b])
                with ExitStack() as ps3:
                    bps = []
                    for i, (row, row_b) in enumerate([(rfin, rfin_b),
                                                      (negmf, negmf_b)]):
                        psb = ps3.enter_context(
                            nc.psum_tensor(f"{tag}bc{i}", [128, HALF], F32))
                        pb = Buf(f"{tag}bc{i}")
                        for ns in range(2):
                            PE.emit(lambda e, psb=psb, ns=ns, row=row: e.matmul(
                                psb[:, ns * 512:(ns + 1) * 512], lhsT=ones1f[:],
                                rhs=row[0:1, ns * 512:(ns + 1) * 512],
                                start=True, stop=True),
                                [row_b, ones1f_b], [pb], inc=(ns == 1))
                        bps.append((psb, pb))
                    (rf_bc, rf_bc_b), (nm_bc, nm_bc_b) = bps
                    for dc in range(NDC):
                        t, t_b = xt[dc % 2]
                        V.emit(lambda e, t=t, dc=dc: e.tensor_mul(
                            out=t[:], in0=X[:, dc * HALF:(dc + 1) * HALF],
                            in1=rf_bc[:]), [X_b, rf_bc_b], [t_b])
                        V.emit(lambda e, t=t: e.tensor_add(
                            out=t[:], in0=t[:], in1=nm_bc[:]),
                            [t_b, nm_bc_b], [t_b])
                        A.emit(lambda e, t=t, dc=dc, half=half: e.activation(
                            out=xn2[:, dc * TPC + half * HALF:
                                    dc * TPC + (half + 1) * HALF],
                            in_=t[:], func=AF.Identity,
                            bias=lnb[:, dc:dc + 1], scale=lng[:, dc:dc + 1]),
                            [t_b, lng_b, lnb_b], [xn2_b])
                p.compute_fence()

            wuvV, wuvV_b = sbl("wuvV", [128, 8 * 2048], BF16)
            wuvU, wuvU_b = sbl("wuvU", [128, 8 * 2048], BF16)
            p.dpool.emit(lambda e: e.dma_start(
                out=wuvV[:].rearrange("p (dc j) -> p dc j", dc=8),
                in_=winV_d.rearrange("dc p j -> p dc j")), [], [wuvV_b])
            p.dpool.emit(lambda e: e.dma_start(
                out=wuvU[:].rearrange("p (dc j) -> p dc j", dc=8),
                in_=winU_d.rearrange("dc p j -> p dc j")), [], [wuvU_b])
            stg = [sbl(f"uvst{i}", [128, 2048], BF16) for i in range(2)]
            uvps_scope = p2.enter_context(ExitStack())
            uv_ps0 = uvps_scope.enter_context(nc.psum_tensor("p2uv0", [128, 2048], F32))
            uv_ps0_b = Buf("p2uv0")
            uv_ps = [uv_ps0, uv_ps0]
            uv_ps_b = [uv_ps0_b, uv_ps0_b]

            def uv_tile(tt, wuv, wuv_b, brow, brow_b, dram_dst, dst_buf):
                psb, pb = uv_ps[tt % 2], uv_ps_b[tt % 2]
                st, st_b = stg[tt % 2]
                for dc in range(NDC):
                    for ns in range(4):
                        PE.emit(lambda e, tt=tt, dc=dc, ns=ns, psb=psb,
                                wuv=wuv: e.matmul(
                            psb[:, ns * 512:(ns + 1) * 512],
                            lhsT=xn2[:, dc * TPC + tt * 128:
                                     dc * TPC + (tt + 1) * 128],
                            rhs=wuv[:, dc * 2048 + ns * 512:
                                    dc * 2048 + (ns + 1) * 512],
                            start=(dc == 0), stop=False),
                            [xn2_b, wuv_b], [pb], inc=False)
                for ns in range(4):
                    PE.emit(lambda e, ns=ns, psb=psb, brow=brow: e.matmul(
                        psb[:, ns * 512:(ns + 1) * 512], lhsT=ones1b[:],
                        rhs=brow[0:1, ns * 512:(ns + 1) * 512],
                        start=False, stop=True),
                        [ones1b_b, brow_b], [pb], inc=(ns == 3))
                A.emit(lambda e, st=st, psb=psb: e.activation(
                    out=st[:], in_=psb[:], func=AF.Silu), [pb], [st_b])
                p.dact.emit(lambda e, st=st, tt=tt, dram_dst=dram_dst: e.dma_start(
                    out=dram_dst[tt * 128:(tt + 1) * 128, :], in_=st[:]),
                    [st_b], [dst_buf])

            for half in range(2):
                p2_half(half)
                for tt in range(half * 8, half * 8 + 8):
                    uv_tile(tt, wuvV, wuvV_b, bv_bf, bv_bf_b, vown_d, vown_B[half])
                p.cc.emit(lambda e, hv=half: e.collective_compute(
                    "AllGather", ALU.bypass,
                    replica_groups=[[0, 1, 2, 3], [4, 5, 6, 7]],
                    ins=[vown_d[hv * HALF:(hv + 1) * HALF, :].opt()],
                    outs=[vall[hv].opt()]),
                    [vown_B[half]], [cc_v_done[half], vall_B[half]])

            wz, wz_b = sbl("wz", [128, 8 * 128], BF16)
            p.dpool.emit(lambda e: e.dma_start(
                out=wz[:].rearrange("p (dc j) -> p dc j", dc=8),
                in_=winZ_d.rearrange("dc p j -> p dc j")), [], [wz_b])
            z_s, z_s_b = sbl("zs", [128, 2 * 512], F32)
            kst, kst_b = sbl("kst", [128, TPC], BF16)
            with ExitStack() as zs_scope:
                z_ps = zs_scope.enter_context(
                    nc.psum_tensor("p2zps", [128, 2 * 512], F32))
                z_ps_b = Buf("p2zps")
                for g in range(4):
                    psl = z_ps[:, (g % 2) * 512:(g % 2 + 1) * 512]
                    zsl = z_s[:, (g % 2) * 512:(g % 2 + 1) * 512]
                    for dc in range(NDC):
                        PE.emit(lambda e, g=g, dc=dc, psl=psl: e.matmul(
                            psl, lhsT=wz[:, dc * 128:(dc + 1) * 128],
                            rhs=xn2[:, dc * TPC + g * 512: dc * TPC + (g + 1) * 512],
                            start=(dc == 0), stop=(dc == NDC - 1)),
                            [wz_b, xn2_b], [z_ps_b], inc=(dc == NDC - 1))
                    A.emit(lambda e, g=g, psl=psl, zsl=zsl: e.activation(
                        out=zsl, in_=psl, func=AF.Silu,
                        bias=bz[:]), [z_ps_b, bz_b], [z_s_b])
                    V.emit(lambda e, g=g, zsl=zsl: e.tensor_scalar(
                        out=zq_sb[:, g * 512:(g + 1) * 512], in0=zsl, scalar1=gqv[:],
                        scalar2=bqv[:], op0=ALU.mult, op1=ALU.add),
                        [z_s_b, gqv_b, bqv_b], [zq_sb_b])
                    V.emit(lambda e, g=g, zsl=zsl: e.tensor_scalar(
                        out=kst[:, g * 512:(g + 1) * 512], in0=zsl, scalar1=gkv[:],
                        scalar2=bkv[:], op0=ALU.mult, op1=ALU.add),
                        [z_s_b, gkv_b, bkv_b], [kst_b])
            p.compute_fence()
            p.dact.emit(lambda e: e.dma_start(out=kown_d, in_=kst[:]),
                        [kst_b], [kown_B])
            p.cc.emit(lambda e: e.collective_compute(
                "AllGather", ALU.bypass,
                replica_groups=[[0, 1, 2, 3], [4, 5, 6, 7]],
                ins=[kown_d.opt()], outs=[kall_d.opt()]),
                [kown_B], [cc_k_done, kall_B])

            for tt in range(NQT):
                uv_tile(tt, wuvU, wuvU_b, bu_bf, bu_bf_b, u_dram,
                        ud_B)
        p.barrier()

        if stop_after == "p2":
            p.barrier()
            block = ctx.enter_context(nc.Block())
            p.replay(block)
            return nc
        with ExitStack() as p4:
            sbl = lambda nm, sh, dtt: sb_in(p4, "p4" + nm, sh, dtt)
            zk_sb, zk_sb_b = sbl("zk", [128, SEQ], BF16)
            p.dsync.emit(lambda e: e.dma_start(
                out=zk_sb[:].rearrange("p (r t) -> p r t", r=4),
                in_=kall_d.rearrange("(r p) t -> p r t", p=128)),
                [kall_B], [zk_sb_b])
            wo_sb, wo_sb_b = sbl("wo", [128, 16384], BF16)
            p.dpool.emit(lambda e: e.dma_start(out=wo_sb[:], in_=wout_d),
                         [], [wo_sb_b])
            acc = [sbl(f"acc{i}", [128, 2048], F32) for i in range(4)]
            at_cache = [sbl(f"at{i}", [128, 8 * 512], BF16) for i in range(2)]
            vb = [sbl(f"vb{i}", [128, 8 * 2048], BF16) for i in range(2)]
            usl = [sbl(f"usl{i}", [128, 2048], BF16) for i in range(2)]
            x1t, x1t_b = sbl("x1t", [128, 1024], F32)
            x2st, x2st_b = sbl("x2st", [128, 1024], F32)
            rsb, rsb_b = sbl("rs", [128, 512], F32)
            ptok, ptok_b = sbl("ptok", [128, 2048], BF16)
            pfm, pfm_b = sbl("pfm", [128, 2048], BF16)
            r1_row, r1_row_b = sbl("r1row", [1, TPC], BF16)
            n1_row, n1_row_b = sbl("n1row", [1, TPC], BF16)
            p.dsync.emit(lambda e: e.dma_start(out=r1_row[:], in_=rows_d[0:1, :]),
                         [rowsd_B], [r1_row_b])
            p.dsync.emit(lambda e: e.dma_start(out=n1_row[:], in_=rows_d[1:2, :]),
                         [rowsd_B], [n1_row_b])
            sc_ps = p4.enter_context(nc.psum_tensor("p4sc", [128, 512], F32))
            sc_b = Buf("p4sc")
            av_ps = [p4.enter_context(nc.psum_tensor(f"p4av{i}", [128, 512], F32))
                     for i in range(4)]
            av_b = [Buf(f"p4av{i}") for i in range(4)]
            tr_ps = p4.enter_context(nc.psum_tensor("p4tr", [128, 1024], BF16))
            tr_b = Buf("p4tr")
            wout_ps = p4.enter_context(nc.psum_tensor("p4wout", [128, 512], F32))
            wout_b = Buf("p4wout")
            lnbc_ps = p4.enter_context(nc.psum_tensor("p4lnbc", [128, 256], F32))
            lnbc_b = Buf("p4lnbc")
            Xd_r = Xd.rearrange("(dc p) t -> p dc t", p=128)

            def emit_A(qg, kb):
                atc, atc_b = at_cache[kb % 2]
                for k8 in range(8):
                    PE.emit(lambda e, qg=qg, kb=kb, k8=k8: e.matmul(
                        sc_ps[:],
                        lhsT=zk_sb[:, (kb * 8 + k8) * 128:(kb * 8 + k8 + 1) * 128],
                        rhs=zq_sb[:, qg * 512:(qg + 1) * 512],
                        start=True, stop=True),
                        [zk_sb_b, zq_sb_b], [sc_b], inc=True)
                    A.emit(lambda e: e.activation(out=rsb[:], in_=sc_ps[:],
                                                  func=AF.Relu, scale=INV_L),
                           [sc_b], [rsb_b])
                    V.emit(lambda e, k8=k8, atc=atc: e.tensor_mul(
                        out=atc[:, k8 * 512:(k8 + 1) * 512], in0=rsb[:], in1=rsb[:]),
                        [rsb_b], [atc_b])

            def load_vblock(kb):
                buf, buf_b = vb[kb % 2]
                r = kb // 2
                src = vall[kb % 2]
                srcr = src.rearrange("(r kt p) ee -> p r kt ee", p=128, kt=8)
                p.dsync.emit(lambda e, buf=buf, r=r, srcr=srcr: e.dma_start(
                    out=buf[:].rearrange("p (kt ee) -> p kt ee", kt=8),
                    in_=srcr[:, r]), [cc_v_done[kb % 2]], [buf_b])

            def emit_B(qg, kb, first):
                atc, atc_b = at_cache[kb % 2]
                buf, buf_b = vb[kb % 2]
                for qt4 in range(4):
                    a4, a4_b = acc[qt4]
                    for ns in range(4):
                        for k8 in range(8):
                            PE.emit(lambda e, qt4=qt4, ns=ns, k8=k8, atc=atc,
                                    buf=buf: e.matmul(
                                av_ps[ns][:],
                                lhsT=atc[:, k8 * 512 + qt4 * 128:
                                         k8 * 512 + (qt4 + 1) * 128],
                                rhs=buf[:, k8 * 2048 + ns * 512:
                                        k8 * 2048 + (ns + 1) * 512],
                                start=(k8 == 0), stop=(k8 == 7)),
                                [atc_b, buf_b], [av_b[ns]], inc=(k8 == 7))
                        if first:
                            V.emit(lambda e, a4=a4, ns=ns: e.tensor_copy(
                                out=a4[:, ns * 512:(ns + 1) * 512],
                                in_=av_ps[ns][:]), [av_b[ns]], [a4_b])
                        else:
                            V.emit(lambda e, a4=a4, ns=ns: e.tensor_add(
                                out=a4[:, ns * 512:(ns + 1) * 512],
                                in0=a4[:, ns * 512:(ns + 1) * 512],
                                in1=av_ps[ns][:]), [av_b[ns], a4_b], [a4_b])

            load_vblock(0)
            load_vblock(1)
            emit_A(0, 0)
            for qg in range(4):
                for kb in range(8):
                    if kb + 1 < 8:
                        emit_A(qg, kb + 1)
                    emit_B(qg, kb, first=(kb == 0))
                    if kb + 2 < 8:
                        load_vblock(kb + 2)
                if qg + 1 < 4:
                    load_vblock(0)
                    load_vblock(1)
                    emit_A(qg + 1, 0)
                for qt4 in range(4):
                    qt = qg * 4 + qt4
                    a4, a4_b = acc[qt4]
                    u_t, u_tb = usl[qt % 2]
                    p.dsync.emit(lambda e, u_t=u_t, qt=qt: e.dma_start(
                        out=u_t[:], in_=u_dram[qt * 128:(qt + 1) * 128, :]),
                        [ud_B], [u_tb])
                    p.dsync.emit(lambda e, qt=qt: e.dma_start(
                        out=x1t[:].rearrange("p (dc t) -> p dc t", dc=NDC),
                        in_=Xd_r[:, :, qt * 128:(qt + 1) * 128]),
                        [Xd_B], [x1t_b])
                    V.emit(lambda e, u_t=u_t, a4=a4: e.tensor_mul(
                        out=ptok[:], in0=a4[:], in1=u_t[:]),
                        [a4_b, u_tb], [ptok_b])
                    for half8 in range(2):
                        for j in range(8):
                            ec = half8 * 8 + j
                            PE.emit(lambda e, ec=ec, j=j: e.transpose(
                                tr_ps[:, j * 128:(j + 1) * 128],
                                in_=ptok[:, ec * 128:(ec + 1) * 128],
                                identity=ident[:]),
                                [ptok_b, ident_b], [tr_b], inc=(j == 7))
                        V.emit(lambda e, half8=half8: e.tensor_copy(
                            out=pfm[:, half8 * 1024:(half8 + 1) * 1024],
                            in_=tr_ps[:]), [tr_b], [pfm_b])
                    PE.emit(lambda e, qt=qt: e.matmul(
                        lnbc_ps[:, 0:128], lhsT=ones1b[:],
                        rhs=r1_row[0:1, qt * 128:(qt + 1) * 128],
                        start=True, stop=True),
                        [r1_row_b, ones1b_b], [lnbc_b], inc=False)
                    PE.emit(lambda e, qt=qt: e.matmul(
                        lnbc_ps[:, 128:256], lhsT=ones1b[:],
                        rhs=n1_row[0:1, qt * 128:(qt + 1) * 128],
                        start=True, stop=True),
                        [n1_row_b, ones1b_b], [lnbc_b], inc=True)
                    V.emit(lambda e: e.tensor_tensor(
                        out=x2st[:].rearrange("p (dc t) -> p dc t", dc=NDC),
                        in0=x1t[:].rearrange("p (dc t) -> p dc t", dc=NDC),
                        in1=lnbc_ps[:, 0:128].unsqueeze(1)
                        .to_broadcast([128, 8, 128]),
                        op=ALU.mult), [x1t_b, lnbc_b], [x2st_b])
                    V.emit(lambda e: e.tensor_tensor(
                        out=x2st[:].rearrange("p (dc t) -> p dc t", dc=NDC),
                        in0=x2st[:].rearrange("p (dc t) -> p dc t", dc=NDC),
                        in1=lnbc_ps[:, 128:256].unsqueeze(1)
                        .to_broadcast([128, 8, 128]),
                        op=ALU.add), [x2st_b, lnbc_b], [x2st_b])
                    V.emit(lambda e: e.tensor_add(out=x2st[:], in0=x2st[:],
                                                  in1=x1t[:]),
                           [x2st_b, x1t_b], [x2st_b])
                    V.emit(lambda e: e.tensor_tensor(
                        out=x2st[:].rearrange("p (dc t) -> p dc t", dc=NDC),
                        in0=x2st[:].rearrange("p (dc t) -> p dc t", dc=NDC),
                        in1=bout[:].unsqueeze(2).to_broadcast([128, 8, 128]),
                        op=ALU.add), [x2st_b, bout_b], [x2st_b])
                    for dcb in range(2):
                        for dc4 in range(4):
                            dc = dcb * 4 + dc4
                            for ec in range(16):
                                PE.emit(lambda e, dc=dc, dc4=dc4, ec=ec: e.matmul(
                                    wout_ps[:, dc4 * 128:(dc4 + 1) * 128],
                                    lhsT=wo_sb[:, (ec * 8 + dc) * 128:
                                               (ec * 8 + dc + 1) * 128],
                                    rhs=pfm[:, ec * 128:(ec + 1) * 128],
                                    start=(ec == 0), stop=(ec == 15)),
                                    [wo_sb_b, pfm_b], [wout_b],
                                    inc=(dc4 == 3 and ec == 15))
                        V.emit(lambda e, dcb=dcb: e.tensor_add(
                            out=x2st[:, dcb * 512:(dcb + 1) * 512],
                            in0=x2st[:, dcb * 512:(dcb + 1) * 512],
                            in1=wout_ps[:]), [wout_b, x2st_b], [x2st_b])
                    p.dact.emit(lambda e, qt=qt: e.dma_start(
                        out=Xd_r[:, :, qt * 128:(qt + 1) * 128],
                        in_=x2st[:].rearrange("p (dc t) -> p dc t", dc=NDC)),
                        [x2st_b], [Xd_B])
        p.barrier()

        if stop_after == "attn":
            ob = Buf("odbg")
            p.dsync.emit(lambda e: e.dma_start(out=out_d, in_=Xd), [Xd_B], [ob])
            p.barrier()
            block = ctx.enter_context(nc.Block())
            p.replay(block)
            return nc
        ffn_pass("f2", Xd, Xd_B, out_d, out_B)

        p.barrier()
        block = ctx.enter_context(nc.Block())
        p.replay(block)
    return nc


STOP_AFTER = None


def kernel(**inputs):
    x = np.asarray(inputs["L0_attn_x"])
    W_in = np.asarray(inputs["L0_attn_W_in"])
    b_in = np.asarray(inputs["L0_attn_b_in"])
    W_out = np.asarray(inputs["L0_attn_W_out"])
    b_out = np.asarray(inputs["L0_attn_b_out"])
    ln_g = np.asarray(inputs["L0_attn_ln_g"])
    ln_b = np.asarray(inputs["L0_attn_ln_b"])
    gq = np.asarray(inputs["L0_attn_gamma_q"])
    bq = np.asarray(inputs["L0_attn_beta_q"])
    gk = np.asarray(inputs["L0_attn_gamma_k"])
    bk = np.asarray(inputs["L0_attn_beta_k"])
    Wr = np.asarray(inputs["L0_ffn_router_weight"])
    rbias = np.asarray(inputs["L0_ffn_router_bias"])
    w_gate = np.asarray(inputs["L0_ffn_w_gate"])
    w_up = np.asarray(inputs["L0_ffn_w_up"])
    w_down = np.asarray(inputs["L0_ffn_w_down"])

    expw = np.empty((NEXP, 128, 12288), np.float32)
    for e in range(NEXP):
        WgT = w_gate[e].T
        WuT = w_up[e].T
        WdT = w_down[e].T
        expw[e, :, 0:4096] = WgT.reshape(8, 128, 512).transpose(1, 0, 2).reshape(128, 4096)
        expw[e, :, 4096:8192] = WuT.reshape(8, 128, 512).transpose(1, 0, 2).reshape(128, 4096)
        expw[e, :, 8192:12288] = WdT.reshape(4, 128, 1024).transpose(1, 0, 2).reshape(128, 4096)
    shared = dict(
        expw=expw,
        winU=np.ascontiguousarray(W_in[:, 0:2048].reshape(8, 128, 2048)),
        winV=np.ascontiguousarray(W_in[:, 2048:4096].reshape(8, 128, 2048)),
        winZ=np.ascontiguousarray(W_in[:, 4096:4224].reshape(8, 128, 128)),
        wout=np.ascontiguousarray(
            W_out.reshape(16, 128, 8, 128).transpose(1, 0, 2, 3).reshape(128, 16384)),
        wr=np.ascontiguousarray(Wr.T.reshape(8, 128, 16)),
        bu=np.ascontiguousarray(b_in[None, 0:2048].astype(np.float32)),
        bv=np.ascontiguousarray(b_in[None, 2048:4096].astype(np.float32)),
        bz=np.ascontiguousarray(b_in[4096:4224].reshape(128, 1)),
        lng=np.ascontiguousarray(ln_g.reshape(8, 128).T),
        lnb=np.ascontiguousarray(ln_b.reshape(8, 128).T),
        bout=np.ascontiguousarray(b_out.reshape(8, 128).T),
        gq=np.ascontiguousarray(gq.reshape(128, 1)),
        bq=np.ascontiguousarray(bq.reshape(128, 1)),
        gk=np.ascontiguousarray(gk.reshape(128, 1)),
        bk=np.ascontiguousarray(bk.reshape(128, 1)),
        rb=np.ascontiguousarray(rbias[None, :].astype(np.float32)),
        ident=np.eye(128, dtype=ml_dtypes.bfloat16),
        onesc=np.ones((128, 1), dtype=ml_dtypes.bfloat16),
        ones1b=np.ones((1, 128), dtype=ml_dtypes.bfloat16),
        ones1f=np.ones((1, 128), dtype=np.float32),
        ones11=np.ones((1, 1), dtype=np.float32),
    )
    in_maps = []
    for c in range(8):
        b = c // 4
        t0 = (c % 4) * TPC
        m = dict(shared)
        m["x"] = np.ascontiguousarray(x[b, t0:t0 + TPC, :].T)
        in_maps.append(m)

    nc = build_nc(stop_after=STOP_AFTER)
    res = run_bass_kernel_spmd(nc, in_maps, core_ids=list(range(8)))
    out = np.empty((2, 8192, 1024), np.float32)
    for c in range(8):
        b = c // 4
        t0 = (c % 4) * TPC
        out[b, t0:t0 + TPC, :] = res.results[c]["out"].T
    return out


# version 07
# speedup vs baseline: 1.0004x; 1.0004x over previous
import numpy as np
import ml_dtypes
from contextlib import ExitStack
import concourse.bass as bass
import concourse.mybir as mybir
from concourse.bass_utils import run_bass_kernel_spmd

F32 = mybir.dt.float32
BF16 = mybir.dt.bfloat16
AF = mybir.ActivationFunctionType
ALU = mybir.AluOpType
AXX = mybir.AxisListType.X



class Buf:
    __slots__ = ("name", "w", "rs")

    def __init__(self, name=""):
        self.name = name
        self.w = []
        self.rs = []


class Stream:
    def __init__(self, name):
        self.name = name
        self.records = []
        self.max_waited = {}


class Q:
    def __init__(self, stream, sem, step=1, name=""):
        self.stream = stream
        self.sem = sem
        self.step = step
        self.n = 0
        self.name = name
        self.pend_w = []
        self.pend_r = []

    def emit(self, build, reads=(), writes=(), inc=True):
        need = {}
        if self.step == 16 and self.n > 0:
            need[self] = self.n
        for b in reads:
            for (q, v) in b.w:
                if need.get(q, -1) < v:
                    need[q] = v
        for b in writes:
            for (q, v) in b.w:
                if need.get(q, -1) < v:
                    need[q] = v
            for (q, v) in b.rs:
                if need.get(q, -1) < v:
                    need[q] = v
        waits = []
        st = self.stream
        for q, v in need.items():
            assert v <= q.n, f"wait on unproduced token {q.name}:{v}"
            if st.max_waited.get(q, -1) < v:
                waits.append((q.sem, v))
                st.max_waited[q] = v
        st.records.append((waits, build, (self.sem, self.step) if inc else None))
        for b in writes:
            self.pend_w.append(b)
        for b in reads:
            self.pend_r.append(b)
        if inc:
            self.n += self.step
            tok = (self, self.n)
            for b in self.pend_w:
                b.w.append(tok)
                b.rs = []
            for b in self.pend_r:
                b.rs.append(tok)
            self.pend_w = []
            self.pend_r = []

    def seal(self):
        if self.pend_w or self.pend_r:
            raise RuntimeError(f"{self.name}: pending accesses without an inc")

    def token(self):
        self.seal()
        return (self, self.n)


class DmaQ:
    def __init__(self, stream, sems, name):
        self.subqs = [Q(stream, s, 16, f"{name}{i}") for i, s in enumerate(sems)]
        self.i = 0
        self.name = name

    def emit(self, build, reads=(), writes=()):
        q = self.subqs[self.i]
        self.i = (self.i + 1) % len(self.subqs)
        return q.emit(build, reads, writes, inc=True)

    def qs(self):
        return list(self.subqs)


class Prog:
    def __init__(self, nc, ctx, n_dsync=12, n_dact=10, n_dpool=6):
        self.nc = nc
        self.streams = {n: Stream(n) for n in
                        ("tensor", "vector", "scalar", "gpsimd", "sync")}

        def sem(name):
            return ctx.enter_context(nc.semaphore(name))
        self.pe = Q(self.streams["tensor"], sem("q_pe"), 1, "pe")
        self.dve = Q(self.streams["vector"], sem("q_dve"), 1, "dve")
        self.act = Q(self.streams["scalar"], sem("q_act"), 1, "act")
        self.pool = Q(self.streams["gpsimd"], sem("q_pool"), 1, "pool")
        self.dsync = DmaQ(self.streams["sync"],
                          [sem(f"q_dsync{i}") for i in range(n_dsync)], "dsync")
        self.dact = DmaQ(self.streams["scalar"],
                         [sem(f"q_dact{i}") for i in range(n_dact)], "dact")
        self.dpool = DmaQ(self.streams["gpsimd"],
                          [sem(f"q_dpool{i}") for i in range(n_dpool)], "dpool")
        self.cc = Q(self.streams["gpsimd"], sem("q_cc"), 1, "cc")
        self.all_qs = ([self.pe, self.dve, self.act, self.pool, self.cc]
                       + self.dsync.qs() + self.dact.qs() + self.dpool.qs())

    def barrier(self, qs=None):
        qs = qs or self.all_qs
        toks = [q.token() for q in qs]
        for q in qs:
            st = q.stream
            waits = []
            for (q2, v) in toks:
                if v > 0 and st.max_waited.get(q2, -1) < v:
                    waits.append((q2.sem, v))
                    st.max_waited[q2] = v
            if waits:
                st.records.append((waits, None, None))

    def compute_fence(self):
        self.barrier([self.pe, self.dve, self.act])

    def replay(self, block):
        eng_of = {"tensor": block.tensor, "vector": block.vector,
                  "scalar": block.scalar, "gpsimd": block.gpsimd,
                  "sync": block.sync}

        def mk(stream):
            def body(eng):
                for waits, build, inc in stream.records:
                    if build is None:
                        for (sem, val) in waits:
                            eng.wait_ge(sem, val)
                        continue
                    for (sem, val) in waits[:-1]:
                        eng.wait_ge(sem, val)
                    inst = build(eng)
                    if waits:
                        inst._wait_ge(waits[-1][0], waits[-1][1])
                    if inc is not None:
                        inst.then_inc(inc[0], inc[1])
            return body
        for name, stream in self.streams.items():
            if stream.records:
                eng_of[name](mk(stream))


TPC = 2048
HALF = 1024
NSUB = 4
NFF = 256
NDC = 8
NEXP = 16
SEQ = 8192
NKT = 64
NQT = 16
EU = 2048
LN_EPS = 1e-5
INV_L = 1.0 / 8192.0


def build_nc(stop_after=None):
    nc = bass.Bass("TRN2", target_bir_lowering=False)
    dt = nc.dram_tensor
    x_d = dt("x", [1024, TPC], F32, kind="ExternalInput").ap()
    expw_d = dt("expw", [NEXP, 128, 12288], F32, kind="ExternalInput").ap()
    winU_d = dt("winU", [8, 128, 2048], F32, kind="ExternalInput").ap()
    winV_d = dt("winV", [8, 128, 2048], F32, kind="ExternalInput").ap()
    winZ_d = dt("winZ", [8, 128, 128], F32, kind="ExternalInput").ap()
    wout_d = dt("wout", [128, 16384], F32, kind="ExternalInput").ap()
    wr_d = dt("wr", [8, 128, 16], F32, kind="ExternalInput").ap()
    bu_d = dt("bu", [1, 2048], F32, kind="ExternalInput").ap()
    bv_d = dt("bv", [1, 2048], F32, kind="ExternalInput").ap()
    bz_d = dt("bz", [128, 1], F32, kind="ExternalInput").ap()
    lng_d = dt("lng", [128, 8], F32, kind="ExternalInput").ap()
    lnb_d = dt("lnb", [128, 8], F32, kind="ExternalInput").ap()
    bout_d = dt("bout", [128, 8], F32, kind="ExternalInput").ap()
    gq_d = dt("gq", [128, 1], F32, kind="ExternalInput").ap()
    bq_d = dt("bq", [128, 1], F32, kind="ExternalInput").ap()
    gk_d = dt("gk", [128, 1], F32, kind="ExternalInput").ap()
    bk_d = dt("bk", [128, 1], F32, kind="ExternalInput").ap()
    rb_d = dt("rb", [1, 16], F32, kind="ExternalInput").ap()
    ident_d = dt("ident", [128, 128], BF16, kind="ExternalInput").ap()
    onesc_d = dt("onesc", [128, 1], BF16, kind="ExternalInput").ap()
    ones1b_d = dt("ones1b", [1, 128], BF16, kind="ExternalInput").ap()
    ones1f_d = dt("ones1f", [1, 128], F32, kind="ExternalInput").ap()
    ones11_d = dt("ones11", [1, 1], F32, kind="ExternalInput").ap()
    out_d = dt("out", [1024, TPC], F32, kind="ExternalOutput").ap()
    Xd = dt("Xd", [1024, TPC], F32).ap()
    u_dram = dt("u_dram", [TPC, EU], BF16).ap()
    vown_d = dt("vown", [TPC, EU], BF16).ap()
    vall = [dt(f"vall{i}", [4 * HALF, EU], BF16).ap() for i in range(2)]
    kown_d = dt("kown", [128, TPC], BF16).ap()
    kall_d = dt("kall", [4 * 128, TPC], BF16).ap()
    cb_dram = dt("cb_dram", [NEXP, TPC], BF16).ap()
    rows_d = dt("rows_d", [2, TPC], BF16).ap()
    rowsb_d = dt("rowsb_d", [2, HALF], F32).ap()

    Xd_B = Buf("Xd")
    out_B = Buf("outd")
    ud_B = Buf("u_dram")
    vown_B = [Buf("vown0"), Buf("vown1")]
    kown_B = Buf("kown")
    kall_B = Buf("kall")
    vall_B = [Buf("vall0"), Buf("vall1")]
    cb_B = Buf("cb_dram")
    rowsd_B = Buf("rows_d")
    rowsb_B = Buf("rowsb_d")

    ctx = ExitStack()
    with ctx:
        p = Prog(nc, ctx)
        PE, V, A, PL = p.pe, p.dve, p.act, p.pool

        def sb_in(scope, name, shape, dtype):
            t = scope.enter_context(nc.sbuf_tensor(name, shape, dtype))
            return t, Buf(name)

        sb = lambda nm, sh, dtt: sb_in(ctx, nm, sh, dtt)

        ident, ident_b = sb("c_ident", [128, 128], BF16)
        onesc, onesc_b = sb("c_onesc", [128, 1], BF16)
        ones1b, ones1b_b = sb("c_ones1b", [1, 128], BF16)
        ones1f, ones1f_b = sb("c_ones1f", [1, 128], F32)
        ones11, ones11_b = sb("c_ones11", [1, 1], F32)
        bz, bz_b = sb("c_bz", [128, 1], F32)
        lng, lng_b = sb("c_lng", [128, 8], F32)
        lnb, lnb_b = sb("c_lnb", [128, 8], F32)
        bout, bout_b = sb("c_bout", [128, 8], F32)
        gqv, gqv_b = sb("gqv", [128, 1], F32)
        bqv, bqv_b = sb("bqv", [128, 1], F32)
        gkv, gkv_b = sb("gkv", [128, 1], F32)
        bkv, bkv_b = sb("bkv", [128, 1], F32)
        rb_bc, rb_bc_b = sb("c_rb_bc", [128, 16], F32)
        cw_bc, cw_bc_b = sb("c_cw_bc", [128, 16], F32)
        zq_sb, zq_sb_b = sb("c_zq_sb", [128, TPC], BF16)
        wr_sb, wr_sb_b = sb("c_wr_sb", [128, 8 * 16], F32)
        eps_t, eps_b = sb("c_eps", [1, 1], F32)
        V.emit(lambda e: e.memset(eps_t[:], LN_EPS), [], [eps_b])


        for dst, dst_b, src in [
                (ident, ident_b, ident_d), (onesc, onesc_b, onesc_d),
                (ones1b, ones1b_b, ones1b_d), (ones1f, ones1f_b, ones1f_d),
                (ones11, ones11_b, ones11_d), (bz, bz_b, bz_d),
                (lng, lng_b, lng_d), (lnb, lnb_b, lnb_d), (bout, bout_b, bout_d),
                (gqv, gqv_b, gq_d), (bqv, bqv_b, bq_d), (gkv, gkv_b, gk_d),
                (bkv, bkv_b, bk_d)]:
            p.dsync.emit(lambda e, dst=dst, src=src: e.dma_start(out=dst[:], in_=src),
                         [], [dst_b])
        p.dsync.emit(lambda e: e.dma_start(
            out=wr_sb[:].rearrange("p (dc j) -> p dc j", dc=8),
            in_=wr_d.rearrange("dc p j -> p dc j")), [], [wr_sb_b])
        p.dsync.emit(lambda e: e.dma_start(out=rb_bc[:], in_=rb_d.partition_broadcast(128)),
                     [], [rb_bc_b])

        with ExitStack() as cws:
            cw_ps = cws.enter_context(nc.psum_tensor("cw_ps", [1, 16], F32))
            cw_ps_b = Buf("cw_ps")
            onesf_t, onesf_b = sb_in(cws, "onesf128", [128, 1], F32)
            V.emit(lambda e: e.memset(onesf_t[:], 1.0), [], [onesf_b])
            for dc in range(8):
                PE.emit(lambda e, dc=dc: e.matmul(
                    cw_ps[:], lhsT=onesf_t[:], rhs=wr_sb[:, dc * 16:(dc + 1) * 16],
                    start=(dc == 0), stop=(dc == 7)),
                    [onesf_b, wr_sb_b], [cw_ps_b], inc=(dc == 7))
            cw_row, cw_row_b = sb_in(cws, "cw_row", [1, 16], F32)
            V.emit(lambda e: e.tensor_copy(out=cw_row[:], in_=cw_ps[:]),
                   [cw_ps_b], [cw_row_b])
            cwb_ps = cws.enter_context(nc.psum_tensor("cwb_ps", [128, 16], F32))
            cwb_b = Buf("cwb_ps")
            PE.emit(lambda e: e.matmul(cwb_ps[:], lhsT=ones1f[:], rhs=cw_row[:],
                                       start=True, stop=True),
                    [ones1f_b, cw_row_b], [cwb_b])
            V.emit(lambda e: e.tensor_copy(out=cw_bc[:], in_=cwb_ps[:]),
                   [cwb_b], [cw_bc_b])
        p.barrier()

        def ln_stats_rows(tag, rows, X, X_b, X_cells=None):
            (m_row, m_row_b) = rows["m"]
            (var_row, var_row_b) = rows["var"]
            (rstd, rstd_b) = rows["rstd"]
            (scr, scr_b) = rows["scr"]
            scr16 = rows["scr16"]
            with ExitStack() as ps_scope:
                stats_ps = ps_scope.enter_context(
                    nc.psum_tensor(tag + "stats", [1, 2 * HALF], F32))
                stats_b = Buf(tag + "stats")
                for dc in range(NDC):
                    tb, tb_b = scr16[dc % 2]
                    ts, ts_b = scr16[2 + dc % 2]
                    xsl = X[:, dc * HALF:(dc + 1) * HALF]
                    xdep = [X_cells[dc]] if X_cells else [X_b]
                    V.emit(lambda e, tb=tb, xsl=xsl: e.tensor_copy(out=tb[:], in_=xsl),
                           xdep, [tb_b])
                    A.emit(lambda e, ts=ts, xsl=xsl: e.activation(
                        out=ts[:], in_=xsl, func=AF.Square), xdep, [ts_b])
                    for ns in range(2):
                        PE.emit(lambda e, tb=tb, ns=ns, dc=dc: e.matmul(
                            stats_ps[0:1, ns * 512:(ns + 1) * 512],
                            lhsT=onesc[:], rhs=tb[:, ns * 512:(ns + 1) * 512],
                            start=(dc == 0), stop=(dc == NDC - 1)),
                            [tb_b, onesc_b], [stats_b], inc=False)
                        PE.emit(lambda e, ts=ts, ns=ns, dc=dc: e.matmul(
                            stats_ps[0:1, HALF + ns * 512:HALF + (ns + 1) * 512],
                            lhsT=onesc[:], rhs=ts[:, ns * 512:(ns + 1) * 512],
                            start=(dc == 0), stop=(dc == NDC - 1)),
                            [ts_b, onesc_b], [stats_b],
                            inc=(ns == 1))
                A.emit(lambda e: e.mul(m_row[:], stats_ps[0:1, 0:HALF], 1.0 / 1024.0),
                       [stats_b], [m_row_b])
                A.emit(lambda e: e.activation(out=scr[:], in_=m_row[:], func=AF.Square),
                       [m_row_b], [scr_b])
                V.emit(lambda e: e.scalar_tensor_tensor(
                    out=var_row[:], in0=stats_ps[0:1, HALF:2 * HALF],
                    scalar=1.0 / 1024.0, in1=scr[:],
                    op0=ALU.mult, op1=ALU.subtract), [stats_b, scr_b], [var_row_b])
                A.emit(lambda e: e.activation(out=scr[:], in_=var_row[:], func=AF.Ln,
                                              bias=eps_t[0:1, 0:1]),
                       [var_row_b, eps_b], [scr_b])
                A.emit(lambda e: e.activation(out=rstd[:], in_=scr[:], func=AF.Exp,
                                              scale=-0.5), [scr_b], [rstd_b])
            p.compute_fence()

        def ffn_pass(pass_tag, src, src_B, dst, dst_B):
            src_r = src.rearrange("(dc p) t -> p dc t", p=128)
            dst_r = dst.rearrange("(dc p) t -> p dc t", p=128)
            with ExitStack() as fs:
                sbl = lambda nm, sh, dtt: sb_in(fs, pass_tag + nm, sh, dtt)
                X, X_b = sbl("X", [128, NDC * HALF], F32)
                X_cells = [Buf(f"{pass_tag}Xc{dc}") for dc in range(NDC)]
                xn, _xn_b = sbl("xn", [128, NDC * HALF], BF16)
                xn_bs = [Buf(f"{pass_tag}xn{dc}") for dc in range(NDC)]
                wslot = [sbl(f"w{i}", [128, 12288], BF16) for i in range(3)]
                cbslot = [sbl(f"cb{i}", [128, HALF], BF16) for i in range(3)]
                gsil = [sbl(f"gs{i}", [128, 4 * NFF], BF16) for i in range(2)]
                hbuf = [sbl(f"h{i}", [128, 4 * NFF], BF16) for i in range(2)]
                rows = {
                    "m": sbl("m_row", [1, HALF], F32),
                    "var": sbl("var_row", [1, HALF], F32),
                    "rstd": sbl("rstd", [1, HALF], F32),
                    "scr": sbl("scrrow", [1, HALF], F32),
                    "scr16": [sbl(f"scr16_{i}", [128, HALF], BF16) for i in range(4)],
                }
                negm, negm_b = sbl("negm", [1, HALF], F32)
                tokv, tokv_b = sbl("tokv", [128, 16], F32)
                xt = [sbl(f"xt{i}", [128, HALF], F32) for i in range(2)]
                rbc_sb, rbc_b = sbl("rbc", [128, HALF], F32)
                nbc_sb, nbc_b = sbl("nbc", [128, HALF], F32)
                rt = {nm: sbl(nm, [128, 128], F32) for nm in
                      ("lgr", "logit", "score", "sch", "emask", "mk0", "mkw",
                       "eqm", "sel", "wraw")}
                pr, pr_b = sbl("pr", [128, 32 * 6], F32)
                gsum, gsum_b = sbl("gsum", [128, 32], F32)
                gmin, gmin_b = sbl("gmin", [128, 8 * 6], F32)
                thr2, thr2_b = sbl("thr2", [128, 8], F32)
                gmask, gmask_b = sbl("gmask", [128, 32], F32)
                mx, mx_b = sbl("mx", [128, 8], F32)
                wsum, wsum_b = sbl("wsum", [128, 8], F32)
                winv, winv_b = sbl("winv", [128, 8], F32)
                cbt, cbt_b = sbl("cbt", [128, 128], BF16)
                cbst, cbst_b = sbl("cbst", [16, 1024], BF16)

                def load_w(e_i):
                    wt, wt_b = wslot[e_i % 3]
                    p.dpool.emit(lambda e, wt=wt, e_i=e_i: e.dma_start(
                        out=wt[:], in_=expw_d[e_i]), [], [wt_b])

                def do_half(half):
                    tag = f"{pass_tag}h{half}_"
                    for dc in range(NDC):
                        p.dsync.emit(lambda e, half=half, dc=dc: e.dma_start(
                            out=X[:, dc * HALF:(dc + 1) * HALF],
                            in_=src_r[:, dc, half * HALF:(half + 1) * HALF]),
                            [src_B] if src_B else [], [X_cells[dc], X_b])
                    load_w(0)
                    load_w(1)
                    (m_row, m_row_b) = rows["m"]
                    (rstd, rstd_b) = rows["rstd"]
                    ln_stats_rows(tag, rows, X, X_b, X_cells)
                    V.emit(lambda e: e.scalar_tensor_tensor(
                        out=negm[:], in0=m_row[:], scalar=-1.0, in1=rstd[:],
                        op0=ALU.mult, op1=ALU.mult), [m_row_b, rstd_b], [negm_b])

                    p.dact.emit(lambda e: e.dma_start(out=rowsb_d[0:1, :], in_=rstd[:]),
                                [rstd_b], [rowsb_B])
                    p.dact.emit(lambda e: e.dma_start(out=rowsb_d[1:2, :], in_=negm[:]),
                                [negm_b], [rowsb_B])
                    p.dsync.emit(lambda e: e.dma_start(
                        out=rbc_sb[:], in_=rowsb_d[0:1, :].partition_broadcast(128)),
                        [rowsb_B], [rbc_b])
                    p.dsync.emit(lambda e: e.dma_start(
                        out=nbc_sb[:], in_=rowsb_d[1:2, :].partition_broadcast(128)),
                        [rowsb_B], [nbc_b])
                    for dc in range(NDC):
                        t, t_b = xt[dc % 2]
                        V.emit(lambda e, t=t, dc=dc: e.tensor_mul(
                            out=t[:], in0=X[:, dc * HALF:(dc + 1) * HALF],
                            in1=rbc_sb[:]), [X_cells[dc], rbc_b], [t_b])
                        V.emit(lambda e, t=t, dc=dc: e.tensor_add(
                            out=xn[:, dc * HALF:(dc + 1) * HALF], in0=t[:],
                            in1=nbc_sb[:]), [t_b, nbc_b], [xn_bs[dc]])

                    with ExitStack() as ps2:
                        tok_ps = ps2.enter_context(
                            nc.psum_tensor(tag + "tokps", [128, 16], F32))
                        tok_ps_b = Buf(tag + "tokps")
                        for tt in range(8):
                            PE.emit(lambda e, tt=tt: e.matmul(
                                tok_ps[:, tt:tt + 1],
                                lhsT=rstd[0:1, tt * 128:(tt + 1) * 128],
                                rhs=ones11[:], start=True, stop=True),
                                [rstd_b, ones11_b], [tok_ps_b], inc=False)
                            PE.emit(lambda e, tt=tt: e.matmul(
                                tok_ps[:, 8 + tt:9 + tt],
                                lhsT=negm[0:1, tt * 128:(tt + 1) * 128],
                                rhs=ones11[:], start=True, stop=True),
                                [negm_b, ones11_b], [tok_ps_b], inc=(tt == 7))
                        V.emit(lambda e: e.tensor_copy(out=tokv[:], in_=tok_ps[:]),
                               [tok_ps_b], [tokv_b])

                        lg_ps = ps2.enter_context(
                            nc.psum_tensor(tag + "lg", [128, 8 * 16], F32))
                        lg_b = Buf(tag + "lg")
                        for tt in range(8):
                            for dc in range(NDC):
                                PE.emit(lambda e, tt=tt, dc=dc: e.matmul(
                                    lg_ps[:, tt * 16:(tt + 1) * 16],
                                    lhsT=X[:, dc * HALF + tt * 128:
                                           dc * HALF + (tt + 1) * 128],
                                    rhs=wr_sb[:, dc * 16:(dc + 1) * 16],
                                    start=(dc == 0), stop=(dc == NDC - 1)),
                                    [X_cells[dc], wr_sb_b], [lg_b],
                                    inc=(dc == NDC - 1 and tt == 7))

                        r3 = lambda t: t[:].rearrange("p (tt j) -> p tt j", tt=8)
                        lgr, lgr_b = rt["lgr"]
                        logit, logit_b = rt["logit"]
                        score, score_b = rt["score"]
                        sch, sch_b = rt["sch"]
                        emask, emask_b = rt["emask"]
                        mk0, mk0_b = rt["mk0"]
                        mkw, mkw_b = rt["mkw"]
                        eqm, eqm_b = rt["eqm"]
                        sel, sel_b = rt["sel"]
                        wraw, wraw_b = rt["wraw"]
                        V.emit(lambda e: e.tensor_tensor(
                            out=r3(lgr),
                            in0=lg_ps[:].rearrange("p (tt j) -> p tt j", tt=8),
                            in1=tokv[:, 0:8].unsqueeze(2).to_broadcast([128, 8, 16]),
                            op=ALU.mult), [lg_b, tokv_b], [lgr_b])
                        V.emit(lambda e: e.tensor_tensor(
                            out=r3(logit),
                            in0=cw_bc[:].unsqueeze(1).to_broadcast([128, 8, 16]),
                            in1=tokv[:, 8:16].unsqueeze(2).to_broadcast([128, 8, 16]),
                            op=ALU.mult), [cw_bc_b, tokv_b], [logit_b])
                        V.emit(lambda e: e.tensor_add(out=logit[:], in0=logit[:],
                                                      in1=lgr[:]),
                               [lgr_b, logit_b], [logit_b])
                        A.emit(lambda e: e.activation(out=score[:], in_=logit[:],
                                                      func=AF.Sigmoid),
                               [logit_b], [score_b])
                        V.emit(lambda e: e.tensor_tensor(
                            out=r3(sch), in0=r3(score),
                            in1=rb_bc[:].unsqueeze(1).to_broadcast([128, 8, 16]),
                            op=ALU.add), [score_b, rb_bc_b], [sch_b])
                        schr = r3(sch)
                        el = [schr[:, :, j:16:4] for j in range(4)]
                        prk = lambda k: pr[:, k * 32:(k + 1) * 32].rearrange(
                            "p (tt g) -> p tt g", tt=8)
                        pairs = [(0, 1), (0, 2), (0, 3), (1, 2), (1, 3), (2, 3)]
                        for k, (i, j) in enumerate(pairs):
                            V.emit(lambda e, k=k, i=i, j=j: e.tensor_add(
                                out=prk(k), in0=el[i], in1=el[j]), [sch_b], [pr_b])
                        gsr = gsum[:].rearrange("p (tt g) -> p tt g", tt=8)
                        V.emit(lambda e: e.tensor_max(out=gsr, in0=prk(0), in1=prk(1)),
                               [pr_b], [gsum_b])
                        for k in range(2, 6):
                            V.emit(lambda e, k=k: e.tensor_max(
                                out=gsr, in0=gsr, in1=prk(k)),
                                [pr_b, gsum_b], [gsum_b])
                        for k, (i, j) in enumerate(pairs):
                            V.emit(lambda e, k=k, i=i, j=j: e.tensor_tensor(
                                out=gmin[:, k * 8:(k + 1) * 8].unsqueeze(2),
                                in0=gsr[:, :, i:i + 1], in1=gsr[:, :, j:j + 1],
                                op=ALU.min), [gsum_b], [gmin_b])
                        V.emit(lambda e: e.tensor_max(
                            out=thr2[:], in0=gmin[:, 0:8], in1=gmin[:, 8:16]),
                            [gmin_b], [thr2_b])
                        for k in range(2, 6):
                            V.emit(lambda e, k=k: e.tensor_max(
                                out=thr2[:], in0=thr2[:],
                                in1=gmin[:, k * 8:(k + 1) * 8]),
                                [gmin_b, thr2_b], [thr2_b])
                        V.emit(lambda e: e.tensor_tensor(
                            out=gmask[:].rearrange("p (tt g) -> p tt g", tt=8),
                            in0=gsr,
                            in1=thr2[:].unsqueeze(2).to_broadcast([128, 8, 4]),
                            op=ALU.is_ge), [gsum_b, thr2_b], [gmask_b])
                        emr = r3(emask)
                        for j in range(4):
                            V.emit(lambda e, j=j: e.tensor_copy(
                                out=emr[:, :, j:16:4],
                                in_=gmask[:].rearrange("p (tt g) -> p tt g", tt=8)),
                                [gmask_b], [emask_b])
                        V.emit(lambda e: e.tensor_scalar(
                            out=eqm[:], in0=emask[:], scalar1=-1.0, scalar2=1e9,
                            op0=ALU.add, op1=ALU.mult), [emask_b], [eqm_b])
                        V.emit(lambda e: e.tensor_add(
                            out=mk0[:], in0=sch[:], in1=eqm[:]),
                            [sch_b, eqm_b], [mk0_b])
                        V.emit(lambda e: e.tensor_copy(out=mkw[:], in_=mk0[:]),
                               [mk0_b], [mkw_b])
                        for it in range(3):
                            V.emit(lambda e: e.tensor_reduce(
                                out=mx[:].unsqueeze(2), in_=r3(mkw), axis=AXX,
                                op=ALU.max), [mkw_b], [mx_b])
                            V.emit(lambda e: e.tensor_tensor(
                                out=r3(eqm), in0=r3(mkw),
                                in1=mx[:].unsqueeze(2).to_broadcast([128, 8, 16]),
                                op=ALU.is_ge), [mkw_b, mx_b], [eqm_b])
                            V.emit(lambda e: e.scalar_tensor_tensor(
                                out=mkw[:], in0=eqm[:], scalar=-2e9, in1=mkw[:],
                                op0=ALU.mult, op1=ALU.add), [eqm_b, mkw_b], [mkw_b])
                        V.emit(lambda e: e.tensor_reduce(
                            out=mx[:].unsqueeze(2), in_=r3(mkw), axis=AXX, op=ALU.max),
                            [mkw_b], [mx_b])
                        V.emit(lambda e: e.tensor_tensor(
                            out=r3(sel), in0=r3(mk0),
                            in1=mx[:].unsqueeze(2).to_broadcast([128, 8, 16]),
                            op=ALU.is_ge), [mk0_b, mx_b], [sel_b])
                        V.emit(lambda e: e.tensor_mul(out=wraw[:], in0=sel[:],
                                                      in1=score[:]),
                               [sel_b, score_b], [wraw_b])
                        V.emit(lambda e: e.tensor_reduce(
                            out=wsum[:].unsqueeze(2), in_=r3(wraw), axis=AXX,
                            op=ALU.add), [wraw_b], [wsum_b])
                        V.emit(lambda e: e.tensor_scalar_add(
                            out=wsum[:], in0=wsum[:], scalar1=1e-20),
                            [wsum_b], [wsum_b])
                        V.emit(lambda e: e.reciprocal(out=winv[:], in_=wsum[:]),
                               [wsum_b], [winv_b])
                        V.emit(lambda e: e.tensor_scalar_mul(
                            out=winv[:], in0=winv[:], scalar1=2.5),
                            [winv_b], [winv_b])
                        V.emit(lambda e: e.tensor_tensor(
                            out=r3(cbt), in0=r3(wraw),
                            in1=winv[:].unsqueeze(2).to_broadcast([128, 8, 16]),
                            op=ALU.mult), [wraw_b, winv_b], [cbt_b])
                        cbtr_ps = ps2.enter_context(
                            nc.psum_tensor(tag + "cbtr", [16, 8 * 128], BF16))
                        cbtr_b = Buf(tag + "cbtr")
                        for tt in range(8):
                            PE.emit(lambda e, tt=tt: e.transpose(
                                cbtr_ps[:, tt * 128:(tt + 1) * 128],
                                in_=cbt[:, tt * 16:(tt + 1) * 16],
                                identity=ident[:]),
                                [cbt_b, ident_b], [cbtr_b], inc=(tt == 7))
                        V.emit(lambda e: e.tensor_copy(out=cbst[:], in_=cbtr_ps[:]),
                               [cbtr_b], [cbst_b])
                        cb_done = Buf(tag + "cbd")
                        p.dact.emit(lambda e, half=half: e.dma_start(
                            out=cb_dram[:, half * HALF:(half + 1) * HALF],
                            in_=cbst[:]), [cbst_b], [cb_done, cb_B])
                    p.compute_fence()

                    with ExitStack() as ps4:
                        gate_ps = ps4.enter_context(
                            nc.psum_tensor(tag + "gate", [128, 4 * NFF], F32))
                        up_ps = ps4.enter_context(
                            nc.psum_tensor(tag + "up", [128, 4 * NFF], F32))
                        down_ps = ps4.enter_context(
                            nc.psum_tensor(tag + "down", [128, 8 * NFF], F32))
                        gate_b, up_b, down_b = (Buf(tag + "g"), Buf(tag + "u"),
                                                Buf(tag + "d"))

                        def load_cb(e_i):
                            cbs, cbs_b = cbslot[e_i % 3]
                            p.dpool.emit(lambda e, cbs=cbs, e_i=e_i, half=half: e.dma_start(
                                out=cbs[:],
                                in_=cb_dram[e_i, half * HALF:(half + 1) * HALF]
                                .unsqueeze(0).partition_broadcast(128)),
                                [cb_done], [cbs_b])

                        load_cb(0)
                        load_cb(1)
                        for c in range(NEXP // 2):
                            if c + 1 < NEXP // 2:
                                load_w(2 * c + 2)
                                load_cb(2 * c + 2)
                            for sub in range(NSUB):
                                t0 = sub * NFF
                                xsl = [xn[:, dc * HALF + t0: dc * HALF + t0 + NFF]
                                       for dc in range(NDC)]
                                ws = [(wslot[(2 * c + ei) % 3],
                                       cbslot[(2 * c + ei) % 3]) for ei in range(2)]
                                for ei in range(2):
                                    (wt, wt_b), (cbs, cbs_b) = ws[ei]
                                    gs, gs_b = gsil[ei]
                                    hh, hh_b = hbuf[ei]
                                    for off, psb, pb in ((0, gate_ps, gate_b),
                                                         (4096, up_ps, up_b)):
                                        for ic in range(4):
                                            for dc in range(NDC):
                                                PE.emit(
                                                    lambda e, ic=ic, dc=dc, wt=wt,
                                                    off=off, psb=psb, xsl=xsl: e.matmul(
                                                        psb[:, ic * NFF:(ic + 1) * NFF],
                                                        lhsT=wt[:, off + dc * 512 + ic * 128:
                                                                off + dc * 512 + (ic + 1) * 128],
                                                        rhs=xsl[dc],
                                                        start=(dc == 0 and ic % 2 == 0),
                                                        stop=(dc == 7 and ic % 2 == 1)),
                                                    [wt_b, xn_bs[dc]], [pb],
                                                    inc=(dc == 7 and ic == 3))
                                    A.emit(lambda e, gs=gs: e.activation(
                                        out=gs[:], in_=gate_ps[:], func=AF.Silu),
                                        [gate_b], [gs_b])
                                    V.emit(lambda e, gs=gs, hh=hh: e.tensor_mul(
                                        out=hh[:], in0=gs[:], in1=up_ps[:]),
                                        [gs_b, up_b], [hh_b])
                                    PL.emit(lambda e, cbs=cbs, hh=hh, t0=t0: e.tensor_tensor(
                                        out=hh[:].rearrange("p (i t) -> p i t", i=4),
                                        in0=hh[:].rearrange("p (i t) -> p i t", i=4),
                                        in1=cbs[:, t0:t0 + NFF].unsqueeze(1)
                                        .to_broadcast([128, 4, NFF]),
                                        op=ALU.mult), [hh_b, cbs_b], [hh_b])
                                for ei in range(2):
                                    (wt, wt_b), _cb = ws[ei]
                                    hh, hh_b = hbuf[ei]
                                    for dc in range(NDC):
                                        for ic in range(4):
                                            PE.emit(lambda e, ic=ic, dc=dc, ei=ei, wt=wt,
                                                    hh=hh: e.matmul(
                                                down_ps[:, dc * NFF:(dc + 1) * NFF],
                                                lhsT=wt[:, 8192 + ic * 1024 + dc * 128:
                                                        8192 + ic * 1024 + (dc + 1) * 128],
                                                rhs=hh[:, ic * NFF:(ic + 1) * NFF],
                                                start=(ei == 0 and dc % 2 == 0 and ic == 0),
                                                stop=(ei == 1 and dc % 2 == 1 and ic == 3)),
                                                [wt_b, hh_b], [down_b],
                                                inc=(dc == 7 and ic == 3))
                                Xr = X[:].rearrange("p (dc t) -> p dc t", dc=NDC)
                                Xsl = Xr[:, :, t0:t0 + NFF]
                                V.emit(lambda e, Xsl=Xsl: e.scalar_tensor_tensor(
                                    out=Xsl,
                                    in0=down_ps[:].rearrange("p (dc t) -> p dc t", dc=NDC),
                                    scalar=0.5, in1=Xsl, op0=ALU.mult, op1=ALU.add),
                                    [down_b], [X_b] + X_cells)
                                if c == NEXP // 2 - 1:
                                    p.dact.emit(lambda e, half=half, t0=t0: e.dma_start(
                                        out=dst_r[:, :, half * HALF + t0:
                                                  half * HALF + t0 + NFF],
                                        in_=X[:].rearrange(
                                            "p (dc t) -> p dc t",
                                            dc=NDC)[:, :, t0:t0 + NFF]),
                                        [X_b], [dst_B])
                            if c + 1 < NEXP // 2:
                                load_w(2 * c + 3)
                                load_cb(2 * c + 3)
                    p.compute_fence()

                for half in range(2):
                    do_half(half)
            p.barrier()

        if stop_after == "ffn1":
            ffn_pass("f1", x_d, None, out_d, out_B)
        else:
            ffn_pass("f1", x_d, None, Xd, Xd_B)

        if stop_after == "ffn1":
            p.barrier()
            block = ctx.enter_context(nc.Block())
            p.replay(block)
            return nc
        cc_k_done = Buf("cc_k")
        cc_v_done = [Buf("cc_v0"), Buf("cc_v1")]
        with ExitStack() as p2:
            sbl = lambda nm, sh, dtt: sb_in(p2, "p2" + nm, sh, dtt)
            bu_bf, bu_bf_b = sbl("bu_bf", [1, 2048], BF16)
            bv_bf, bv_bf_b = sbl("bv_bf", [1, 2048], BF16)
            p.dpool.emit(lambda e: e.dma_start(out=bu_bf[:], in_=bu_d), [], [bu_bf_b])
            p.dpool.emit(lambda e: e.dma_start(out=bv_bf[:], in_=bv_d), [], [bv_bf_b])
            xn2, xn2_b = sbl("xn2", [128, NDC * TPC], BF16)
            X, X_b = sbl("X", [128, NDC * HALF], F32)
            rows = {
                "m": sbl("m_row", [1, HALF], F32),
                "var": sbl("var_row", [1, HALF], F32),
                "rstd": sbl("rstd", [1, HALF], F32),
                "scr": sbl("scrrow", [1, HALF], F32),
                "scr16": [sbl(f"scr16_{i}", [128, HALF], BF16) for i in range(4)],
            }
            v2, v2_b = sbl("v2row", [1, HALF], F32)
            rfin, rfin_b = sbl("rfin", [1, HALF], F32)
            negmf, negmf_b = sbl("negmf", [1, HALF], F32)
            xt = [sbl(f"xt{i}", [128, HALF], F32) for i in range(2)]
            Xsrc_r = Xd.rearrange("(dc p) t -> p dc t", p=128)
            def p2_half(half):
                tag = f"p2h{half}_"
                p.dsync.emit(lambda e, half=half: e.dma_start(
                    out=X[:].rearrange("p (dc t) -> p dc t", dc=NDC),
                    in_=Xsrc_r[:, :, half * HALF:(half + 1) * HALF]),
                    [Xd_B], [X_b])
                (m_row, m_row_b) = rows["m"]
                (var_row, var_row_b) = rows["var"]
                (rstd, rstd_b) = rows["rstd"]
                (scr, scr_b) = rows["scr"]
                ln_stats_rows(tag, rows, X, X_b)
                p.dpool.emit(lambda e, half=half: e.dma_start(
                    out=rows_d[0:1, half * HALF:(half + 1) * HALF], in_=rstd[:]),
                    [rstd_b], [rowsd_B])
                V.emit(lambda e: e.scalar_tensor_tensor(
                    out=scr[:], in0=m_row[:],
                    scalar=-1.0, in1=rstd[:], op0=ALU.mult, op1=ALU.mult),
                    [m_row_b, rstd_b], [scr_b])
                p.dpool.emit(lambda e, half=half: e.dma_start(
                    out=rows_d[1:2, half * HALF:(half + 1) * HALF], in_=scr[:]),
                    [scr_b], [rowsd_B])
                V.emit(lambda e: e.tensor_mul(out=v2[:], in0=var_row[:], in1=rstd[:]),
                       [var_row_b, rstd_b], [v2_b])
                V.emit(lambda e: e.tensor_mul(out=v2[:], in0=v2[:], in1=rstd[:]),
                       [v2_b, rstd_b], [v2_b])
                A.emit(lambda e: e.activation(out=scr[:], in_=v2[:], func=AF.Ln,
                                              bias=eps_t[0:1, 0:1]),
                       [v2_b, eps_b], [scr_b])
                A.emit(lambda e: e.activation(out=v2[:], in_=scr[:], func=AF.Exp,
                                              scale=-0.5), [scr_b], [v2_b])
                V.emit(lambda e: e.tensor_mul(out=rfin[:], in0=rstd[:], in1=v2[:]),
                       [rstd_b, v2_b], [rfin_b])
                V.emit(lambda e: e.scalar_tensor_tensor(
                    out=negmf[:], in0=m_row[:], scalar=-1.0, in1=rfin[:],
                    op0=ALU.mult, op1=ALU.mult), [m_row_b, rfin_b], [negmf_b])
                with ExitStack() as ps3:
                    bps = []
                    for i, (row, row_b) in enumerate([(rfin, rfin_b),
                                                      (negmf, negmf_b)]):
                        psb = ps3.enter_context(
                            nc.psum_tensor(f"{tag}bc{i}", [128, HALF], F32))
                        pb = Buf(f"{tag}bc{i}")
                        for ns in range(2):
                            PE.emit(lambda e, psb=psb, ns=ns, row=row: e.matmul(
                                psb[:, ns * 512:(ns + 1) * 512], lhsT=ones1f[:],
                                rhs=row[0:1, ns * 512:(ns + 1) * 512],
                                start=True, stop=True),
                                [row_b, ones1f_b], [pb], inc=(ns == 1))
                        bps.append((psb, pb))
                    (rf_bc, rf_bc_b), (nm_bc, nm_bc_b) = bps
                    for dc in range(NDC):
                        t, t_b = xt[dc % 2]
                        V.emit(lambda e, t=t, dc=dc: e.tensor_mul(
                            out=t[:], in0=X[:, dc * HALF:(dc + 1) * HALF],
                            in1=rf_bc[:]), [X_b, rf_bc_b], [t_b])
                        V.emit(lambda e, t=t: e.tensor_add(
                            out=t[:], in0=t[:], in1=nm_bc[:]),
                            [t_b, nm_bc_b], [t_b])
                        A.emit(lambda e, t=t, dc=dc, half=half: e.activation(
                            out=xn2[:, dc * TPC + half * HALF:
                                    dc * TPC + (half + 1) * HALF],
                            in_=t[:], func=AF.Identity,
                            bias=lnb[:, dc:dc + 1], scale=lng[:, dc:dc + 1]),
                            [t_b, lng_b, lnb_b], [xn2_b])
                p.compute_fence()

            wuvV, wuvV_b = sbl("wuvV", [128, 8 * 2048], BF16)
            wuvU, wuvU_b = sbl("wuvU", [128, 8 * 2048], BF16)
            p.dpool.emit(lambda e: e.dma_start(
                out=wuvV[:].rearrange("p (dc j) -> p dc j", dc=8),
                in_=winV_d.rearrange("dc p j -> p dc j")), [], [wuvV_b])
            p.dpool.emit(lambda e: e.dma_start(
                out=wuvU[:].rearrange("p (dc j) -> p dc j", dc=8),
                in_=winU_d.rearrange("dc p j -> p dc j")), [], [wuvU_b])
            stg = [sbl(f"uvst{i}", [128, 2048], BF16) for i in range(2)]
            uvps_scope = p2.enter_context(ExitStack())
            uv_ps0 = uvps_scope.enter_context(nc.psum_tensor("p2uv0", [128, 2048], F32))
            uv_ps0_b = Buf("p2uv0")
            uv_ps = [uv_ps0, uv_ps0]
            uv_ps_b = [uv_ps0_b, uv_ps0_b]

            def uv_tile(tt, wuv, wuv_b, brow, brow_b, dram_dst, dst_buf):
                psb, pb = uv_ps[tt % 2], uv_ps_b[tt % 2]
                st, st_b = stg[tt % 2]
                for dc in range(NDC):
                    for ns in range(4):
                        PE.emit(lambda e, tt=tt, dc=dc, ns=ns, psb=psb,
                                wuv=wuv: e.matmul(
                            psb[:, ns * 512:(ns + 1) * 512],
                            lhsT=xn2[:, dc * TPC + tt * 128:
                                     dc * TPC + (tt + 1) * 128],
                            rhs=wuv[:, dc * 2048 + ns * 512:
                                    dc * 2048 + (ns + 1) * 512],
                            start=(dc == 0), stop=False),
                            [xn2_b, wuv_b], [pb], inc=False)
                for ns in range(4):
                    PE.emit(lambda e, ns=ns, psb=psb, brow=brow: e.matmul(
                        psb[:, ns * 512:(ns + 1) * 512], lhsT=ones1b[:],
                        rhs=brow[0:1, ns * 512:(ns + 1) * 512],
                        start=False, stop=True),
                        [ones1b_b, brow_b], [pb], inc=(ns == 3))
                A.emit(lambda e, st=st, psb=psb: e.activation(
                    out=st[:], in_=psb[:], func=AF.Silu), [pb], [st_b])
                p.dact.emit(lambda e, st=st, tt=tt, dram_dst=dram_dst: e.dma_start(
                    out=dram_dst[tt * 128:(tt + 1) * 128, :], in_=st[:]),
                    [st_b], [dst_buf])

            for half in range(2):
                p2_half(half)
                for tt in range(half * 8, half * 8 + 8):
                    uv_tile(tt, wuvV, wuvV_b, bv_bf, bv_bf_b, vown_d, vown_B[half])
                p.cc.emit(lambda e, hv=half: e.collective_compute(
                    "AllGather", ALU.bypass,
                    replica_groups=[[0, 1, 2, 3], [4, 5, 6, 7]],
                    ins=[vown_d[hv * HALF:(hv + 1) * HALF, :].opt()],
                    outs=[vall[hv].opt()]),
                    [vown_B[half]], [cc_v_done[half], vall_B[half]])

            wz, wz_b = sbl("wz", [128, 8 * 128], BF16)
            p.dpool.emit(lambda e: e.dma_start(
                out=wz[:].rearrange("p (dc j) -> p dc j", dc=8),
                in_=winZ_d.rearrange("dc p j -> p dc j")), [], [wz_b])
            z_s, z_s_b = sbl("zs", [128, 2 * 512], F32)
            kst, kst_b = sbl("kst", [128, TPC], BF16)
            with ExitStack() as zs_scope:
                z_ps = zs_scope.enter_context(
                    nc.psum_tensor("p2zps", [128, 2 * 512], F32))
                z_ps_b = Buf("p2zps")
                for g in range(4):
                    psl = z_ps[:, (g % 2) * 512:(g % 2 + 1) * 512]
                    zsl = z_s[:, (g % 2) * 512:(g % 2 + 1) * 512]
                    for dc in range(NDC):
                        PE.emit(lambda e, g=g, dc=dc, psl=psl: e.matmul(
                            psl, lhsT=wz[:, dc * 128:(dc + 1) * 128],
                            rhs=xn2[:, dc * TPC + g * 512: dc * TPC + (g + 1) * 512],
                            start=(dc == 0), stop=(dc == NDC - 1)),
                            [wz_b, xn2_b], [z_ps_b], inc=(dc == NDC - 1))
                    A.emit(lambda e, g=g, psl=psl, zsl=zsl: e.activation(
                        out=zsl, in_=psl, func=AF.Silu,
                        bias=bz[:]), [z_ps_b, bz_b], [z_s_b])
                    V.emit(lambda e, g=g, zsl=zsl: e.tensor_scalar(
                        out=zq_sb[:, g * 512:(g + 1) * 512], in0=zsl, scalar1=gqv[:],
                        scalar2=bqv[:], op0=ALU.mult, op1=ALU.add),
                        [z_s_b, gqv_b, bqv_b], [zq_sb_b])
                    V.emit(lambda e, g=g, zsl=zsl: e.tensor_scalar(
                        out=kst[:, g * 512:(g + 1) * 512], in0=zsl, scalar1=gkv[:],
                        scalar2=bkv[:], op0=ALU.mult, op1=ALU.add),
                        [z_s_b, gkv_b, bkv_b], [kst_b])
            p.compute_fence()
            p.dact.emit(lambda e: e.dma_start(out=kown_d, in_=kst[:]),
                        [kst_b], [kown_B])
            p.cc.emit(lambda e: e.collective_compute(
                "AllGather", ALU.bypass,
                replica_groups=[[0, 1, 2, 3], [4, 5, 6, 7]],
                ins=[kown_d.opt()], outs=[kall_d.opt()]),
                [kown_B], [cc_k_done, kall_B])

            for tt in range(NQT):
                uv_tile(tt, wuvU, wuvU_b, bu_bf, bu_bf_b, u_dram,
                        ud_B)
        p.barrier()

        if stop_after == "p2":
            p.barrier()
            block = ctx.enter_context(nc.Block())
            p.replay(block)
            return nc
        with ExitStack() as p4:
            sbl = lambda nm, sh, dtt: sb_in(p4, "p4" + nm, sh, dtt)
            zk_sb, zk_sb_b = sbl("zk", [128, SEQ], BF16)
            p.dsync.emit(lambda e: e.dma_start(
                out=zk_sb[:].rearrange("p (r t) -> p r t", r=4),
                in_=kall_d.rearrange("(r p) t -> p r t", p=128)),
                [kall_B], [zk_sb_b])
            wo_sb, wo_sb_b = sbl("wo", [128, 16384], BF16)
            p.dpool.emit(lambda e: e.dma_start(out=wo_sb[:], in_=wout_d),
                         [], [wo_sb_b])
            acc = [sbl(f"acc{i}", [128, 2048], F32) for i in range(4)]
            at_cache = [sbl(f"at{i}", [128, 8 * 512], BF16) for i in range(2)]
            vb = [sbl(f"vb{i}", [128, 8 * 2048], BF16) for i in range(2)]
            usl = [sbl(f"usl{i}", [128, 2048], BF16) for i in range(2)]
            x1t, x1t_b = sbl("x1t", [128, 1024], F32)
            x2st, x2st_b = sbl("x2st", [128, 1024], F32)
            rsb, rsb_b = sbl("rs", [128, 512], F32)
            ptok, ptok_b = sbl("ptok", [128, 2048], BF16)
            pfm, pfm_b = sbl("pfm", [128, 2048], BF16)
            r1_row, r1_row_b = sbl("r1row", [1, TPC], BF16)
            n1_row, n1_row_b = sbl("n1row", [1, TPC], BF16)
            p.dsync.emit(lambda e: e.dma_start(out=r1_row[:], in_=rows_d[0:1, :]),
                         [rowsd_B], [r1_row_b])
            p.dsync.emit(lambda e: e.dma_start(out=n1_row[:], in_=rows_d[1:2, :]),
                         [rowsd_B], [n1_row_b])
            sc_ps = p4.enter_context(nc.psum_tensor("p4sc", [128, 512], F32))
            sc_b = Buf("p4sc")
            av_ps = [p4.enter_context(nc.psum_tensor(f"p4av{i}", [128, 512], F32))
                     for i in range(4)]
            av_b = [Buf(f"p4av{i}") for i in range(4)]
            tr_ps = p4.enter_context(nc.psum_tensor("p4tr", [128, 1024], BF16))
            tr_b = Buf("p4tr")
            wout_ps = p4.enter_context(nc.psum_tensor("p4wout", [128, 512], F32))
            wout_b = Buf("p4wout")
            lnbc_ps = p4.enter_context(nc.psum_tensor("p4lnbc", [128, 256], F32))
            lnbc_b = Buf("p4lnbc")
            Xd_r = Xd.rearrange("(dc p) t -> p dc t", p=128)

            def emit_A(qg, kb):
                atc, atc_b = at_cache[kb % 2]
                for k8 in range(8):
                    PE.emit(lambda e, qg=qg, kb=kb, k8=k8: e.matmul(
                        sc_ps[:],
                        lhsT=zk_sb[:, (kb * 8 + k8) * 128:(kb * 8 + k8 + 1) * 128],
                        rhs=zq_sb[:, qg * 512:(qg + 1) * 512],
                        start=True, stop=True),
                        [zk_sb_b, zq_sb_b], [sc_b], inc=True)
                    A.emit(lambda e: e.activation(out=rsb[:], in_=sc_ps[:],
                                                  func=AF.Relu, scale=INV_L),
                           [sc_b], [rsb_b])
                    PL.emit(lambda e, k8=k8, atc=atc: e.tensor_mul(
                        out=atc[:, k8 * 512:(k8 + 1) * 512], in0=rsb[:], in1=rsb[:]),
                        [rsb_b], [atc_b])

            def load_vblock(kb):
                buf, buf_b = vb[kb % 2]
                r = kb // 2
                src = vall[kb % 2]
                srcr = src.rearrange("(r kt p) ee -> p r kt ee", p=128, kt=8)
                p.dsync.emit(lambda e, buf=buf, r=r, srcr=srcr: e.dma_start(
                    out=buf[:].rearrange("p (kt ee) -> p kt ee", kt=8),
                    in_=srcr[:, r]), [cc_v_done[kb % 2]], [buf_b])

            def emit_B(qg, kb, first):
                atc, atc_b = at_cache[kb % 2]
                buf, buf_b = vb[kb % 2]
                for qt4 in range(4):
                    a4, a4_b = acc[qt4]
                    for ns in range(4):
                        for k8 in range(8):
                            PE.emit(lambda e, qt4=qt4, ns=ns, k8=k8, atc=atc,
                                    buf=buf: e.matmul(
                                av_ps[ns][:],
                                lhsT=atc[:, k8 * 512 + qt4 * 128:
                                         k8 * 512 + (qt4 + 1) * 128],
                                rhs=buf[:, k8 * 2048 + ns * 512:
                                        k8 * 2048 + (ns + 1) * 512],
                                start=(k8 == 0), stop=(k8 == 7)),
                                [atc_b, buf_b], [av_b[ns]], inc=(k8 == 7))
                        if first:
                            V.emit(lambda e, a4=a4, ns=ns: e.tensor_copy(
                                out=a4[:, ns * 512:(ns + 1) * 512],
                                in_=av_ps[ns][:]), [av_b[ns]], [a4_b])
                        else:
                            V.emit(lambda e, a4=a4, ns=ns: e.tensor_add(
                                out=a4[:, ns * 512:(ns + 1) * 512],
                                in0=a4[:, ns * 512:(ns + 1) * 512],
                                in1=av_ps[ns][:]), [av_b[ns], a4_b], [a4_b])

            load_vblock(0)
            load_vblock(1)
            emit_A(0, 0)
            for qg in range(4):
                for kb in range(8):
                    if kb + 1 < 8:
                        emit_A(qg, kb + 1)
                    emit_B(qg, kb, first=(kb == 0))
                    if kb + 2 < 8:
                        load_vblock(kb + 2)
                if qg + 1 < 4:
                    load_vblock(0)
                    load_vblock(1)
                    emit_A(qg + 1, 0)
                for qt4 in range(4):
                    qt = qg * 4 + qt4
                    a4, a4_b = acc[qt4]
                    u_t, u_tb = usl[qt % 2]
                    p.dsync.emit(lambda e, u_t=u_t, qt=qt: e.dma_start(
                        out=u_t[:], in_=u_dram[qt * 128:(qt + 1) * 128, :]),
                        [ud_B], [u_tb])
                    p.dsync.emit(lambda e, qt=qt: e.dma_start(
                        out=x1t[:].rearrange("p (dc t) -> p dc t", dc=NDC),
                        in_=Xd_r[:, :, qt * 128:(qt + 1) * 128]),
                        [Xd_B], [x1t_b])
                    V.emit(lambda e, u_t=u_t, a4=a4: e.tensor_mul(
                        out=ptok[:], in0=a4[:], in1=u_t[:]),
                        [a4_b, u_tb], [ptok_b])
                    for half8 in range(2):
                        for j in range(8):
                            ec = half8 * 8 + j
                            PE.emit(lambda e, ec=ec, j=j: e.transpose(
                                tr_ps[:, j * 128:(j + 1) * 128],
                                in_=ptok[:, ec * 128:(ec + 1) * 128],
                                identity=ident[:]),
                                [ptok_b, ident_b], [tr_b], inc=(j == 7))
                        V.emit(lambda e, half8=half8: e.tensor_copy(
                            out=pfm[:, half8 * 1024:(half8 + 1) * 1024],
                            in_=tr_ps[:]), [tr_b], [pfm_b])
                    PE.emit(lambda e, qt=qt: e.matmul(
                        lnbc_ps[:, 0:128], lhsT=ones1b[:],
                        rhs=r1_row[0:1, qt * 128:(qt + 1) * 128],
                        start=True, stop=True),
                        [r1_row_b, ones1b_b], [lnbc_b], inc=False)
                    PE.emit(lambda e, qt=qt: e.matmul(
                        lnbc_ps[:, 128:256], lhsT=ones1b[:],
                        rhs=n1_row[0:1, qt * 128:(qt + 1) * 128],
                        start=True, stop=True),
                        [n1_row_b, ones1b_b], [lnbc_b], inc=True)
                    V.emit(lambda e: e.tensor_tensor(
                        out=x2st[:].rearrange("p (dc t) -> p dc t", dc=NDC),
                        in0=x1t[:].rearrange("p (dc t) -> p dc t", dc=NDC),
                        in1=lnbc_ps[:, 0:128].unsqueeze(1)
                        .to_broadcast([128, 8, 128]),
                        op=ALU.mult), [x1t_b, lnbc_b], [x2st_b])
                    V.emit(lambda e: e.tensor_tensor(
                        out=x2st[:].rearrange("p (dc t) -> p dc t", dc=NDC),
                        in0=x2st[:].rearrange("p (dc t) -> p dc t", dc=NDC),
                        in1=lnbc_ps[:, 128:256].unsqueeze(1)
                        .to_broadcast([128, 8, 128]),
                        op=ALU.add), [x2st_b, lnbc_b], [x2st_b])
                    V.emit(lambda e: e.tensor_add(out=x2st[:], in0=x2st[:],
                                                  in1=x1t[:]),
                           [x2st_b, x1t_b], [x2st_b])
                    V.emit(lambda e: e.tensor_tensor(
                        out=x2st[:].rearrange("p (dc t) -> p dc t", dc=NDC),
                        in0=x2st[:].rearrange("p (dc t) -> p dc t", dc=NDC),
                        in1=bout[:].unsqueeze(2).to_broadcast([128, 8, 128]),
                        op=ALU.add), [x2st_b, bout_b], [x2st_b])
                    for dcb in range(2):
                        for dc4 in range(4):
                            dc = dcb * 4 + dc4
                            for ec in range(16):
                                PE.emit(lambda e, dc=dc, dc4=dc4, ec=ec: e.matmul(
                                    wout_ps[:, dc4 * 128:(dc4 + 1) * 128],
                                    lhsT=wo_sb[:, (ec * 8 + dc) * 128:
                                               (ec * 8 + dc + 1) * 128],
                                    rhs=pfm[:, ec * 128:(ec + 1) * 128],
                                    start=(ec == 0), stop=(ec == 15)),
                                    [wo_sb_b, pfm_b], [wout_b],
                                    inc=(dc4 == 3 and ec == 15))
                        V.emit(lambda e, dcb=dcb: e.tensor_add(
                            out=x2st[:, dcb * 512:(dcb + 1) * 512],
                            in0=x2st[:, dcb * 512:(dcb + 1) * 512],
                            in1=wout_ps[:]), [wout_b, x2st_b], [x2st_b])
                    p.dact.emit(lambda e, qt=qt: e.dma_start(
                        out=Xd_r[:, :, qt * 128:(qt + 1) * 128],
                        in_=x2st[:].rearrange("p (dc t) -> p dc t", dc=NDC)),
                        [x2st_b], [Xd_B])
        p.barrier()

        if stop_after == "attn":
            ob = Buf("odbg")
            p.dsync.emit(lambda e: e.dma_start(out=out_d, in_=Xd), [Xd_B], [ob])
            p.barrier()
            block = ctx.enter_context(nc.Block())
            p.replay(block)
            return nc
        ffn_pass("f2", Xd, Xd_B, out_d, out_B)

        p.barrier()
        block = ctx.enter_context(nc.Block())
        p.replay(block)
    return nc


STOP_AFTER = None


def kernel(**inputs):
    x = np.asarray(inputs["L0_attn_x"])
    W_in = np.asarray(inputs["L0_attn_W_in"])
    b_in = np.asarray(inputs["L0_attn_b_in"])
    W_out = np.asarray(inputs["L0_attn_W_out"])
    b_out = np.asarray(inputs["L0_attn_b_out"])
    ln_g = np.asarray(inputs["L0_attn_ln_g"])
    ln_b = np.asarray(inputs["L0_attn_ln_b"])
    gq = np.asarray(inputs["L0_attn_gamma_q"])
    bq = np.asarray(inputs["L0_attn_beta_q"])
    gk = np.asarray(inputs["L0_attn_gamma_k"])
    bk = np.asarray(inputs["L0_attn_beta_k"])
    Wr = np.asarray(inputs["L0_ffn_router_weight"])
    rbias = np.asarray(inputs["L0_ffn_router_bias"])
    w_gate = np.asarray(inputs["L0_ffn_w_gate"])
    w_up = np.asarray(inputs["L0_ffn_w_up"])
    w_down = np.asarray(inputs["L0_ffn_w_down"])

    expw = np.empty((NEXP, 128, 12288), np.float32)
    for e in range(NEXP):
        WgT = w_gate[e].T
        WuT = w_up[e].T
        WdT = w_down[e].T
        expw[e, :, 0:4096] = WgT.reshape(8, 128, 512).transpose(1, 0, 2).reshape(128, 4096)
        expw[e, :, 4096:8192] = WuT.reshape(8, 128, 512).transpose(1, 0, 2).reshape(128, 4096)
        expw[e, :, 8192:12288] = WdT.reshape(4, 128, 1024).transpose(1, 0, 2).reshape(128, 4096)
    shared = dict(
        expw=expw,
        winU=np.ascontiguousarray(W_in[:, 0:2048].reshape(8, 128, 2048)),
        winV=np.ascontiguousarray(W_in[:, 2048:4096].reshape(8, 128, 2048)),
        winZ=np.ascontiguousarray(W_in[:, 4096:4224].reshape(8, 128, 128)),
        wout=np.ascontiguousarray(
            W_out.reshape(16, 128, 8, 128).transpose(1, 0, 2, 3).reshape(128, 16384)),
        wr=np.ascontiguousarray(Wr.T.reshape(8, 128, 16)),
        bu=np.ascontiguousarray(b_in[None, 0:2048].astype(np.float32)),
        bv=np.ascontiguousarray(b_in[None, 2048:4096].astype(np.float32)),
        bz=np.ascontiguousarray(b_in[4096:4224].reshape(128, 1)),
        lng=np.ascontiguousarray(ln_g.reshape(8, 128).T),
        lnb=np.ascontiguousarray(ln_b.reshape(8, 128).T),
        bout=np.ascontiguousarray(b_out.reshape(8, 128).T),
        gq=np.ascontiguousarray(gq.reshape(128, 1)),
        bq=np.ascontiguousarray(bq.reshape(128, 1)),
        gk=np.ascontiguousarray(gk.reshape(128, 1)),
        bk=np.ascontiguousarray(bk.reshape(128, 1)),
        rb=np.ascontiguousarray(rbias[None, :].astype(np.float32)),
        ident=np.eye(128, dtype=ml_dtypes.bfloat16),
        onesc=np.ones((128, 1), dtype=ml_dtypes.bfloat16),
        ones1b=np.ones((1, 128), dtype=ml_dtypes.bfloat16),
        ones1f=np.ones((1, 128), dtype=np.float32),
        ones11=np.ones((1, 1), dtype=np.float32),
    )
    in_maps = []
    for c in range(8):
        b = c // 4
        t0 = (c % 4) * TPC
        m = dict(shared)
        m["x"] = np.ascontiguousarray(x[b, t0:t0 + TPC, :].T)
        in_maps.append(m)

    nc = build_nc(stop_after=STOP_AFTER)
    res = run_bass_kernel_spmd(nc, in_maps, core_ids=list(range(8)))
    out = np.empty((2, 8192, 1024), np.float32)
    for c in range(8):
        b = c // 4
        t0 = (c % 4) * TPC
        out[b, t0:t0 + TPC, :] = res.results[c]["out"].T
    return out


# version 08
# speedup vs baseline: 1.0102x; 1.0099x over previous
import numpy as np
import ml_dtypes
from contextlib import ExitStack
import concourse.bass as bass
import concourse.mybir as mybir
from concourse.bass_utils import run_bass_kernel_spmd

F32 = mybir.dt.float32
BF16 = mybir.dt.bfloat16
AF = mybir.ActivationFunctionType
ALU = mybir.AluOpType
AXX = mybir.AxisListType.X



class Buf:
    __slots__ = ("name", "w", "rs")

    def __init__(self, name=""):
        self.name = name
        self.w = []
        self.rs = []


class Stream:
    def __init__(self, name):
        self.name = name
        self.records = []
        self.max_waited = {}


class Q:
    def __init__(self, stream, sem, step=1, name=""):
        self.stream = stream
        self.sem = sem
        self.step = step
        self.n = 0
        self.name = name
        self.pend_w = []
        self.pend_r = []

    def emit(self, build, reads=(), writes=(), inc=True):
        need = {}
        if self.step == 16 and self.n > 0:
            need[self] = self.n
        for b in reads:
            for (q, v) in b.w:
                if need.get(q, -1) < v:
                    need[q] = v
        for b in writes:
            for (q, v) in b.w:
                if need.get(q, -1) < v:
                    need[q] = v
            for (q, v) in b.rs:
                if need.get(q, -1) < v:
                    need[q] = v
        waits = []
        st = self.stream
        for q, v in need.items():
            assert v <= q.n, f"wait on unproduced token {q.name}:{v}"
            if st.max_waited.get(q, -1) < v:
                waits.append((q.sem, v))
                st.max_waited[q] = v
        st.records.append((waits, build, (self.sem, self.step) if inc else None))
        for b in writes:
            self.pend_w.append(b)
        for b in reads:
            self.pend_r.append(b)
        if inc:
            self.n += self.step
            tok = (self, self.n)
            for b in self.pend_w:
                b.w.append(tok)
                b.rs = []
            for b in self.pend_r:
                b.rs.append(tok)
            self.pend_w = []
            self.pend_r = []

    def seal(self):
        if self.pend_w or self.pend_r:
            raise RuntimeError(f"{self.name}: pending accesses without an inc")

    def token(self):
        self.seal()
        return (self, self.n)


class DmaQ:
    def __init__(self, stream, sems, name):
        self.subqs = [Q(stream, s, 16, f"{name}{i}") for i, s in enumerate(sems)]
        self.i = 0
        self.name = name

    def emit(self, build, reads=(), writes=()):
        q = self.subqs[self.i]
        self.i = (self.i + 1) % len(self.subqs)
        return q.emit(build, reads, writes, inc=True)

    def qs(self):
        return list(self.subqs)


class Prog:
    def __init__(self, nc, ctx, n_dsync=12, n_dact=10, n_dpool=6):
        self.nc = nc
        self.streams = {n: Stream(n) for n in
                        ("tensor", "vector", "scalar", "gpsimd", "sync")}

        def sem(name):
            return ctx.enter_context(nc.semaphore(name))
        self.pe = Q(self.streams["tensor"], sem("q_pe"), 1, "pe")
        self.dve = Q(self.streams["vector"], sem("q_dve"), 1, "dve")
        self.act = Q(self.streams["scalar"], sem("q_act"), 1, "act")
        self.pool = Q(self.streams["gpsimd"], sem("q_pool"), 1, "pool")
        self.dsync = DmaQ(self.streams["sync"],
                          [sem(f"q_dsync{i}") for i in range(n_dsync)], "dsync")
        self.dact = DmaQ(self.streams["scalar"],
                         [sem(f"q_dact{i}") for i in range(n_dact)], "dact")
        self.dpool = DmaQ(self.streams["gpsimd"],
                          [sem(f"q_dpool{i}") for i in range(n_dpool)], "dpool")
        self.cc = Q(self.streams["gpsimd"], sem("q_cc"), 1, "cc")
        self.all_qs = ([self.pe, self.dve, self.act, self.pool, self.cc]
                       + self.dsync.qs() + self.dact.qs() + self.dpool.qs())

    def barrier(self, qs=None):
        qs = qs or self.all_qs
        toks = [q.token() for q in qs]
        for q in qs:
            st = q.stream
            waits = []
            for (q2, v) in toks:
                if v > 0 and st.max_waited.get(q2, -1) < v:
                    waits.append((q2.sem, v))
                    st.max_waited[q2] = v
            if waits:
                st.records.append((waits, None, None))

    def compute_fence(self):
        self.barrier([self.pe, self.dve, self.act])

    def replay(self, block):
        eng_of = {"tensor": block.tensor, "vector": block.vector,
                  "scalar": block.scalar, "gpsimd": block.gpsimd,
                  "sync": block.sync}

        def mk(stream):
            def body(eng):
                for waits, build, inc in stream.records:
                    if build is None:
                        for (sem, val) in waits:
                            eng.wait_ge(sem, val)
                        continue
                    for (sem, val) in waits[:-1]:
                        eng.wait_ge(sem, val)
                    inst = build(eng)
                    if waits:
                        inst._wait_ge(waits[-1][0], waits[-1][1])
                    if inc is not None:
                        inst.then_inc(inc[0], inc[1])
            return body
        for name, stream in self.streams.items():
            if stream.records:
                eng_of[name](mk(stream))


TPC = 2048
HALF = 1024
NSUB = 4
NFF = 256
NDC = 8
NEXP = 16
SEQ = 8192
NKT = 64
NQT = 16
EU = 2048
LN_EPS = 1e-5
INV_L = 1.0 / 8192.0


def build_nc(stop_after=None):
    nc = bass.Bass("TRN2", target_bir_lowering=False)
    dt = nc.dram_tensor
    x_d = dt("x", [1024, TPC], F32, kind="ExternalInput").ap()
    expw_d = dt("expw", [NEXP, 128, 12288], F32, kind="ExternalInput").ap()
    winU_d = dt("winU", [8, 128, 2048], F32, kind="ExternalInput").ap()
    winV_d = dt("winV", [8, 128, 2048], F32, kind="ExternalInput").ap()
    winZ_d = dt("winZ", [8, 128, 128], F32, kind="ExternalInput").ap()
    wout_d = dt("wout", [128, 16384], F32, kind="ExternalInput").ap()
    wr_d = dt("wr", [8, 128, 16], F32, kind="ExternalInput").ap()
    bu_d = dt("bu", [1, 2048], F32, kind="ExternalInput").ap()
    bv_d = dt("bv", [1, 2048], F32, kind="ExternalInput").ap()
    bz_d = dt("bz", [128, 1], F32, kind="ExternalInput").ap()
    lng_d = dt("lng", [128, 8], F32, kind="ExternalInput").ap()
    lnb_d = dt("lnb", [128, 8], F32, kind="ExternalInput").ap()
    bout_d = dt("bout", [128, 8], F32, kind="ExternalInput").ap()
    gq_d = dt("gq", [128, 1], F32, kind="ExternalInput").ap()
    bq_d = dt("bq", [128, 1], F32, kind="ExternalInput").ap()
    gk_d = dt("gk", [128, 1], F32, kind="ExternalInput").ap()
    bk_d = dt("bk", [128, 1], F32, kind="ExternalInput").ap()
    rb_d = dt("rb", [1, 16], F32, kind="ExternalInput").ap()
    ident_d = dt("ident", [128, 128], BF16, kind="ExternalInput").ap()
    onesc_d = dt("onesc", [128, 1], BF16, kind="ExternalInput").ap()
    ones1b_d = dt("ones1b", [1, 128], BF16, kind="ExternalInput").ap()
    ones1f_d = dt("ones1f", [1, 128], F32, kind="ExternalInput").ap()
    ones11_d = dt("ones11", [1, 1], F32, kind="ExternalInput").ap()
    out_d = dt("out", [1024, TPC], F32, kind="ExternalOutput").ap()
    Xd = dt("Xd", [1024, TPC], F32).ap()
    u_dram = dt("u_dram", [TPC, EU], BF16).ap()
    vown_d = dt("vown", [HALF, EU], BF16).ap()
    vkown_d = dt("vkown", [HALF + 128, EU], BF16).ap()
    vall = [dt("vall0", [4 * HALF, EU], BF16).ap(),
            dt("vall1", [4 * (HALF + 128), EU], BF16).ap()]
    cb_dram = dt("cb_dram", [NEXP, TPC], BF16).ap()
    rows_d = dt("rows_d", [2, TPC], BF16).ap()
    r1f_d = dt("r1f_d", [1, TPC], F32).ap()
    rowsb_d = dt("rowsb_d", [2, HALF], F32).ap()

    Xd_B = Buf("Xd")
    out_B = Buf("outd")
    ud_B = Buf("u_dram")
    vown_B = [Buf("vown0"), Buf("vown1")]
    vall_B = [Buf("vall0"), Buf("vall1")]
    cb_B = Buf("cb_dram")
    rowsd_B = Buf("rows_d")
    rowsb_B = Buf("rowsb_d")

    ctx = ExitStack()
    with ctx:
        p = Prog(nc, ctx)
        PE, V, A, PL = p.pe, p.dve, p.act, p.pool

        def sb_in(scope, name, shape, dtype):
            t = scope.enter_context(nc.sbuf_tensor(name, shape, dtype))
            return t, Buf(name)

        sb = lambda nm, sh, dtt: sb_in(ctx, nm, sh, dtt)

        ident, ident_b = sb("c_ident", [128, 128], BF16)
        onesc, onesc_b = sb("c_onesc", [128, 1], BF16)
        ones1b, ones1b_b = sb("c_ones1b", [1, 128], BF16)
        ones1f, ones1f_b = sb("c_ones1f", [1, 128], F32)
        ones11, ones11_b = sb("c_ones11", [1, 1], F32)
        bz, bz_b = sb("c_bz", [128, 1], F32)
        lng, lng_b = sb("c_lng", [128, 8], F32)
        lnb, lnb_b = sb("c_lnb", [128, 8], F32)
        bout, bout_b = sb("c_bout", [128, 8], F32)
        gqv, gqv_b = sb("gqv", [128, 1], F32)
        bqv, bqv_b = sb("bqv", [128, 1], F32)
        gkv, gkv_b = sb("gkv", [128, 1], F32)
        bkv, bkv_b = sb("bkv", [128, 1], F32)
        rb_bc, rb_bc_b = sb("c_rb_bc", [128, 16], F32)
        cw_bc, cw_bc_b = sb("c_cw_bc", [128, 16], F32)
        zq_sb, zq_sb_b = sb("c_zq_sb", [128, TPC], BF16)
        wr_sb, wr_sb_b = sb("c_wr_sb", [128, 8 * 16], F32)
        eps_t, eps_b = sb("c_eps", [1, 1], F32)
        V.emit(lambda e: e.memset(eps_t[:], LN_EPS), [], [eps_b])


        for dst, dst_b, src in [
                (ident, ident_b, ident_d), (onesc, onesc_b, onesc_d),
                (ones1b, ones1b_b, ones1b_d), (ones1f, ones1f_b, ones1f_d),
                (ones11, ones11_b, ones11_d), (bz, bz_b, bz_d),
                (lng, lng_b, lng_d), (lnb, lnb_b, lnb_d), (bout, bout_b, bout_d),
                (gqv, gqv_b, gq_d), (bqv, bqv_b, bq_d), (gkv, gkv_b, gk_d),
                (bkv, bkv_b, bk_d)]:
            p.dsync.emit(lambda e, dst=dst, src=src: e.dma_start(out=dst[:], in_=src),
                         [], [dst_b])
        p.dsync.emit(lambda e: e.dma_start(
            out=wr_sb[:].rearrange("p (dc j) -> p dc j", dc=8),
            in_=wr_d.rearrange("dc p j -> p dc j")), [], [wr_sb_b])
        p.dsync.emit(lambda e: e.dma_start(out=rb_bc[:], in_=rb_d.partition_broadcast(128)),
                     [], [rb_bc_b])

        with ExitStack() as cws:
            cw_ps = cws.enter_context(nc.psum_tensor("cw_ps", [1, 16], F32))
            cw_ps_b = Buf("cw_ps")
            onesf_t, onesf_b = sb_in(cws, "onesf128", [128, 1], F32)
            V.emit(lambda e: e.memset(onesf_t[:], 1.0), [], [onesf_b])
            for dc in range(8):
                PE.emit(lambda e, dc=dc: e.matmul(
                    cw_ps[:], lhsT=onesf_t[:], rhs=wr_sb[:, dc * 16:(dc + 1) * 16],
                    start=(dc == 0), stop=(dc == 7)),
                    [onesf_b, wr_sb_b], [cw_ps_b], inc=(dc == 7))
            cw_row, cw_row_b = sb_in(cws, "cw_row", [1, 16], F32)
            V.emit(lambda e: e.tensor_copy(out=cw_row[:], in_=cw_ps[:]),
                   [cw_ps_b], [cw_row_b])
            cwb_ps = cws.enter_context(nc.psum_tensor("cwb_ps", [128, 16], F32))
            cwb_b = Buf("cwb_ps")
            PE.emit(lambda e: e.matmul(cwb_ps[:], lhsT=ones1f[:], rhs=cw_row[:],
                                       start=True, stop=True),
                    [ones1f_b, cw_row_b], [cwb_b])
            V.emit(lambda e: e.tensor_copy(out=cw_bc[:], in_=cwb_ps[:]),
                   [cwb_b], [cw_bc_b])
        p.barrier()

        def ln_stats_rows(tag, rows, X, X_b, X_cells=None):
            (m_row, m_row_b) = rows["m"]
            (var_row, var_row_b) = rows["var"]
            (rstd, rstd_b) = rows["rstd"]
            (scr, scr_b) = rows["scr"]
            scr16 = rows["scr16"]
            with ExitStack() as ps_scope:
                stats_ps = ps_scope.enter_context(
                    nc.psum_tensor(tag + "stats", [1, 2 * HALF], F32))
                stats_b = Buf(tag + "stats")
                for dc in range(NDC):
                    tb, tb_b = scr16[dc % 2]
                    ts, ts_b = scr16[2 + dc % 2]
                    xsl = X[:, dc * HALF:(dc + 1) * HALF]
                    xdep = [X_cells[dc]] if X_cells else [X_b]
                    V.emit(lambda e, tb=tb, xsl=xsl: e.tensor_copy(out=tb[:], in_=xsl),
                           xdep, [tb_b])
                    A.emit(lambda e, ts=ts, xsl=xsl: e.activation(
                        out=ts[:], in_=xsl, func=AF.Square), xdep, [ts_b])
                    for ns in range(2):
                        PE.emit(lambda e, tb=tb, ns=ns, dc=dc: e.matmul(
                            stats_ps[0:1, ns * 512:(ns + 1) * 512],
                            lhsT=onesc[:], rhs=tb[:, ns * 512:(ns + 1) * 512],
                            start=(dc == 0), stop=(dc == NDC - 1)),
                            [tb_b, onesc_b], [stats_b], inc=False)
                        PE.emit(lambda e, ts=ts, ns=ns, dc=dc: e.matmul(
                            stats_ps[0:1, HALF + ns * 512:HALF + (ns + 1) * 512],
                            lhsT=onesc[:], rhs=ts[:, ns * 512:(ns + 1) * 512],
                            start=(dc == 0), stop=(dc == NDC - 1)),
                            [ts_b, onesc_b], [stats_b],
                            inc=(ns == 1))
                A.emit(lambda e: e.mul(m_row[:], stats_ps[0:1, 0:HALF], 1.0 / 1024.0),
                       [stats_b], [m_row_b])
                A.emit(lambda e: e.activation(out=scr[:], in_=m_row[:], func=AF.Square),
                       [m_row_b], [scr_b])
                V.emit(lambda e: e.scalar_tensor_tensor(
                    out=var_row[:], in0=stats_ps[0:1, HALF:2 * HALF],
                    scalar=1.0 / 1024.0, in1=scr[:],
                    op0=ALU.mult, op1=ALU.subtract), [stats_b, scr_b], [var_row_b])
                A.emit(lambda e: e.activation(out=scr[:], in_=var_row[:], func=AF.Ln,
                                              bias=eps_t[0:1, 0:1]),
                       [var_row_b, eps_b], [scr_b])
                A.emit(lambda e: e.activation(out=rstd[:], in_=scr[:], func=AF.Exp,
                                              scale=-0.5), [scr_b], [rstd_b])
            p.compute_fence()

        def ffn_pass(pass_tag, src, src_B, dst, dst_B):
            src_r = src.rearrange("(dc p) t -> p dc t", p=128)
            dst_r = dst.rearrange("(dc p) t -> p dc t", p=128)
            with ExitStack() as fs:
                sbl = lambda nm, sh, dtt: sb_in(fs, pass_tag + nm, sh, dtt)
                X, X_b = sbl("X", [128, NDC * HALF], F32)
                X_cells = [Buf(f"{pass_tag}Xc{dc}") for dc in range(NDC)]
                xn, _xn_b = sbl("xn", [128, NDC * HALF], BF16)
                xn_bs = [Buf(f"{pass_tag}xn{dc}") for dc in range(NDC)]
                wslot = [sbl(f"w{i}", [128, 12288], BF16) for i in range(3)]
                cbslot = [sbl(f"cb{i}", [128, HALF], BF16) for i in range(3)]
                gsil = [sbl(f"gs{i}", [128, 4 * NFF], BF16) for i in range(2)]
                hbuf = [sbl(f"h{i}", [128, 4 * NFF], BF16) for i in range(2)]
                rows = {
                    "m": sbl("m_row", [1, HALF], F32),
                    "var": sbl("var_row", [1, HALF], F32),
                    "rstd": sbl("rstd", [1, HALF], F32),
                    "scr": sbl("scrrow", [1, HALF], F32),
                    "scr16": [sbl(f"scr16_{i}", [128, HALF], BF16) for i in range(4)],
                }
                negm, negm_b = sbl("negm", [1, HALF], F32)
                tokv, tokv_b = sbl("tokv", [128, 16], F32)
                xt = [sbl(f"xt{i}", [128, HALF], F32) for i in range(2)]
                rbc_sb, rbc_b = sbl("rbc", [128, HALF], F32)
                nbc_sb, nbc_b = sbl("nbc", [128, HALF], F32)
                rt = {nm: sbl(nm, [128, 128], F32) for nm in
                      ("lgr", "logit", "score", "sch", "emask", "mk0", "mkw",
                       "eqm", "sel", "wraw")}
                pr, pr_b = sbl("pr", [128, 32 * 6], F32)
                gsum, gsum_b = sbl("gsum", [128, 32], F32)
                gmin, gmin_b = sbl("gmin", [128, 8 * 6], F32)
                thr2, thr2_b = sbl("thr2", [128, 8], F32)
                gmask, gmask_b = sbl("gmask", [128, 32], F32)
                mx, mx_b = sbl("mx", [128, 8], F32)
                wsum, wsum_b = sbl("wsum", [128, 8], F32)
                winv, winv_b = sbl("winv", [128, 8], F32)
                cbt, cbt_b = sbl("cbt", [128, 128], BF16)
                cbst, cbst_b = sbl("cbst", [16, 1024], BF16)

                def load_w(e_i):
                    wt, wt_b = wslot[e_i % 3]
                    p.dpool.emit(lambda e, wt=wt, e_i=e_i: e.dma_start(
                        out=wt[:], in_=expw_d[e_i]), [], [wt_b])

                def do_half(half):
                    tag = f"{pass_tag}h{half}_"
                    for dc in range(NDC):
                        p.dsync.emit(lambda e, half=half, dc=dc: e.dma_start(
                            out=X[:, dc * HALF:(dc + 1) * HALF],
                            in_=src_r[:, dc, half * HALF:(half + 1) * HALF]),
                            [src_B] if src_B else [], [X_cells[dc], X_b])
                    load_w(0)
                    load_w(1)
                    (m_row, m_row_b) = rows["m"]
                    (rstd, rstd_b) = rows["rstd"]
                    ln_stats_rows(tag, rows, X, X_b, X_cells)
                    V.emit(lambda e: e.scalar_tensor_tensor(
                        out=negm[:], in0=m_row[:], scalar=-1.0, in1=rstd[:],
                        op0=ALU.mult, op1=ALU.mult), [m_row_b, rstd_b], [negm_b])

                    p.dact.emit(lambda e: e.dma_start(out=rowsb_d[0:1, :], in_=rstd[:]),
                                [rstd_b], [rowsb_B])
                    p.dact.emit(lambda e: e.dma_start(out=rowsb_d[1:2, :], in_=negm[:]),
                                [negm_b], [rowsb_B])
                    p.dsync.emit(lambda e: e.dma_start(
                        out=rbc_sb[:], in_=rowsb_d[0:1, :].partition_broadcast(128)),
                        [rowsb_B], [rbc_b])
                    p.dsync.emit(lambda e: e.dma_start(
                        out=nbc_sb[:], in_=rowsb_d[1:2, :].partition_broadcast(128)),
                        [rowsb_B], [nbc_b])
                    for dc in range(NDC):
                        t, t_b = xt[dc % 2]
                        V.emit(lambda e, t=t, dc=dc: e.tensor_mul(
                            out=t[:], in0=X[:, dc * HALF:(dc + 1) * HALF],
                            in1=rbc_sb[:]), [X_cells[dc], rbc_b], [t_b])
                        V.emit(lambda e, t=t, dc=dc: e.tensor_add(
                            out=xn[:, dc * HALF:(dc + 1) * HALF], in0=t[:],
                            in1=nbc_sb[:]), [t_b, nbc_b], [xn_bs[dc]])

                    with ExitStack() as ps2:
                        tok_ps = ps2.enter_context(
                            nc.psum_tensor(tag + "tokps", [128, 16], F32))
                        tok_ps_b = Buf(tag + "tokps")
                        for tt in range(8):
                            PE.emit(lambda e, tt=tt: e.matmul(
                                tok_ps[:, tt:tt + 1],
                                lhsT=rstd[0:1, tt * 128:(tt + 1) * 128],
                                rhs=ones11[:], start=True, stop=True),
                                [rstd_b, ones11_b], [tok_ps_b], inc=False)
                            PE.emit(lambda e, tt=tt: e.matmul(
                                tok_ps[:, 8 + tt:9 + tt],
                                lhsT=negm[0:1, tt * 128:(tt + 1) * 128],
                                rhs=ones11[:], start=True, stop=True),
                                [negm_b, ones11_b], [tok_ps_b], inc=(tt == 7))
                        V.emit(lambda e: e.tensor_copy(out=tokv[:], in_=tok_ps[:]),
                               [tok_ps_b], [tokv_b])

                        lg_ps = ps2.enter_context(
                            nc.psum_tensor(tag + "lg", [128, 8 * 16], F32))
                        lg_b = Buf(tag + "lg")
                        for tt in range(8):
                            for dc in range(NDC):
                                PE.emit(lambda e, tt=tt, dc=dc: e.matmul(
                                    lg_ps[:, tt * 16:(tt + 1) * 16],
                                    lhsT=X[:, dc * HALF + tt * 128:
                                           dc * HALF + (tt + 1) * 128],
                                    rhs=wr_sb[:, dc * 16:(dc + 1) * 16],
                                    start=(dc == 0), stop=(dc == NDC - 1)),
                                    [X_cells[dc], wr_sb_b], [lg_b],
                                    inc=(dc == NDC - 1 and tt == 7))

                        r3 = lambda t: t[:].rearrange("p (tt j) -> p tt j", tt=8)
                        lgr, lgr_b = rt["lgr"]
                        logit, logit_b = rt["logit"]
                        score, score_b = rt["score"]
                        sch, sch_b = rt["sch"]
                        emask, emask_b = rt["emask"]
                        mk0, mk0_b = rt["mk0"]
                        mkw, mkw_b = rt["mkw"]
                        eqm, eqm_b = rt["eqm"]
                        sel, sel_b = rt["sel"]
                        wraw, wraw_b = rt["wraw"]
                        V.emit(lambda e: e.tensor_tensor(
                            out=r3(lgr),
                            in0=lg_ps[:].rearrange("p (tt j) -> p tt j", tt=8),
                            in1=tokv[:, 0:8].unsqueeze(2).to_broadcast([128, 8, 16]),
                            op=ALU.mult), [lg_b, tokv_b], [lgr_b])
                        V.emit(lambda e: e.tensor_tensor(
                            out=r3(logit),
                            in0=cw_bc[:].unsqueeze(1).to_broadcast([128, 8, 16]),
                            in1=tokv[:, 8:16].unsqueeze(2).to_broadcast([128, 8, 16]),
                            op=ALU.mult), [cw_bc_b, tokv_b], [logit_b])
                        V.emit(lambda e: e.tensor_add(out=logit[:], in0=logit[:],
                                                      in1=lgr[:]),
                               [lgr_b, logit_b], [logit_b])
                        A.emit(lambda e: e.activation(out=score[:], in_=logit[:],
                                                      func=AF.Sigmoid),
                               [logit_b], [score_b])
                        V.emit(lambda e: e.tensor_tensor(
                            out=r3(sch), in0=r3(score),
                            in1=rb_bc[:].unsqueeze(1).to_broadcast([128, 8, 16]),
                            op=ALU.add), [score_b, rb_bc_b], [sch_b])
                        schr = r3(sch)
                        el = [schr[:, :, j:16:4] for j in range(4)]
                        prk = lambda k: pr[:, k * 32:(k + 1) * 32].rearrange(
                            "p (tt g) -> p tt g", tt=8)
                        pairs = [(0, 1), (0, 2), (0, 3), (1, 2), (1, 3), (2, 3)]
                        for k, (i, j) in enumerate(pairs):
                            V.emit(lambda e, k=k, i=i, j=j: e.tensor_add(
                                out=prk(k), in0=el[i], in1=el[j]), [sch_b], [pr_b])
                        gsr = gsum[:].rearrange("p (tt g) -> p tt g", tt=8)
                        V.emit(lambda e: e.tensor_max(out=gsr, in0=prk(0), in1=prk(1)),
                               [pr_b], [gsum_b])
                        for k in range(2, 6):
                            V.emit(lambda e, k=k: e.tensor_max(
                                out=gsr, in0=gsr, in1=prk(k)),
                                [pr_b, gsum_b], [gsum_b])
                        for k, (i, j) in enumerate(pairs):
                            V.emit(lambda e, k=k, i=i, j=j: e.tensor_tensor(
                                out=gmin[:, k * 8:(k + 1) * 8].unsqueeze(2),
                                in0=gsr[:, :, i:i + 1], in1=gsr[:, :, j:j + 1],
                                op=ALU.min), [gsum_b], [gmin_b])
                        V.emit(lambda e: e.tensor_max(
                            out=thr2[:], in0=gmin[:, 0:8], in1=gmin[:, 8:16]),
                            [gmin_b], [thr2_b])
                        for k in range(2, 6):
                            V.emit(lambda e, k=k: e.tensor_max(
                                out=thr2[:], in0=thr2[:],
                                in1=gmin[:, k * 8:(k + 1) * 8]),
                                [gmin_b, thr2_b], [thr2_b])
                        V.emit(lambda e: e.tensor_tensor(
                            out=gmask[:].rearrange("p (tt g) -> p tt g", tt=8),
                            in0=gsr,
                            in1=thr2[:].unsqueeze(2).to_broadcast([128, 8, 4]),
                            op=ALU.is_ge), [gsum_b, thr2_b], [gmask_b])
                        emr = r3(emask)
                        for j in range(4):
                            V.emit(lambda e, j=j: e.tensor_copy(
                                out=emr[:, :, j:16:4],
                                in_=gmask[:].rearrange("p (tt g) -> p tt g", tt=8)),
                                [gmask_b], [emask_b])
                        V.emit(lambda e: e.tensor_scalar(
                            out=eqm[:], in0=emask[:], scalar1=-1.0, scalar2=1e9,
                            op0=ALU.add, op1=ALU.mult), [emask_b], [eqm_b])
                        V.emit(lambda e: e.tensor_add(
                            out=mk0[:], in0=sch[:], in1=eqm[:]),
                            [sch_b, eqm_b], [mk0_b])
                        V.emit(lambda e: e.tensor_copy(out=mkw[:], in_=mk0[:]),
                               [mk0_b], [mkw_b])
                        for it in range(3):
                            V.emit(lambda e: e.tensor_reduce(
                                out=mx[:].unsqueeze(2), in_=r3(mkw), axis=AXX,
                                op=ALU.max), [mkw_b], [mx_b])
                            V.emit(lambda e: e.tensor_tensor(
                                out=r3(eqm), in0=r3(mkw),
                                in1=mx[:].unsqueeze(2).to_broadcast([128, 8, 16]),
                                op=ALU.is_ge), [mkw_b, mx_b], [eqm_b])
                            V.emit(lambda e: e.scalar_tensor_tensor(
                                out=mkw[:], in0=eqm[:], scalar=-2e9, in1=mkw[:],
                                op0=ALU.mult, op1=ALU.add), [eqm_b, mkw_b], [mkw_b])
                        V.emit(lambda e: e.tensor_reduce(
                            out=mx[:].unsqueeze(2), in_=r3(mkw), axis=AXX, op=ALU.max),
                            [mkw_b], [mx_b])
                        V.emit(lambda e: e.tensor_tensor(
                            out=r3(sel), in0=r3(mk0),
                            in1=mx[:].unsqueeze(2).to_broadcast([128, 8, 16]),
                            op=ALU.is_ge), [mk0_b, mx_b], [sel_b])
                        V.emit(lambda e: e.tensor_mul(out=wraw[:], in0=sel[:],
                                                      in1=score[:]),
                               [sel_b, score_b], [wraw_b])
                        V.emit(lambda e: e.tensor_reduce(
                            out=wsum[:].unsqueeze(2), in_=r3(wraw), axis=AXX,
                            op=ALU.add), [wraw_b], [wsum_b])
                        V.emit(lambda e: e.tensor_scalar_add(
                            out=wsum[:], in0=wsum[:], scalar1=1e-20),
                            [wsum_b], [wsum_b])
                        V.emit(lambda e: e.reciprocal(out=winv[:], in_=wsum[:]),
                               [wsum_b], [winv_b])
                        V.emit(lambda e: e.tensor_scalar_mul(
                            out=winv[:], in0=winv[:], scalar1=2.5),
                            [winv_b], [winv_b])
                        V.emit(lambda e: e.tensor_tensor(
                            out=r3(cbt), in0=r3(wraw),
                            in1=winv[:].unsqueeze(2).to_broadcast([128, 8, 16]),
                            op=ALU.mult), [wraw_b, winv_b], [cbt_b])
                        cbtr_ps = ps2.enter_context(
                            nc.psum_tensor(tag + "cbtr", [16, 8 * 128], BF16))
                        cbtr_b = Buf(tag + "cbtr")
                        for tt in range(8):
                            PE.emit(lambda e, tt=tt: e.transpose(
                                cbtr_ps[:, tt * 128:(tt + 1) * 128],
                                in_=cbt[:, tt * 16:(tt + 1) * 16],
                                identity=ident[:]),
                                [cbt_b, ident_b], [cbtr_b], inc=(tt == 7))
                        V.emit(lambda e: e.tensor_copy(out=cbst[:], in_=cbtr_ps[:]),
                               [cbtr_b], [cbst_b])
                        cb_done = Buf(tag + "cbd")
                        p.dact.emit(lambda e, half=half: e.dma_start(
                            out=cb_dram[:, half * HALF:(half + 1) * HALF],
                            in_=cbst[:]), [cbst_b], [cb_done, cb_B])
                    p.compute_fence()

                    with ExitStack() as ps4:
                        gate_ps = ps4.enter_context(
                            nc.psum_tensor(tag + "gate", [128, 4 * NFF], F32))
                        up_ps = ps4.enter_context(
                            nc.psum_tensor(tag + "up", [128, 4 * NFF], F32))
                        down_ps = ps4.enter_context(
                            nc.psum_tensor(tag + "down", [128, 8 * NFF], F32))
                        gate_b, up_b, down_b = (Buf(tag + "g"), Buf(tag + "u"),
                                                Buf(tag + "d"))

                        def load_cb(e_i):
                            cbs, cbs_b = cbslot[e_i % 3]
                            p.dpool.emit(lambda e, cbs=cbs, e_i=e_i, half=half: e.dma_start(
                                out=cbs[:],
                                in_=cb_dram[e_i, half * HALF:(half + 1) * HALF]
                                .unsqueeze(0).partition_broadcast(128)),
                                [cb_done], [cbs_b])

                        load_cb(0)
                        load_cb(1)
                        for c in range(NEXP // 2):
                            if c + 1 < NEXP // 2:
                                load_w(2 * c + 2)
                                load_cb(2 * c + 2)
                            for sub in range(NSUB):
                                t0 = sub * NFF
                                xsl = [xn[:, dc * HALF + t0: dc * HALF + t0 + NFF]
                                       for dc in range(NDC)]
                                ws = [(wslot[(2 * c + ei) % 3],
                                       cbslot[(2 * c + ei) % 3]) for ei in range(2)]
                                for ei in range(2):
                                    (wt, wt_b), (cbs, cbs_b) = ws[ei]
                                    gs, gs_b = gsil[ei]
                                    hh, hh_b = hbuf[ei]
                                    for off, psb, pb in ((0, gate_ps, gate_b),
                                                         (4096, up_ps, up_b)):
                                        for ic in range(4):
                                            for dc in range(NDC):
                                                PE.emit(
                                                    lambda e, ic=ic, dc=dc, wt=wt,
                                                    off=off, psb=psb, xsl=xsl: e.matmul(
                                                        psb[:, ic * NFF:(ic + 1) * NFF],
                                                        lhsT=wt[:, off + dc * 512 + ic * 128:
                                                                off + dc * 512 + (ic + 1) * 128],
                                                        rhs=xsl[dc],
                                                        start=(dc == 0 and ic % 2 == 0),
                                                        stop=(dc == 7 and ic % 2 == 1)),
                                                    [wt_b, xn_bs[dc]], [pb],
                                                    inc=(dc == 7 and ic == 3))
                                    A.emit(lambda e, gs=gs: e.activation(
                                        out=gs[:], in_=gate_ps[:], func=AF.Silu),
                                        [gate_b], [gs_b])
                                    V.emit(lambda e, gs=gs, hh=hh: e.tensor_mul(
                                        out=hh[:], in0=gs[:], in1=up_ps[:]),
                                        [gs_b, up_b], [hh_b])
                                    PL.emit(lambda e, cbs=cbs, hh=hh, t0=t0: e.tensor_tensor(
                                        out=hh[:].rearrange("p (i t) -> p i t", i=4),
                                        in0=hh[:].rearrange("p (i t) -> p i t", i=4),
                                        in1=cbs[:, t0:t0 + NFF].unsqueeze(1)
                                        .to_broadcast([128, 4, NFF]),
                                        op=ALU.mult), [hh_b, cbs_b], [hh_b])
                                for ei in range(2):
                                    (wt, wt_b), _cb = ws[ei]
                                    hh, hh_b = hbuf[ei]
                                    for dc in range(NDC):
                                        for ic in range(4):
                                            PE.emit(lambda e, ic=ic, dc=dc, ei=ei, wt=wt,
                                                    hh=hh: e.matmul(
                                                down_ps[:, dc * NFF:(dc + 1) * NFF],
                                                lhsT=wt[:, 8192 + ic * 1024 + dc * 128:
                                                        8192 + ic * 1024 + (dc + 1) * 128],
                                                rhs=hh[:, ic * NFF:(ic + 1) * NFF],
                                                start=(ei == 0 and dc % 2 == 0 and ic == 0),
                                                stop=(ei == 1 and dc % 2 == 1 and ic == 3)),
                                                [wt_b, hh_b], [down_b],
                                                inc=(dc == 7 and ic == 3))
                                Xr = X[:].rearrange("p (dc t) -> p dc t", dc=NDC)
                                Xsl = Xr[:, :, t0:t0 + NFF]
                                V.emit(lambda e, Xsl=Xsl: e.scalar_tensor_tensor(
                                    out=Xsl,
                                    in0=down_ps[:].rearrange("p (dc t) -> p dc t", dc=NDC),
                                    scalar=0.5, in1=Xsl, op0=ALU.mult, op1=ALU.add),
                                    [down_b], [X_b] + X_cells)
                                if c == NEXP // 2 - 1:
                                    p.dact.emit(lambda e, half=half, t0=t0: e.dma_start(
                                        out=dst_r[:, :, half * HALF + t0:
                                                  half * HALF + t0 + NFF],
                                        in_=X[:].rearrange(
                                            "p (dc t) -> p dc t",
                                            dc=NDC)[:, :, t0:t0 + NFF]),
                                        [X_b], [dst_B])
                            if c + 1 < NEXP // 2:
                                load_w(2 * c + 3)
                                load_cb(2 * c + 3)
                    p.compute_fence()

                for half in range(2):
                    do_half(half)
            p.barrier()

        if stop_after == "ffn1":
            ffn_pass("f1", x_d, None, out_d, out_B)
        else:
            ffn_pass("f1", x_d, None, Xd, Xd_B)

        if stop_after == "ffn1":
            p.barrier()
            block = ctx.enter_context(nc.Block())
            p.replay(block)
            return nc
        cc_v_done = [Buf("cc_v0"), Buf("cc_v1")]
        with ExitStack() as p2:
            sbl = lambda nm, sh, dtt: sb_in(p2, "p2" + nm, sh, dtt)
            bu_bf, bu_bf_b = sbl("bu_bf", [1, 2048], BF16)
            bv_bf, bv_bf_b = sbl("bv_bf", [1, 2048], BF16)
            p.dpool.emit(lambda e: e.dma_start(out=bu_bf[:], in_=bu_d), [], [bu_bf_b])
            p.dpool.emit(lambda e: e.dma_start(out=bv_bf[:], in_=bv_d), [], [bv_bf_b])
            xn2, xn2_b = sbl("xn2", [128, NDC * TPC], BF16)
            X, X_b = sbl("X", [128, NDC * HALF], F32)
            rows = {
                "m": sbl("m_row", [1, HALF], F32),
                "var": sbl("var_row", [1, HALF], F32),
                "rstd": sbl("rstd", [1, HALF], F32),
                "scr": sbl("scrrow", [1, HALF], F32),
                "scr16": [sbl(f"scr16_{i}", [128, HALF], BF16) for i in range(4)],
            }
            v2, v2_b = sbl("v2row", [1, HALF], F32)
            rfin, rfin_b = sbl("rfin", [1, HALF], F32)
            negmf, negmf_b = sbl("negmf", [1, HALF], F32)
            xt = [sbl(f"xt{i}", [128, HALF], F32) for i in range(2)]
            Xsrc_r = Xd.rearrange("(dc p) t -> p dc t", p=128)
            def p2_half(half):
                tag = f"p2h{half}_"
                p.dsync.emit(lambda e, half=half: e.dma_start(
                    out=X[:].rearrange("p (dc t) -> p dc t", dc=NDC),
                    in_=Xsrc_r[:, :, half * HALF:(half + 1) * HALF]),
                    [Xd_B], [X_b])
                (m_row, m_row_b) = rows["m"]
                (var_row, var_row_b) = rows["var"]
                (rstd, rstd_b) = rows["rstd"]
                (scr, scr_b) = rows["scr"]
                ln_stats_rows(tag, rows, X, X_b)
                V.emit(lambda e: e.tensor_scalar_add(out=scr[:], in0=rstd[:],
                                                     scalar1=1.0), [rstd_b], [scr_b])
                p.dact.emit(lambda e, half=half: e.dma_start(
                    out=r1f_d[0:1, half * HALF:(half + 1) * HALF], in_=scr[:]),
                    [scr_b], [rowsd_B])
                V.emit(lambda e: e.scalar_tensor_tensor(
                    out=scr[:], in0=m_row[:],
                    scalar=-1.0, in1=rstd[:], op0=ALU.mult, op1=ALU.mult),
                    [m_row_b, rstd_b], [scr_b])
                p.dpool.emit(lambda e, half=half: e.dma_start(
                    out=rows_d[1:2, half * HALF:(half + 1) * HALF], in_=scr[:]),
                    [scr_b], [rowsd_B])
                V.emit(lambda e: e.tensor_mul(out=v2[:], in0=var_row[:], in1=rstd[:]),
                       [var_row_b, rstd_b], [v2_b])
                V.emit(lambda e: e.tensor_mul(out=v2[:], in0=v2[:], in1=rstd[:]),
                       [v2_b, rstd_b], [v2_b])
                A.emit(lambda e: e.activation(out=scr[:], in_=v2[:], func=AF.Ln,
                                              bias=eps_t[0:1, 0:1]),
                       [v2_b, eps_b], [scr_b])
                A.emit(lambda e: e.activation(out=v2[:], in_=scr[:], func=AF.Exp,
                                              scale=-0.5), [scr_b], [v2_b])
                V.emit(lambda e: e.tensor_mul(out=rfin[:], in0=rstd[:], in1=v2[:]),
                       [rstd_b, v2_b], [rfin_b])
                V.emit(lambda e: e.scalar_tensor_tensor(
                    out=negmf[:], in0=m_row[:], scalar=-1.0, in1=rfin[:],
                    op0=ALU.mult, op1=ALU.mult), [m_row_b, rfin_b], [negmf_b])
                with ExitStack() as ps3:
                    bps = []
                    for i, (row, row_b) in enumerate([(rfin, rfin_b),
                                                      (negmf, negmf_b)]):
                        psb = ps3.enter_context(
                            nc.psum_tensor(f"{tag}bc{i}", [128, HALF], F32))
                        pb = Buf(f"{tag}bc{i}")
                        for ns in range(2):
                            PE.emit(lambda e, psb=psb, ns=ns, row=row: e.matmul(
                                psb[:, ns * 512:(ns + 1) * 512], lhsT=ones1f[:],
                                rhs=row[0:1, ns * 512:(ns + 1) * 512],
                                start=True, stop=True),
                                [row_b, ones1f_b], [pb], inc=(ns == 1))
                        bps.append((psb, pb))
                    (rf_bc, rf_bc_b), (nm_bc, nm_bc_b) = bps
                    for dc in range(NDC):
                        t, t_b = xt[dc % 2]
                        V.emit(lambda e, t=t, dc=dc: e.tensor_mul(
                            out=t[:], in0=X[:, dc * HALF:(dc + 1) * HALF],
                            in1=rf_bc[:]), [X_b, rf_bc_b], [t_b])
                        V.emit(lambda e, t=t: e.tensor_add(
                            out=t[:], in0=t[:], in1=nm_bc[:]),
                            [t_b, nm_bc_b], [t_b])
                        A.emit(lambda e, t=t, dc=dc, half=half: e.activation(
                            out=xn2[:, dc * TPC + half * HALF:
                                    dc * TPC + (half + 1) * HALF],
                            in_=t[:], func=AF.Identity,
                            bias=lnb[:, dc:dc + 1], scale=lng[:, dc:dc + 1]),
                            [t_b, lng_b, lnb_b], [xn2_b])
                p.compute_fence()

            wuvV, wuvV_b = sbl("wuvV", [128, 8 * 2048], BF16)
            wuvU, wuvU_b = sbl("wuvU", [128, 8 * 2048], BF16)
            p.dpool.emit(lambda e: e.dma_start(
                out=wuvV[:].rearrange("p (dc j) -> p dc j", dc=8),
                in_=winV_d.rearrange("dc p j -> p dc j")), [], [wuvV_b])
            p.dpool.emit(lambda e: e.dma_start(
                out=wuvU[:].rearrange("p (dc j) -> p dc j", dc=8),
                in_=winU_d.rearrange("dc p j -> p dc j")), [], [wuvU_b])
            stg = [sbl(f"uvst{i}", [128, 2048], BF16) for i in range(2)]
            uvps_scope = p2.enter_context(ExitStack())
            uv_ps0 = uvps_scope.enter_context(nc.psum_tensor("p2uv0", [128, 2048], F32))
            uv_ps0_b = Buf("p2uv0")
            uv_ps = [uv_ps0, uv_ps0]
            uv_ps_b = [uv_ps0_b, uv_ps0_b]

            def uv_tile(tt, wuv, wuv_b, brow, brow_b, dram_dst, dst_buf, row0=None):
                psb, pb = uv_ps[tt % 2], uv_ps_b[tt % 2]
                st, st_b = stg[tt % 2]
                for dc in range(NDC):
                    for ns in range(4):
                        PE.emit(lambda e, tt=tt, dc=dc, ns=ns, psb=psb,
                                wuv=wuv: e.matmul(
                            psb[:, ns * 512:(ns + 1) * 512],
                            lhsT=xn2[:, dc * TPC + tt * 128:
                                     dc * TPC + (tt + 1) * 128],
                            rhs=wuv[:, dc * 2048 + ns * 512:
                                    dc * 2048 + (ns + 1) * 512],
                            start=(dc == 0), stop=False),
                            [xn2_b, wuv_b], [pb], inc=False)
                for ns in range(4):
                    PE.emit(lambda e, ns=ns, psb=psb, brow=brow: e.matmul(
                        psb[:, ns * 512:(ns + 1) * 512], lhsT=ones1b[:],
                        rhs=brow[0:1, ns * 512:(ns + 1) * 512],
                        start=False, stop=True),
                        [ones1b_b, brow_b], [pb], inc=(ns == 3))
                A.emit(lambda e, st=st, psb=psb: e.activation(
                    out=st[:], in_=psb[:], func=AF.Silu), [pb], [st_b])
                r0 = tt * 128 if row0 is None else row0
                p.dact.emit(lambda e, st=st, r0=r0, dram_dst=dram_dst: e.dma_start(
                    out=dram_dst[r0:r0 + 128, :], in_=st[:]),
                    [st_b], [dst_buf])

            for half in range(2):
                p2_half(half)
                for tt in range(half * 8, half * 8 + 8):
                    dst = vown_d if half == 0 else vkown_d
                    uv_tile(tt, wuvV, wuvV_b, bv_bf, bv_bf_b, dst,
                            vown_B[half], row0=(tt - half * 8) * 128)
                if half == 0:
                    p.cc.emit(lambda e: e.collective_compute(
                        "AllGather", ALU.bypass,
                        replica_groups=[[0, 1, 2, 3], [4, 5, 6, 7]],
                        ins=[vown_d.opt()], outs=[vall[0].opt()]),
                        [vown_B[0]], [cc_v_done[0], vall_B[0]])

            wz, wz_b = sbl("wz", [128, 8 * 128], BF16)
            p.dpool.emit(lambda e: e.dma_start(
                out=wz[:].rearrange("p (dc j) -> p dc j", dc=8),
                in_=winZ_d.rearrange("dc p j -> p dc j")), [], [wz_b])
            z_s, z_s_b = sbl("zs", [128, 2 * 512], F32)
            kst, kst_b = sbl("kst", [128, TPC], BF16)
            with ExitStack() as zs_scope:
                z_ps = zs_scope.enter_context(
                    nc.psum_tensor("p2zps", [128, 2 * 512], F32))
                z_ps_b = Buf("p2zps")
                for g in range(4):
                    psl = z_ps[:, (g % 2) * 512:(g % 2 + 1) * 512]
                    zsl = z_s[:, (g % 2) * 512:(g % 2 + 1) * 512]
                    for dc in range(NDC):
                        PE.emit(lambda e, g=g, dc=dc, psl=psl: e.matmul(
                            psl, lhsT=wz[:, dc * 128:(dc + 1) * 128],
                            rhs=xn2[:, dc * TPC + g * 512: dc * TPC + (g + 1) * 512],
                            start=(dc == 0), stop=(dc == NDC - 1)),
                            [wz_b, xn2_b], [z_ps_b], inc=(dc == NDC - 1))
                    A.emit(lambda e, g=g, psl=psl, zsl=zsl: e.activation(
                        out=zsl, in_=psl, func=AF.Silu,
                        bias=bz[:]), [z_ps_b, bz_b], [z_s_b])
                    V.emit(lambda e, g=g, zsl=zsl: e.tensor_scalar(
                        out=zq_sb[:, g * 512:(g + 1) * 512], in0=zsl, scalar1=gqv[:],
                        scalar2=bqv[:], op0=ALU.mult, op1=ALU.add),
                        [z_s_b, gqv_b, bqv_b], [zq_sb_b])
                    V.emit(lambda e, g=g, zsl=zsl: e.tensor_scalar(
                        out=kst[:, g * 512:(g + 1) * 512], in0=zsl, scalar1=gkv[:],
                        scalar2=bkv[:], op0=ALU.mult, op1=ALU.add),
                        [z_s_b, gkv_b, bkv_b], [kst_b])
            p.compute_fence()
            p.dact.emit(lambda e: e.dma_start(
                out=vkown_d[HALF:HALF + 128, :],
                in_=kst[:]), [kst_b], [vown_B[1]])
            p.cc.emit(lambda e: e.collective_compute(
                "AllGather", ALU.bypass,
                replica_groups=[[0, 1, 2, 3], [4, 5, 6, 7]],
                ins=[vkown_d.opt()], outs=[vall[1].opt()]),
                [vown_B[1]], [cc_v_done[1], vall_B[1]])

            for tt in range(NQT):
                uv_tile(tt, wuvU, wuvU_b, bu_bf, bu_bf_b, u_dram,
                        ud_B)
        p.barrier()

        if stop_after == "p2":
            p.barrier()
            block = ctx.enter_context(nc.Block())
            p.replay(block)
            return nc
        with ExitStack() as p4:
            sbl = lambda nm, sh, dtt: sb_in(p4, "p4" + nm, sh, dtt)
            zk_sb, zk_sb_b = sbl("zk", [128, SEQ], BF16)
            kview = vall[1].rearrange("(r x) e -> r x e", x=HALF + 128)
            p.dsync.emit(lambda e: e.dma_start(
                out=zk_sb[:].rearrange("p (r t) -> p r t", r=4),
                in_=kview[:, HALF:HALF + 128, :].rearrange("r p e -> p r e")),
                [cc_v_done[1]], [zk_sb_b])
            wo_sb, wo_sb_b = sbl("wo", [128, 16384], BF16)
            p.dpool.emit(lambda e: e.dma_start(out=wo_sb[:], in_=wout_d),
                         [], [wo_sb_b])
            acc = [sbl(f"acc{i}", [128, 2048], F32) for i in range(4)]
            at_cache = [sbl(f"at{i}", [128, 8 * 512], BF16) for i in range(2)]
            vb = [sbl(f"vb{i}", [128, 8 * 2048], BF16) for i in range(2)]
            usl0 = sbl("usl0", [128, 2048], BF16)
            usl = [usl0, usl0]
            x1t, x1t_b = sbl("x1t", [128, 2048], F32)
            rsb, rsb_b = sbl("rs", [128, 512], F32)
            ptok, ptok_b = sbl("ptok", [128, 2048], BF16)
            pfm, pfm_b = sbl("pfm", [128, 4096], BF16)
            r1_row, r1_row_b = sbl("r1row", [1, TPC], F32)
            n1_row, n1_row_b = sbl("n1row", [1, TPC], BF16)
            p.dsync.emit(lambda e: e.dma_start(out=r1_row[:], in_=r1f_d[0:1, :]),
                         [rowsd_B], [r1_row_b])
            p.dsync.emit(lambda e: e.dma_start(out=n1_row[:], in_=rows_d[1:2, :]),
                         [rowsd_B], [n1_row_b])
            sc_ps = p4.enter_context(nc.psum_tensor("p4sc", [128, 512], F32))
            sc_b = Buf("p4sc")
            av_ps = [p4.enter_context(nc.psum_tensor(f"p4av{i}", [128, 512], F32))
                     for i in range(4)]
            av_b = [Buf(f"p4av{i}") for i in range(4)]
            tr_ps = p4.enter_context(nc.psum_tensor("p4tr", [128, 1024], BF16))
            tr_b = Buf("p4tr")
            wout_ps = p4.enter_context(nc.psum_tensor("p4wout", [128, 512], F32))
            wout_b = Buf("p4wout")
            lnbc_ps = p4.enter_context(nc.psum_tensor("p4lnbc", [128, 512], F32))
            lnbc_b = Buf("p4lnbc")
            Xd_r = Xd.rearrange("(dc p) t -> p dc t", p=128)

            def emit_A(qg, kb):
                atc, atc_b = at_cache[kb % 2]
                for k8 in range(8):
                    PE.emit(lambda e, qg=qg, kb=kb, k8=k8: e.matmul(
                        sc_ps[:],
                        lhsT=zk_sb[:, (kb * 8 + k8) * 128:(kb * 8 + k8 + 1) * 128],
                        rhs=zq_sb[:, qg * 512:(qg + 1) * 512],
                        start=True, stop=True),
                        [zk_sb_b, zq_sb_b], [sc_b], inc=True)
                    A.emit(lambda e: e.activation(out=rsb[:], in_=sc_ps[:],
                                                  func=AF.Relu, scale=INV_L),
                           [sc_b], [rsb_b])
                    PL.emit(lambda e, k8=k8, atc=atc: e.tensor_mul(
                        out=atc[:, k8 * 512:(k8 + 1) * 512], in0=rsb[:], in1=rsb[:]),
                        [rsb_b], [atc_b])

            def load_vblock(kb):
                buf, buf_b = vb[kb % 2]
                r = kb // 2
                src = vall[kb % 2]
                nkt = 8 if kb % 2 == 0 else 9
                srcr = src.rearrange("(r kt p) ee -> p r kt ee", p=128, kt=nkt)
                p.dsync.emit(lambda e, buf=buf, r=r, srcr=srcr: e.dma_start(
                    out=buf[:].rearrange("p (kt ee) -> p kt ee", kt=8),
                    in_=srcr[:, r, 0:8]), [cc_v_done[kb % 2]], [buf_b])

            def emit_B(qg, kb, first):
                atc, atc_b = at_cache[kb % 2]
                buf, buf_b = vb[kb % 2]
                for qt4 in range(4):
                    a4, a4_b = acc[qt4]
                    for ns in range(4):
                        for k8 in range(8):
                            PE.emit(lambda e, qt4=qt4, ns=ns, k8=k8, atc=atc,
                                    buf=buf: e.matmul(
                                av_ps[ns][:],
                                lhsT=atc[:, k8 * 512 + qt4 * 128:
                                         k8 * 512 + (qt4 + 1) * 128],
                                rhs=buf[:, k8 * 2048 + ns * 512:
                                        k8 * 2048 + (ns + 1) * 512],
                                start=(k8 == 0), stop=(k8 == 7)),
                                [atc_b, buf_b], [av_b[ns]], inc=(k8 == 7))
                        if first:
                            V.emit(lambda e, a4=a4, ns=ns: e.tensor_copy(
                                out=a4[:, ns * 512:(ns + 1) * 512],
                                in_=av_ps[ns][:]), [av_b[ns]], [a4_b])
                        else:
                            V.emit(lambda e, a4=a4, ns=ns: e.tensor_add(
                                out=a4[:, ns * 512:(ns + 1) * 512],
                                in0=a4[:, ns * 512:(ns + 1) * 512],
                                in1=av_ps[ns][:]), [av_b[ns], a4_b], [a4_b])

            load_vblock(0)
            load_vblock(1)
            emit_A(0, 0)
            for qg in range(4):
                for kb in range(8):
                    if kb + 1 < 8:
                        emit_A(qg, kb + 1)
                    emit_B(qg, kb, first=(kb == 0))
                    if kb + 2 < 8:
                        load_vblock(kb + 2)
                if qg + 1 < 4:
                    load_vblock(0)
                    load_vblock(1)
                    emit_A(qg + 1, 0)
                for qp in range(2):
                    for sub2 in range(2):
                        qt4 = qp * 2 + sub2
                        qt = qg * 4 + qt4
                        a4, a4_b = acc[qt4]
                        u_t, u_tb = usl[qt % 2]
                        p.dsync.emit(lambda e, u_t=u_t, qt=qt: e.dma_start(
                            out=u_t[:], in_=u_dram[qt * 128:(qt + 1) * 128, :]),
                            [ud_B], [u_tb])
                        V.emit(lambda e, u_t=u_t, a4=a4: e.tensor_mul(
                            out=ptok[:], in0=a4[:], in1=u_t[:]),
                            [a4_b, u_tb], [ptok_b])
                        for half8 in range(2):
                            for j in range(8):
                                ec = half8 * 8 + j
                                PE.emit(lambda e, ec=ec, j=j: e.transpose(
                                    tr_ps[:, j * 128:(j + 1) * 128],
                                    in_=ptok[:, ec * 128:(ec + 1) * 128],
                                    identity=ident[:]),
                                    [ptok_b, ident_b], [tr_b], inc=(j == 7))
                            V.emit(lambda e, half8=half8, sub2=sub2: e.tensor_copy(
                                out=pfm[:].rearrange("p (ec q) -> p ec q", ec=16)
                                [:, half8 * 8:(half8 + 1) * 8,
                                 sub2 * 128:(sub2 + 1) * 128],
                                in_=tr_ps[:].rearrange("p (j q) -> p j q", j=8)),
                                [tr_b], [pfm_b])
                    qt0 = qg * 4 + qp * 2
                    p.dsync.emit(lambda e, qt0=qt0: e.dma_start(
                        out=x1t[:].rearrange("p (dc t) -> p dc t", dc=NDC),
                        in_=Xd_r[:, :, qt0 * 128:(qt0 + 2) * 128]),
                        [Xd_B], [x1t_b])
                    PE.emit(lambda e, qt0=qt0: e.matmul(
                        lnbc_ps[:, 0:256], lhsT=ones1f[:],
                        rhs=r1_row[0:1, qt0 * 128:(qt0 + 2) * 128],
                        start=True, stop=True),
                        [r1_row_b, ones1f_b], [lnbc_b], inc=False)
                    PE.emit(lambda e, qt0=qt0: e.matmul(
                        lnbc_ps[:, 256:512], lhsT=ones1b[:],
                        rhs=n1_row[0:1, qt0 * 128:(qt0 + 2) * 128],
                        start=True, stop=True),
                        [n1_row_b, ones1b_b], [lnbc_b], inc=True)
                    V.emit(lambda e: e.tensor_tensor(
                        out=x1t[:].rearrange("p (dc t) -> p dc t", dc=NDC),
                        in0=x1t[:].rearrange("p (dc t) -> p dc t", dc=NDC),
                        in1=lnbc_ps[:, 0:256].unsqueeze(1)
                        .to_broadcast([128, 8, 256]),
                        op=ALU.mult), [lnbc_b], [x1t_b])
                    V.emit(lambda e: e.tensor_tensor(
                        out=x1t[:].rearrange("p (dc t) -> p dc t", dc=NDC),
                        in0=x1t[:].rearrange("p (dc t) -> p dc t", dc=NDC),
                        in1=lnbc_ps[:, 256:512].unsqueeze(1)
                        .to_broadcast([128, 8, 256]),
                        op=ALU.add), [lnbc_b], [x1t_b])
                    V.emit(lambda e: e.tensor_tensor(
                        out=x1t[:].rearrange("p (dc t) -> p dc t", dc=NDC),
                        in0=x1t[:].rearrange("p (dc t) -> p dc t", dc=NDC),
                        in1=bout[:].unsqueeze(2).to_broadcast([128, 8, 256]),
                        op=ALU.add), [bout_b], [x1t_b])
                    for dcb in range(4):
                        for dc2 in range(2):
                            dc = dcb * 2 + dc2
                            for ec in range(16):
                                PE.emit(lambda e, dc=dc, dc2=dc2, ec=ec: e.matmul(
                                    wout_ps[:, dc2 * 256:(dc2 + 1) * 256],
                                    lhsT=wo_sb[:, (ec * 8 + dc) * 128:
                                               (ec * 8 + dc + 1) * 128],
                                    rhs=pfm[:, ec * 256:(ec + 1) * 256],
                                    start=(ec == 0), stop=(ec == 15)),
                                    [wo_sb_b, pfm_b], [wout_b],
                                    inc=(dc2 == 1 and ec == 15))
                        V.emit(lambda e, dcb=dcb: e.tensor_add(
                            out=x1t[:, dcb * 512:(dcb + 1) * 512],
                            in0=x1t[:, dcb * 512:(dcb + 1) * 512],
                            in1=wout_ps[:]), [wout_b, x1t_b], [x1t_b])
                    p.dact.emit(lambda e, qt0=qt0: e.dma_start(
                        out=Xd_r[:, :, qt0 * 128:(qt0 + 2) * 128],
                        in_=x1t[:].rearrange("p (dc t) -> p dc t", dc=NDC)),
                        [x1t_b], [Xd_B])
        p.barrier()

        if stop_after == "attn":
            ob = Buf("odbg")
            p.dsync.emit(lambda e: e.dma_start(out=out_d, in_=Xd), [Xd_B], [ob])
            p.barrier()
            block = ctx.enter_context(nc.Block())
            p.replay(block)
            return nc
        ffn_pass("f2", Xd, Xd_B, out_d, out_B)

        p.barrier()
        block = ctx.enter_context(nc.Block())
        p.replay(block)
    return nc


STOP_AFTER = None


def kernel(**inputs):
    x = np.asarray(inputs["L0_attn_x"])
    W_in = np.asarray(inputs["L0_attn_W_in"])
    b_in = np.asarray(inputs["L0_attn_b_in"])
    W_out = np.asarray(inputs["L0_attn_W_out"])
    b_out = np.asarray(inputs["L0_attn_b_out"])
    ln_g = np.asarray(inputs["L0_attn_ln_g"])
    ln_b = np.asarray(inputs["L0_attn_ln_b"])
    gq = np.asarray(inputs["L0_attn_gamma_q"])
    bq = np.asarray(inputs["L0_attn_beta_q"])
    gk = np.asarray(inputs["L0_attn_gamma_k"])
    bk = np.asarray(inputs["L0_attn_beta_k"])
    Wr = np.asarray(inputs["L0_ffn_router_weight"])
    rbias = np.asarray(inputs["L0_ffn_router_bias"])
    w_gate = np.asarray(inputs["L0_ffn_w_gate"])
    w_up = np.asarray(inputs["L0_ffn_w_up"])
    w_down = np.asarray(inputs["L0_ffn_w_down"])

    expw = np.empty((NEXP, 128, 12288), np.float32)
    for e in range(NEXP):
        WgT = w_gate[e].T
        WuT = w_up[e].T
        WdT = w_down[e].T
        expw[e, :, 0:4096] = WgT.reshape(8, 128, 512).transpose(1, 0, 2).reshape(128, 4096)
        expw[e, :, 4096:8192] = WuT.reshape(8, 128, 512).transpose(1, 0, 2).reshape(128, 4096)
        expw[e, :, 8192:12288] = WdT.reshape(4, 128, 1024).transpose(1, 0, 2).reshape(128, 4096)
    shared = dict(
        expw=expw,
        winU=np.ascontiguousarray(W_in[:, 0:2048].reshape(8, 128, 2048)),
        winV=np.ascontiguousarray(W_in[:, 2048:4096].reshape(8, 128, 2048)),
        winZ=np.ascontiguousarray(W_in[:, 4096:4224].reshape(8, 128, 128)),
        wout=np.ascontiguousarray(
            W_out.reshape(16, 128, 8, 128).transpose(1, 0, 2, 3).reshape(128, 16384)),
        wr=np.ascontiguousarray(Wr.T.reshape(8, 128, 16)),
        bu=np.ascontiguousarray(b_in[None, 0:2048].astype(np.float32)),
        bv=np.ascontiguousarray(b_in[None, 2048:4096].astype(np.float32)),
        bz=np.ascontiguousarray(b_in[4096:4224].reshape(128, 1)),
        lng=np.ascontiguousarray(ln_g.reshape(8, 128).T),
        lnb=np.ascontiguousarray(ln_b.reshape(8, 128).T),
        bout=np.ascontiguousarray(b_out.reshape(8, 128).T),
        gq=np.ascontiguousarray(gq.reshape(128, 1)),
        bq=np.ascontiguousarray(bq.reshape(128, 1)),
        gk=np.ascontiguousarray(gk.reshape(128, 1)),
        bk=np.ascontiguousarray(bk.reshape(128, 1)),
        rb=np.ascontiguousarray(rbias[None, :].astype(np.float32)),
        ident=np.eye(128, dtype=ml_dtypes.bfloat16),
        onesc=np.ones((128, 1), dtype=ml_dtypes.bfloat16),
        ones1b=np.ones((1, 128), dtype=ml_dtypes.bfloat16),
        ones1f=np.ones((1, 128), dtype=np.float32),
        ones11=np.ones((1, 1), dtype=np.float32),
    )
    in_maps = []
    for c in range(8):
        b = c // 4
        t0 = (c % 4) * TPC
        m = dict(shared)
        m["x"] = np.ascontiguousarray(x[b, t0:t0 + TPC, :].T)
        in_maps.append(m)

    nc = build_nc(stop_after=STOP_AFTER)
    res = run_bass_kernel_spmd(nc, in_maps, core_ids=list(range(8)))
    out = np.empty((2, 8192, 1024), np.float32)
    for c in range(8):
        b = c // 4
        t0 = (c % 4) * TPC
        out[b, t0:t0 + TPC, :] = res.results[c]["out"].T
    return out


# version 09
# speedup vs baseline: 1.0282x; 1.0178x over previous
import numpy as np
import ml_dtypes
from contextlib import ExitStack
import concourse.bass as bass
import concourse.mybir as mybir
from concourse.bass_utils import run_bass_kernel_spmd

F32 = mybir.dt.float32
BF16 = mybir.dt.bfloat16
AF = mybir.ActivationFunctionType
ALU = mybir.AluOpType
AXX = mybir.AxisListType.X



class Buf:
    __slots__ = ("name", "w", "rs")

    def __init__(self, name=""):
        self.name = name
        self.w = []
        self.rs = []


class Stream:
    def __init__(self, name):
        self.name = name
        self.records = []
        self.max_waited = {}


class Q:
    def __init__(self, stream, sem, step=1, name=""):
        self.stream = stream
        self.sem = sem
        self.step = step
        self.n = 0
        self.name = name
        self.pend_w = []
        self.pend_r = []

    def emit(self, build, reads=(), writes=(), inc=True):
        need = {}
        if self.step == 16 and self.n > 0:
            need[self] = self.n
        for b in reads:
            for (q, v) in b.w:
                if need.get(q, -1) < v:
                    need[q] = v
        for b in writes:
            for (q, v) in b.w:
                if need.get(q, -1) < v:
                    need[q] = v
            for (q, v) in b.rs:
                if need.get(q, -1) < v:
                    need[q] = v
        waits = []
        st = self.stream
        for q, v in need.items():
            assert v <= q.n, f"wait on unproduced token {q.name}:{v}"
            if st.max_waited.get(q, -1) < v:
                waits.append((q.sem, v))
                st.max_waited[q] = v
        st.records.append((waits, build, (self.sem, self.step) if inc else None))
        for b in writes:
            self.pend_w.append(b)
        for b in reads:
            self.pend_r.append(b)
        if inc:
            self.n += self.step
            tok = (self, self.n)
            for b in self.pend_w:
                b.w.append(tok)
                b.rs = []
            for b in self.pend_r:
                b.rs.append(tok)
            self.pend_w = []
            self.pend_r = []

    def seal(self):
        if self.pend_w or self.pend_r:
            raise RuntimeError(f"{self.name}: pending accesses without an inc")

    def token(self):
        self.seal()
        return (self, self.n)


class DmaQ:
    def __init__(self, stream, sems, name):
        self.subqs = [Q(stream, s, 16, f"{name}{i}") for i, s in enumerate(sems)]
        self.i = 0
        self.name = name

    def emit(self, build, reads=(), writes=()):
        q = self.subqs[self.i]
        self.i = (self.i + 1) % len(self.subqs)
        return q.emit(build, reads, writes, inc=True)

    def qs(self):
        return list(self.subqs)


class Prog:
    def __init__(self, nc, ctx, n_dsync=12, n_dact=10, n_dpool=6):
        self.nc = nc
        self.streams = {n: Stream(n) for n in
                        ("tensor", "vector", "scalar", "gpsimd", "sync")}

        def sem(name):
            return ctx.enter_context(nc.semaphore(name))
        self.pe = Q(self.streams["tensor"], sem("q_pe"), 1, "pe")
        self.dve = Q(self.streams["vector"], sem("q_dve"), 1, "dve")
        self.act = Q(self.streams["scalar"], sem("q_act"), 1, "act")
        self.pool = Q(self.streams["gpsimd"], sem("q_pool"), 1, "pool")
        self.dsync = DmaQ(self.streams["sync"],
                          [sem(f"q_dsync{i}") for i in range(n_dsync)], "dsync")
        self.dact = DmaQ(self.streams["scalar"],
                         [sem(f"q_dact{i}") for i in range(n_dact)], "dact")
        self.dpool = DmaQ(self.streams["gpsimd"],
                          [sem(f"q_dpool{i}") for i in range(n_dpool)], "dpool")
        self.cc = Q(self.streams["gpsimd"], sem("q_cc"), 1, "cc")
        self.all_qs = ([self.pe, self.dve, self.act, self.pool, self.cc]
                       + self.dsync.qs() + self.dact.qs() + self.dpool.qs())

    def barrier(self, qs=None):
        qs = qs or self.all_qs
        toks = [q.token() for q in qs]
        for q in qs:
            st = q.stream
            waits = []
            for (q2, v) in toks:
                if v > 0 and st.max_waited.get(q2, -1) < v:
                    waits.append((q2.sem, v))
                    st.max_waited[q2] = v
            if waits:
                st.records.append((waits, None, None))

    def compute_fence(self):
        self.barrier([self.pe, self.dve, self.act])

    def replay(self, block):
        eng_of = {"tensor": block.tensor, "vector": block.vector,
                  "scalar": block.scalar, "gpsimd": block.gpsimd,
                  "sync": block.sync}

        def mk(stream):
            def body(eng):
                for waits, build, inc in stream.records:
                    if build is None:
                        for (sem, val) in waits:
                            eng.wait_ge(sem, val)
                        continue
                    for (sem, val) in waits[:-1]:
                        eng.wait_ge(sem, val)
                    inst = build(eng)
                    if waits:
                        inst._wait_ge(waits[-1][0], waits[-1][1])
                    if inc is not None:
                        inst.then_inc(inc[0], inc[1])
            return body
        for name, stream in self.streams.items():
            if stream.records:
                eng_of[name](mk(stream))


TPC = 2048
HALF = 1024
NSUB = 4
NFF = 256
NDC = 8
NEXP = 16
SEQ = 8192
NKT = 64
NQT = 16
EU = 2048
LN_EPS = 1e-5
INV_L = 1.0 / 8192.0


def build_nc(stop_after=None):
    nc = bass.Bass("TRN2", target_bir_lowering=False)
    dt = nc.dram_tensor
    x_d = dt("x", [1024, TPC], F32, kind="ExternalInput").ap()
    expw_d = dt("expw", [NEXP, 128, 12288], F32, kind="ExternalInput").ap()
    winU_d = dt("winU", [8, 128, 2048], F32, kind="ExternalInput").ap()
    winV_d = dt("winV", [8, 128, 2048], F32, kind="ExternalInput").ap()
    winZ_d = dt("winZ", [8, 128, 128], F32, kind="ExternalInput").ap()
    wout_d = dt("wout", [128, 16384], F32, kind="ExternalInput").ap()
    wr_d = dt("wr", [8, 128, 16], F32, kind="ExternalInput").ap()
    bu_d = dt("bu", [1, 2048], F32, kind="ExternalInput").ap()
    bv_d = dt("bv", [1, 2048], F32, kind="ExternalInput").ap()
    bz_d = dt("bz", [128, 1], F32, kind="ExternalInput").ap()
    lng_d = dt("lng", [128, 8], F32, kind="ExternalInput").ap()
    lnb_d = dt("lnb", [128, 8], F32, kind="ExternalInput").ap()
    bout_d = dt("bout", [128, 8], F32, kind="ExternalInput").ap()
    gq_d = dt("gq", [128, 1], F32, kind="ExternalInput").ap()
    bq_d = dt("bq", [128, 1], F32, kind="ExternalInput").ap()
    gk_d = dt("gk", [128, 1], F32, kind="ExternalInput").ap()
    bk_d = dt("bk", [128, 1], F32, kind="ExternalInput").ap()
    rb_d = dt("rb", [1, 16], F32, kind="ExternalInput").ap()
    ident_d = dt("ident", [128, 128], BF16, kind="ExternalInput").ap()
    onesc_d = dt("onesc", [128, 1], BF16, kind="ExternalInput").ap()
    ones1b_d = dt("ones1b", [1, 128], BF16, kind="ExternalInput").ap()
    ones1f_d = dt("ones1f", [1, 128], F32, kind="ExternalInput").ap()
    ones11_d = dt("ones11", [1, 1], F32, kind="ExternalInput").ap()
    out_d = dt("out", [1024, TPC], F32, kind="ExternalOutput").ap()
    Xd = dt("Xd", [1024, TPC], F32).ap()
    u_dram = dt("u_dram", [TPC, EU], BF16).ap()
    vown_d = dt("vown", [HALF, EU], BF16).ap()
    vkown_d = dt("vkown", [HALF + 128, EU], BF16).ap()
    vall = [dt("vall0", [4 * HALF, EU], BF16).ap(),
            dt("vall1", [4 * (HALF + 128), EU], BF16).ap()]
    cb_dram = dt("cb_dram", [NEXP, TPC], BF16).ap()
    rows_d = dt("rows_d", [2, TPC], BF16).ap()
    r1f_d = dt("r1f_d", [1, TPC], F32).ap()
    rowsb_d = dt("rowsb_d", [2, HALF], F32).ap()

    Xd_B = Buf("Xd")
    out_B = Buf("outd")
    ud_B = Buf("u_dram")
    vown_B = [Buf("vown0"), Buf("vown1")]
    vall_B = [Buf("vall0"), Buf("vall1")]
    cb_B = Buf("cb_dram")
    rowsd_B = Buf("rows_d")
    rowsb_B = Buf("rowsb_d")

    ctx = ExitStack()
    with ctx:
        p = Prog(nc, ctx)
        PE, V, A, PL = p.pe, p.dve, p.act, p.pool

        def sb_in(scope, name, shape, dtype):
            t = scope.enter_context(nc.sbuf_tensor(name, shape, dtype))
            return t, Buf(name)

        sb = lambda nm, sh, dtt: sb_in(ctx, nm, sh, dtt)

        ident, ident_b = sb("c_ident", [128, 128], BF16)
        onesc, onesc_b = sb("c_onesc", [128, 1], BF16)
        ones1b, ones1b_b = sb("c_ones1b", [1, 128], BF16)
        ones1f, ones1f_b = sb("c_ones1f", [1, 128], F32)
        ones11, ones11_b = sb("c_ones11", [1, 1], F32)
        bz, bz_b = sb("c_bz", [128, 1], F32)
        lng, lng_b = sb("c_lng", [128, 8], F32)
        lnb, lnb_b = sb("c_lnb", [128, 8], F32)
        bout, bout_b = sb("c_bout", [128, 8], F32)
        gqv, gqv_b = sb("gqv", [128, 1], F32)
        bqv, bqv_b = sb("bqv", [128, 1], F32)
        gkv, gkv_b = sb("gkv", [128, 1], F32)
        bkv, bkv_b = sb("bkv", [128, 1], F32)
        rb_bc, rb_bc_b = sb("c_rb_bc", [128, 16], F32)
        cw_bc, cw_bc_b = sb("c_cw_bc", [128, 16], F32)
        zq_sb, zq_sb_b = sb("c_zq_sb", [128, TPC], BF16)
        wr_sb, wr_sb_b = sb("c_wr_sb", [128, 8 * 16], F32)
        eps_t, eps_b = sb("c_eps", [1, 1], F32)
        V.emit(lambda e: e.memset(eps_t[:], LN_EPS), [], [eps_b])


        for dst, dst_b, src in [
                (ident, ident_b, ident_d), (onesc, onesc_b, onesc_d),
                (ones1b, ones1b_b, ones1b_d), (ones1f, ones1f_b, ones1f_d),
                (ones11, ones11_b, ones11_d), (bz, bz_b, bz_d),
                (lng, lng_b, lng_d), (lnb, lnb_b, lnb_d), (bout, bout_b, bout_d),
                (gqv, gqv_b, gq_d), (bqv, bqv_b, bq_d), (gkv, gkv_b, gk_d),
                (bkv, bkv_b, bk_d)]:
            p.dsync.emit(lambda e, dst=dst, src=src: e.dma_start(out=dst[:], in_=src),
                         [], [dst_b])
        p.dsync.emit(lambda e: e.dma_start(
            out=wr_sb[:].rearrange("p (dc j) -> p dc j", dc=8),
            in_=wr_d.rearrange("dc p j -> p dc j")), [], [wr_sb_b])
        p.dsync.emit(lambda e: e.dma_start(out=rb_bc[:], in_=rb_d.partition_broadcast(128)),
                     [], [rb_bc_b])

        with ExitStack() as cws:
            cw_ps = cws.enter_context(nc.psum_tensor("cw_ps", [1, 16], F32))
            cw_ps_b = Buf("cw_ps")
            onesf_t, onesf_b = sb_in(cws, "onesf128", [128, 1], F32)
            V.emit(lambda e: e.memset(onesf_t[:], 1.0), [], [onesf_b])
            for dc in range(8):
                PE.emit(lambda e, dc=dc: e.matmul(
                    cw_ps[:], lhsT=onesf_t[:], rhs=wr_sb[:, dc * 16:(dc + 1) * 16],
                    start=(dc == 0), stop=(dc == 7)),
                    [onesf_b, wr_sb_b], [cw_ps_b], inc=(dc == 7))
            cw_row, cw_row_b = sb_in(cws, "cw_row", [1, 16], F32)
            V.emit(lambda e: e.tensor_copy(out=cw_row[:], in_=cw_ps[:]),
                   [cw_ps_b], [cw_row_b])
            cwb_ps = cws.enter_context(nc.psum_tensor("cwb_ps", [128, 16], F32))
            cwb_b = Buf("cwb_ps")
            PE.emit(lambda e: e.matmul(cwb_ps[:], lhsT=ones1f[:], rhs=cw_row[:],
                                       start=True, stop=True),
                    [ones1f_b, cw_row_b], [cwb_b])
            V.emit(lambda e: e.tensor_copy(out=cw_bc[:], in_=cwb_ps[:]),
                   [cwb_b], [cw_bc_b])
        p.barrier()

        def ln_stats_rows(tag, rows, X, X_b, X_cells=None):
            (m_row, m_row_b) = rows["m"]
            (var_row, var_row_b) = rows["var"]
            (rstd, rstd_b) = rows["rstd"]
            (scr, scr_b) = rows["scr"]
            scr16 = rows["scr16"]
            with ExitStack() as ps_scope:
                stats_ps = ps_scope.enter_context(
                    nc.psum_tensor(tag + "stats", [1, 2 * HALF], F32))
                stats_b = Buf(tag + "stats")
                for dc in range(NDC):
                    tb, tb_b = scr16[dc % 2]
                    ts, ts_b = scr16[2 + dc % 2]
                    xsl = X[:, dc * HALF:(dc + 1) * HALF]
                    xdep = [X_cells[dc]] if X_cells else [X_b]
                    V.emit(lambda e, tb=tb, xsl=xsl: e.tensor_copy(out=tb[:], in_=xsl),
                           xdep, [tb_b])
                    A.emit(lambda e, ts=ts, xsl=xsl: e.activation(
                        out=ts[:], in_=xsl, func=AF.Square), xdep, [ts_b])
                    for ns in range(2):
                        PE.emit(lambda e, tb=tb, ns=ns, dc=dc: e.matmul(
                            stats_ps[0:1, ns * 512:(ns + 1) * 512],
                            lhsT=onesc[:], rhs=tb[:, ns * 512:(ns + 1) * 512],
                            start=(dc == 0), stop=(dc == NDC - 1)),
                            [tb_b, onesc_b], [stats_b], inc=False)
                        PE.emit(lambda e, ts=ts, ns=ns, dc=dc: e.matmul(
                            stats_ps[0:1, HALF + ns * 512:HALF + (ns + 1) * 512],
                            lhsT=onesc[:], rhs=ts[:, ns * 512:(ns + 1) * 512],
                            start=(dc == 0), stop=(dc == NDC - 1)),
                            [ts_b, onesc_b], [stats_b],
                            inc=(ns == 1))
                A.emit(lambda e: e.mul(m_row[:], stats_ps[0:1, 0:HALF], 1.0 / 1024.0),
                       [stats_b], [m_row_b])
                A.emit(lambda e: e.activation(out=scr[:], in_=m_row[:], func=AF.Square),
                       [m_row_b], [scr_b])
                V.emit(lambda e: e.scalar_tensor_tensor(
                    out=var_row[:], in0=stats_ps[0:1, HALF:2 * HALF],
                    scalar=1.0 / 1024.0, in1=scr[:],
                    op0=ALU.mult, op1=ALU.subtract), [stats_b, scr_b], [var_row_b])
                A.emit(lambda e: e.activation(out=scr[:], in_=var_row[:], func=AF.Ln,
                                              bias=eps_t[0:1, 0:1]),
                       [var_row_b, eps_b], [scr_b])
                A.emit(lambda e: e.activation(out=rstd[:], in_=scr[:], func=AF.Exp,
                                              scale=-0.5), [scr_b], [rstd_b])
            p.compute_fence()

        def ffn_pass(pass_tag, src, src_B, dst, dst_B):
            src_r = src.rearrange("(dc p) t -> p dc t", p=128)
            dst_r = dst.rearrange("(dc p) t -> p dc t", p=128)
            with ExitStack() as fs:
                sbl = lambda nm, sh, dtt: sb_in(fs, pass_tag + nm, sh, dtt)
                X, X_b = sbl("X", [128, NDC * HALF], F32)
                X_cells = [Buf(f"{pass_tag}Xc{dc}") for dc in range(NDC)]
                xn, _xn_b = sbl("xn", [128, NDC * HALF], BF16)
                xn_bs = [Buf(f"{pass_tag}xn{dc}") for dc in range(NDC)]
                wslot = [sbl(f"w{i}", [128, 12288], BF16) for i in range(3)]
                cbslot = [sbl(f"cb{i}", [128, HALF], BF16) for i in range(3)]
                gsil = [sbl(f"gs{i}", [128, 4 * NFF], BF16) for i in range(2)]
                hbuf = [sbl(f"h{i}", [128, 4 * NFF], BF16) for i in range(2)]
                rows = {
                    "m": sbl("m_row", [1, HALF], F32),
                    "var": sbl("var_row", [1, HALF], F32),
                    "rstd": sbl("rstd", [1, HALF], F32),
                    "scr": sbl("scrrow", [1, HALF], F32),
                    "scr16": [sbl(f"scr16_{i}", [128, HALF], BF16) for i in range(4)],
                }
                negm, negm_b = sbl("negm", [1, HALF], F32)
                tokv, tokv_b = sbl("tokv", [128, 16], F32)
                xt = [sbl(f"xt{i}", [128, HALF], F32) for i in range(2)]
                rbc_sb, rbc_b = sbl("rbc", [128, HALF], F32)
                nbc_sb, nbc_b = sbl("nbc", [128, HALF], F32)
                rt = {nm: sbl(nm, [128, 128], F32) for nm in
                      ("lgr", "logit", "score", "sch", "emask", "mk0", "mkw",
                       "eqm", "sel", "wraw")}
                pr, pr_b = sbl("pr", [128, 32 * 6], F32)
                gsum, gsum_b = sbl("gsum", [128, 32], F32)
                gmin, gmin_b = sbl("gmin", [128, 8 * 6], F32)
                thr2, thr2_b = sbl("thr2", [128, 8], F32)
                gmask, gmask_b = sbl("gmask", [128, 32], F32)
                mx, mx_b = sbl("mx", [128, 8], F32)
                wsum, wsum_b = sbl("wsum", [128, 8], F32)
                winv, winv_b = sbl("winv", [128, 8], F32)
                cbt, cbt_b = sbl("cbt", [128, 128], BF16)
                cbst, cbst_b = sbl("cbst", [16, 1024], BF16)

                def load_w(e_i):
                    wt, wt_b = wslot[e_i % 3]
                    p.dpool.emit(lambda e, wt=wt, e_i=e_i: e.dma_start(
                        out=wt[:], in_=expw_d[e_i]), [], [wt_b])

                def do_half(half):
                    tag = f"{pass_tag}h{half}_"
                    for dc in range(NDC):
                        p.dsync.emit(lambda e, half=half, dc=dc: e.dma_start(
                            out=X[:, dc * HALF:(dc + 1) * HALF],
                            in_=src_r[:, dc, half * HALF:(half + 1) * HALF]),
                            [src_B] if src_B else [], [X_cells[dc], X_b])
                    load_w(0)
                    load_w(1)
                    (m_row, m_row_b) = rows["m"]
                    (rstd, rstd_b) = rows["rstd"]
                    ln_stats_rows(tag, rows, X, X_b, X_cells)
                    V.emit(lambda e: e.scalar_tensor_tensor(
                        out=negm[:], in0=m_row[:], scalar=-1.0, in1=rstd[:],
                        op0=ALU.mult, op1=ALU.mult), [m_row_b, rstd_b], [negm_b])

                    p.dact.emit(lambda e: e.dma_start(out=rowsb_d[0:1, :], in_=rstd[:]),
                                [rstd_b], [rowsb_B])
                    p.dact.emit(lambda e: e.dma_start(out=rowsb_d[1:2, :], in_=negm[:]),
                                [negm_b], [rowsb_B])
                    p.dsync.emit(lambda e: e.dma_start(
                        out=rbc_sb[:], in_=rowsb_d[0:1, :].partition_broadcast(128)),
                        [rowsb_B], [rbc_b])
                    p.dsync.emit(lambda e: e.dma_start(
                        out=nbc_sb[:], in_=rowsb_d[1:2, :].partition_broadcast(128)),
                        [rowsb_B], [nbc_b])
                    for dc in range(NDC):
                        t, t_b = xt[dc % 2]
                        V.emit(lambda e, t=t, dc=dc: e.tensor_mul(
                            out=t[:], in0=X[:, dc * HALF:(dc + 1) * HALF],
                            in1=rbc_sb[:]), [X_cells[dc], rbc_b], [t_b])
                        V.emit(lambda e, t=t, dc=dc: e.tensor_add(
                            out=xn[:, dc * HALF:(dc + 1) * HALF], in0=t[:],
                            in1=nbc_sb[:]), [t_b, nbc_b], [xn_bs[dc]])

                    with ExitStack() as ps2:
                        tok_ps = ps2.enter_context(
                            nc.psum_tensor(tag + "tokps", [128, 16], F32))
                        tok_ps_b = Buf(tag + "tokps")
                        for tt in range(8):
                            PE.emit(lambda e, tt=tt: e.matmul(
                                tok_ps[:, tt:tt + 1],
                                lhsT=rstd[0:1, tt * 128:(tt + 1) * 128],
                                rhs=ones11[:], start=True, stop=True),
                                [rstd_b, ones11_b], [tok_ps_b], inc=False)
                            PE.emit(lambda e, tt=tt: e.matmul(
                                tok_ps[:, 8 + tt:9 + tt],
                                lhsT=negm[0:1, tt * 128:(tt + 1) * 128],
                                rhs=ones11[:], start=True, stop=True),
                                [negm_b, ones11_b], [tok_ps_b], inc=(tt == 7))
                        V.emit(lambda e: e.tensor_copy(out=tokv[:], in_=tok_ps[:]),
                               [tok_ps_b], [tokv_b])

                        lg_ps = ps2.enter_context(
                            nc.psum_tensor(tag + "lg", [128, 8 * 16], F32))
                        lg_b = Buf(tag + "lg")
                        for tt in range(8):
                            for dc in range(NDC):
                                PE.emit(lambda e, tt=tt, dc=dc: e.matmul(
                                    lg_ps[:, tt * 16:(tt + 1) * 16],
                                    lhsT=X[:, dc * HALF + tt * 128:
                                           dc * HALF + (tt + 1) * 128],
                                    rhs=wr_sb[:, dc * 16:(dc + 1) * 16],
                                    start=(dc == 0), stop=(dc == NDC - 1)),
                                    [X_cells[dc], wr_sb_b], [lg_b],
                                    inc=(dc == NDC - 1 and tt == 7))

                        r3 = lambda t: t[:].rearrange("p (tt j) -> p tt j", tt=8)
                        lgr, lgr_b = rt["lgr"]
                        logit, logit_b = rt["logit"]
                        score, score_b = rt["score"]
                        sch, sch_b = rt["sch"]
                        emask, emask_b = rt["emask"]
                        mk0, mk0_b = rt["mk0"]
                        mkw, mkw_b = rt["mkw"]
                        eqm, eqm_b = rt["eqm"]
                        sel, sel_b = rt["sel"]
                        wraw, wraw_b = rt["wraw"]
                        V.emit(lambda e: e.tensor_tensor(
                            out=r3(lgr),
                            in0=lg_ps[:].rearrange("p (tt j) -> p tt j", tt=8),
                            in1=tokv[:, 0:8].unsqueeze(2).to_broadcast([128, 8, 16]),
                            op=ALU.mult), [lg_b, tokv_b], [lgr_b])
                        V.emit(lambda e: e.tensor_tensor(
                            out=r3(logit),
                            in0=cw_bc[:].unsqueeze(1).to_broadcast([128, 8, 16]),
                            in1=tokv[:, 8:16].unsqueeze(2).to_broadcast([128, 8, 16]),
                            op=ALU.mult), [cw_bc_b, tokv_b], [logit_b])
                        V.emit(lambda e: e.tensor_add(out=logit[:], in0=logit[:],
                                                      in1=lgr[:]),
                               [lgr_b, logit_b], [logit_b])
                        A.emit(lambda e: e.activation(out=score[:], in_=logit[:],
                                                      func=AF.Sigmoid),
                               [logit_b], [score_b])
                        V.emit(lambda e: e.tensor_tensor(
                            out=r3(sch), in0=r3(score),
                            in1=rb_bc[:].unsqueeze(1).to_broadcast([128, 8, 16]),
                            op=ALU.add), [score_b, rb_bc_b], [sch_b])
                        schr = r3(sch)
                        el = [schr[:, :, j:16:4] for j in range(4)]
                        prk = lambda k: pr[:, k * 32:(k + 1) * 32].rearrange(
                            "p (tt g) -> p tt g", tt=8)
                        pairs = [(0, 1), (0, 2), (0, 3), (1, 2), (1, 3), (2, 3)]
                        for k, (i, j) in enumerate(pairs):
                            V.emit(lambda e, k=k, i=i, j=j: e.tensor_add(
                                out=prk(k), in0=el[i], in1=el[j]), [sch_b], [pr_b])
                        gsr = gsum[:].rearrange("p (tt g) -> p tt g", tt=8)
                        V.emit(lambda e: e.tensor_max(out=gsr, in0=prk(0), in1=prk(1)),
                               [pr_b], [gsum_b])
                        for k in range(2, 6):
                            V.emit(lambda e, k=k: e.tensor_max(
                                out=gsr, in0=gsr, in1=prk(k)),
                                [pr_b, gsum_b], [gsum_b])
                        for k, (i, j) in enumerate(pairs):
                            V.emit(lambda e, k=k, i=i, j=j: e.tensor_tensor(
                                out=gmin[:, k * 8:(k + 1) * 8].unsqueeze(2),
                                in0=gsr[:, :, i:i + 1], in1=gsr[:, :, j:j + 1],
                                op=ALU.min), [gsum_b], [gmin_b])
                        V.emit(lambda e: e.tensor_max(
                            out=thr2[:], in0=gmin[:, 0:8], in1=gmin[:, 8:16]),
                            [gmin_b], [thr2_b])
                        for k in range(2, 6):
                            V.emit(lambda e, k=k: e.tensor_max(
                                out=thr2[:], in0=thr2[:],
                                in1=gmin[:, k * 8:(k + 1) * 8]),
                                [gmin_b, thr2_b], [thr2_b])
                        V.emit(lambda e: e.tensor_tensor(
                            out=gmask[:].rearrange("p (tt g) -> p tt g", tt=8),
                            in0=gsr,
                            in1=thr2[:].unsqueeze(2).to_broadcast([128, 8, 4]),
                            op=ALU.is_ge), [gsum_b, thr2_b], [gmask_b])
                        emr = r3(emask)
                        for j in range(4):
                            V.emit(lambda e, j=j: e.tensor_copy(
                                out=emr[:, :, j:16:4],
                                in_=gmask[:].rearrange("p (tt g) -> p tt g", tt=8)),
                                [gmask_b], [emask_b])
                        V.emit(lambda e: e.tensor_scalar(
                            out=eqm[:], in0=emask[:], scalar1=-1.0, scalar2=1e9,
                            op0=ALU.add, op1=ALU.mult), [emask_b], [eqm_b])
                        V.emit(lambda e: e.tensor_add(
                            out=mk0[:], in0=sch[:], in1=eqm[:]),
                            [sch_b, eqm_b], [mk0_b])
                        V.emit(lambda e: e.tensor_copy(out=mkw[:], in_=mk0[:]),
                               [mk0_b], [mkw_b])
                        for it in range(3):
                            V.emit(lambda e: e.tensor_reduce(
                                out=mx[:].unsqueeze(2), in_=r3(mkw), axis=AXX,
                                op=ALU.max), [mkw_b], [mx_b])
                            V.emit(lambda e: e.tensor_tensor(
                                out=r3(eqm), in0=r3(mkw),
                                in1=mx[:].unsqueeze(2).to_broadcast([128, 8, 16]),
                                op=ALU.is_ge), [mkw_b, mx_b], [eqm_b])
                            V.emit(lambda e: e.scalar_tensor_tensor(
                                out=mkw[:], in0=eqm[:], scalar=-2e9, in1=mkw[:],
                                op0=ALU.mult, op1=ALU.add), [eqm_b, mkw_b], [mkw_b])
                        V.emit(lambda e: e.tensor_reduce(
                            out=mx[:].unsqueeze(2), in_=r3(mkw), axis=AXX, op=ALU.max),
                            [mkw_b], [mx_b])
                        V.emit(lambda e: e.tensor_tensor(
                            out=r3(sel), in0=r3(mk0),
                            in1=mx[:].unsqueeze(2).to_broadcast([128, 8, 16]),
                            op=ALU.is_ge), [mk0_b, mx_b], [sel_b])
                        V.emit(lambda e: e.tensor_mul(out=wraw[:], in0=sel[:],
                                                      in1=score[:]),
                               [sel_b, score_b], [wraw_b])
                        V.emit(lambda e: e.tensor_reduce(
                            out=wsum[:].unsqueeze(2), in_=r3(wraw), axis=AXX,
                            op=ALU.add), [wraw_b], [wsum_b])
                        V.emit(lambda e: e.tensor_scalar_add(
                            out=wsum[:], in0=wsum[:], scalar1=1e-20),
                            [wsum_b], [wsum_b])
                        V.emit(lambda e: e.reciprocal(out=winv[:], in_=wsum[:]),
                               [wsum_b], [winv_b])
                        V.emit(lambda e: e.tensor_scalar_mul(
                            out=winv[:], in0=winv[:], scalar1=2.5),
                            [winv_b], [winv_b])
                        V.emit(lambda e: e.tensor_tensor(
                            out=r3(cbt), in0=r3(wraw),
                            in1=winv[:].unsqueeze(2).to_broadcast([128, 8, 16]),
                            op=ALU.mult), [wraw_b, winv_b], [cbt_b])
                        cbtr_ps = ps2.enter_context(
                            nc.psum_tensor(tag + "cbtr", [16, 8 * 128], BF16))
                        cbtr_b = Buf(tag + "cbtr")
                        for tt in range(8):
                            PE.emit(lambda e, tt=tt: e.transpose(
                                cbtr_ps[:, tt * 128:(tt + 1) * 128],
                                in_=cbt[:, tt * 16:(tt + 1) * 16],
                                identity=ident[:]),
                                [cbt_b, ident_b], [cbtr_b], inc=(tt == 7))
                        V.emit(lambda e: e.tensor_copy(out=cbst[:], in_=cbtr_ps[:]),
                               [cbtr_b], [cbst_b])
                        cb_done = Buf(tag + "cbd")
                        p.dact.emit(lambda e, half=half: e.dma_start(
                            out=cb_dram[:, half * HALF:(half + 1) * HALF],
                            in_=cbst[:]), [cbst_b], [cb_done, cb_B])
                    p.compute_fence()

                    with ExitStack() as ps4:
                        gate_ps = ps4.enter_context(
                            nc.psum_tensor(tag + "gate", [128, 4 * NFF], F32))
                        up_ps = ps4.enter_context(
                            nc.psum_tensor(tag + "up", [128, 4 * NFF], F32))
                        down_ps = ps4.enter_context(
                            nc.psum_tensor(tag + "down", [128, 8 * NFF], F32))
                        gate_b, up_b, down_b = (Buf(tag + "g"), Buf(tag + "u"),
                                                Buf(tag + "d"))

                        def load_cb(e_i):
                            cbs, cbs_b = cbslot[e_i % 3]
                            p.dpool.emit(lambda e, cbs=cbs, e_i=e_i, half=half: e.dma_start(
                                out=cbs[:],
                                in_=cb_dram[e_i, half * HALF:(half + 1) * HALF]
                                .unsqueeze(0).partition_broadcast(128)),
                                [cb_done], [cbs_b])

                        load_cb(0)
                        load_cb(1)
                        for c in range(NEXP // 2):
                            if c + 1 < NEXP // 2:
                                load_w(2 * c + 2)
                                load_cb(2 * c + 2)
                            for sub in range(NSUB):
                                t0 = sub * NFF
                                xsl = [xn[:, dc * HALF + t0: dc * HALF + t0 + NFF]
                                       for dc in range(NDC)]
                                ws = [(wslot[(2 * c + ei) % 3],
                                       cbslot[(2 * c + ei) % 3]) for ei in range(2)]
                                for ei in range(2):
                                    (wt, wt_b), (cbs, cbs_b) = ws[ei]
                                    gs, gs_b = gsil[ei]
                                    hh, hh_b = hbuf[ei]
                                    for off, psb, pb in ((0, gate_ps, gate_b),
                                                         (4096, up_ps, up_b)):
                                        for ic in range(4):
                                            for dc in range(NDC):
                                                PE.emit(
                                                    lambda e, ic=ic, dc=dc, wt=wt,
                                                    off=off, psb=psb, xsl=xsl: e.matmul(
                                                        psb[:, ic * NFF:(ic + 1) * NFF],
                                                        lhsT=wt[:, off + dc * 512 + ic * 128:
                                                                off + dc * 512 + (ic + 1) * 128],
                                                        rhs=xsl[dc],
                                                        start=(dc == 0 and ic % 2 == 0),
                                                        stop=(dc == 7 and ic % 2 == 1)),
                                                    [wt_b, xn_bs[dc]], [pb],
                                                    inc=(dc == 7 and ic == 3))
                                    A.emit(lambda e, gs=gs: e.activation(
                                        out=gs[:], in_=gate_ps[:], func=AF.Silu),
                                        [gate_b], [gs_b])
                                    V.emit(lambda e, gs=gs, hh=hh: e.tensor_mul(
                                        out=hh[:], in0=gs[:], in1=up_ps[:]),
                                        [gs_b, up_b], [hh_b])
                                    PL.emit(lambda e, cbs=cbs, hh=hh, t0=t0: e.tensor_tensor(
                                        out=hh[:].rearrange("p (i t) -> p i t", i=4),
                                        in0=hh[:].rearrange("p (i t) -> p i t", i=4),
                                        in1=cbs[:, t0:t0 + NFF].unsqueeze(1)
                                        .to_broadcast([128, 4, NFF]),
                                        op=ALU.mult), [hh_b, cbs_b], [hh_b])
                                for ei in range(2):
                                    (wt, wt_b), _cb = ws[ei]
                                    hh, hh_b = hbuf[ei]
                                    for dc in range(NDC):
                                        for ic in range(4):
                                            PE.emit(lambda e, ic=ic, dc=dc, ei=ei, wt=wt,
                                                    hh=hh: e.matmul(
                                                down_ps[:, dc * NFF:(dc + 1) * NFF],
                                                lhsT=wt[:, 8192 + ic * 1024 + dc * 128:
                                                        8192 + ic * 1024 + (dc + 1) * 128],
                                                rhs=hh[:, ic * NFF:(ic + 1) * NFF],
                                                start=(ei == 0 and dc % 2 == 0 and ic == 0),
                                                stop=(ei == 1 and dc % 2 == 1 and ic == 3)),
                                                [wt_b, hh_b], [down_b],
                                                inc=(dc == 7 and ic == 3))
                                Xr = X[:].rearrange("p (dc t) -> p dc t", dc=NDC)
                                Xsl = Xr[:, :, t0:t0 + NFF]
                                V.emit(lambda e, Xsl=Xsl: e.scalar_tensor_tensor(
                                    out=Xsl,
                                    in0=down_ps[:].rearrange("p (dc t) -> p dc t", dc=NDC),
                                    scalar=0.5, in1=Xsl, op0=ALU.mult, op1=ALU.add),
                                    [down_b], [X_b] + X_cells)
                                if c == NEXP // 2 - 1:
                                    p.dact.emit(lambda e, half=half, t0=t0: e.dma_start(
                                        out=dst_r[:, :, half * HALF + t0:
                                                  half * HALF + t0 + NFF],
                                        in_=X[:].rearrange(
                                            "p (dc t) -> p dc t",
                                            dc=NDC)[:, :, t0:t0 + NFF]),
                                        [X_b], [dst_B])
                            if c + 1 < NEXP // 2:
                                load_w(2 * c + 3)
                                load_cb(2 * c + 3)
                    p.compute_fence()

                for half in range(2):
                    do_half(half)
            p.barrier()

        if stop_after == "ffn1":
            ffn_pass("f1", x_d, None, out_d, out_B)
        else:
            ffn_pass("f1", x_d, None, Xd, Xd_B)

        if stop_after == "ffn1":
            p.barrier()
            block = ctx.enter_context(nc.Block())
            p.replay(block)
            return nc
        cc_v_done = [Buf("cc_v0"), Buf("cc_v1")]
        with ExitStack() as p2:
            sbl = lambda nm, sh, dtt: sb_in(p2, "p2" + nm, sh, dtt)
            bu_bf, bu_bf_b = sbl("bu_bf", [1, 2048], BF16)
            bv_bf, bv_bf_b = sbl("bv_bf", [1, 2048], BF16)
            p.dpool.emit(lambda e: e.dma_start(out=bu_bf[:], in_=bu_d), [], [bu_bf_b])
            p.dpool.emit(lambda e: e.dma_start(out=bv_bf[:], in_=bv_d), [], [bv_bf_b])
            xn2, xn2_b = sbl("xn2", [128, NDC * TPC], BF16)
            X, X_b = sbl("X", [128, NDC * HALF], F32)
            rows = {
                "m": sbl("m_row", [1, HALF], F32),
                "var": sbl("var_row", [1, HALF], F32),
                "rstd": sbl("rstd", [1, HALF], F32),
                "scr": sbl("scrrow", [1, HALF], F32),
                "scr16": [sbl(f"scr16_{i}", [128, HALF], BF16) for i in range(4)],
            }
            v2, v2_b = sbl("v2row", [1, HALF], F32)
            rfin, rfin_b = sbl("rfin", [1, HALF], F32)
            negmf, negmf_b = sbl("negmf", [1, HALF], F32)
            xt = [sbl(f"xt{i}", [128, HALF], F32) for i in range(2)]
            Xsrc_r = Xd.rearrange("(dc p) t -> p dc t", p=128)
            def p2_half(half):
                tag = f"p2h{half}_"
                p.dsync.emit(lambda e, half=half: e.dma_start(
                    out=X[:].rearrange("p (dc t) -> p dc t", dc=NDC),
                    in_=Xsrc_r[:, :, half * HALF:(half + 1) * HALF]),
                    [Xd_B], [X_b])
                (m_row, m_row_b) = rows["m"]
                (var_row, var_row_b) = rows["var"]
                (rstd, rstd_b) = rows["rstd"]
                (scr, scr_b) = rows["scr"]
                ln_stats_rows(tag, rows, X, X_b)
                V.emit(lambda e: e.tensor_scalar_add(out=scr[:], in0=rstd[:],
                                                     scalar1=1.0), [rstd_b], [scr_b])
                p.dact.emit(lambda e, half=half: e.dma_start(
                    out=r1f_d[0:1, half * HALF:(half + 1) * HALF], in_=scr[:]),
                    [scr_b], [rowsd_B])
                V.emit(lambda e: e.scalar_tensor_tensor(
                    out=scr[:], in0=m_row[:],
                    scalar=-1.0, in1=rstd[:], op0=ALU.mult, op1=ALU.mult),
                    [m_row_b, rstd_b], [scr_b])
                p.dpool.emit(lambda e, half=half: e.dma_start(
                    out=rows_d[1:2, half * HALF:(half + 1) * HALF], in_=scr[:]),
                    [scr_b], [rowsd_B])
                V.emit(lambda e: e.tensor_mul(out=v2[:], in0=var_row[:], in1=rstd[:]),
                       [var_row_b, rstd_b], [v2_b])
                V.emit(lambda e: e.tensor_mul(out=v2[:], in0=v2[:], in1=rstd[:]),
                       [v2_b, rstd_b], [v2_b])
                A.emit(lambda e: e.activation(out=scr[:], in_=v2[:], func=AF.Ln,
                                              bias=eps_t[0:1, 0:1]),
                       [v2_b, eps_b], [scr_b])
                A.emit(lambda e: e.activation(out=v2[:], in_=scr[:], func=AF.Exp,
                                              scale=-0.5), [scr_b], [v2_b])
                V.emit(lambda e: e.tensor_mul(out=rfin[:], in0=rstd[:], in1=v2[:]),
                       [rstd_b, v2_b], [rfin_b])
                V.emit(lambda e: e.scalar_tensor_tensor(
                    out=negmf[:], in0=m_row[:], scalar=-1.0, in1=rfin[:],
                    op0=ALU.mult, op1=ALU.mult), [m_row_b, rfin_b], [negmf_b])
                with ExitStack() as ps3:
                    bps = []
                    for i, (row, row_b) in enumerate([(rfin, rfin_b),
                                                      (negmf, negmf_b)]):
                        psb = ps3.enter_context(
                            nc.psum_tensor(f"{tag}bc{i}", [128, HALF], F32))
                        pb = Buf(f"{tag}bc{i}")
                        for ns in range(2):
                            PE.emit(lambda e, psb=psb, ns=ns, row=row: e.matmul(
                                psb[:, ns * 512:(ns + 1) * 512], lhsT=ones1f[:],
                                rhs=row[0:1, ns * 512:(ns + 1) * 512],
                                start=True, stop=True),
                                [row_b, ones1f_b], [pb], inc=(ns == 1))
                        bps.append((psb, pb))
                    (rf_bc, rf_bc_b), (nm_bc, nm_bc_b) = bps
                    for dc in range(NDC):
                        t, t_b = xt[dc % 2]
                        V.emit(lambda e, t=t, dc=dc: e.tensor_mul(
                            out=t[:], in0=X[:, dc * HALF:(dc + 1) * HALF],
                            in1=rf_bc[:]), [X_b, rf_bc_b], [t_b])
                        V.emit(lambda e, t=t: e.tensor_add(
                            out=t[:], in0=t[:], in1=nm_bc[:]),
                            [t_b, nm_bc_b], [t_b])
                        A.emit(lambda e, t=t, dc=dc, half=half: e.activation(
                            out=xn2[:, dc * TPC + half * HALF:
                                    dc * TPC + (half + 1) * HALF],
                            in_=t[:], func=AF.Identity,
                            bias=lnb[:, dc:dc + 1], scale=lng[:, dc:dc + 1]),
                            [t_b, lng_b, lnb_b], [xn2_b])
                p.compute_fence()

            wuvV, wuvV_b = sbl("wuvV", [128, 8 * 2048], BF16)
            wuvU, wuvU_b = sbl("wuvU", [128, 8 * 2048], BF16)
            p.dpool.emit(lambda e: e.dma_start(
                out=wuvV[:].rearrange("p (dc j) -> p dc j", dc=8),
                in_=winV_d.rearrange("dc p j -> p dc j")), [], [wuvV_b])
            p.dpool.emit(lambda e: e.dma_start(
                out=wuvU[:].rearrange("p (dc j) -> p dc j", dc=8),
                in_=winU_d.rearrange("dc p j -> p dc j")), [], [wuvU_b])
            stg = [sbl(f"uvst{i}", [128, 2048], BF16) for i in range(2)]
            uvps_scope = p2.enter_context(ExitStack())
            uv_ps0 = uvps_scope.enter_context(nc.psum_tensor("p2uv0", [128, 2048], F32))
            uv_ps0_b = Buf("p2uv0")
            uv_ps = [uv_ps0, uv_ps0]
            uv_ps_b = [uv_ps0_b, uv_ps0_b]

            def uv_tile(tt, wuv, wuv_b, brow, brow_b, dram_dst, dst_buf, row0=None):
                psb, pb = uv_ps[tt % 2], uv_ps_b[tt % 2]
                st, st_b = stg[tt % 2]
                for dc in range(NDC):
                    for ns in range(4):
                        PE.emit(lambda e, tt=tt, dc=dc, ns=ns, psb=psb,
                                wuv=wuv: e.matmul(
                            psb[:, ns * 512:(ns + 1) * 512],
                            lhsT=xn2[:, dc * TPC + tt * 128:
                                     dc * TPC + (tt + 1) * 128],
                            rhs=wuv[:, dc * 2048 + ns * 512:
                                    dc * 2048 + (ns + 1) * 512],
                            start=(dc == 0), stop=False),
                            [xn2_b, wuv_b], [pb], inc=False)
                for ns in range(4):
                    PE.emit(lambda e, ns=ns, psb=psb, brow=brow: e.matmul(
                        psb[:, ns * 512:(ns + 1) * 512], lhsT=ones1b[:],
                        rhs=brow[0:1, ns * 512:(ns + 1) * 512],
                        start=False, stop=True),
                        [ones1b_b, brow_b], [pb], inc=(ns == 3))
                A.emit(lambda e, st=st, psb=psb: e.activation(
                    out=st[:], in_=psb[:], func=AF.Silu), [pb], [st_b])
                r0 = tt * 128 if row0 is None else row0
                p.dact.emit(lambda e, st=st, r0=r0, dram_dst=dram_dst: e.dma_start(
                    out=dram_dst[r0:r0 + 128, :], in_=st[:]),
                    [st_b], [dst_buf])

            for half in range(2):
                p2_half(half)
                for tt in range(half * 8, half * 8 + 8):
                    dst = vown_d if half == 0 else vkown_d
                    uv_tile(tt, wuvV, wuvV_b, bv_bf, bv_bf_b, dst,
                            vown_B[half], row0=(tt - half * 8) * 128)
                if half == 0:
                    p.cc.emit(lambda e: e.collective_compute(
                        "AllGather", ALU.bypass,
                        replica_groups=[[0, 1, 2, 3], [4, 5, 6, 7]],
                        ins=[vown_d.opt()], outs=[vall[0].opt()]),
                        [vown_B[0]], [cc_v_done[0], vall_B[0]])

            wz, wz_b = sbl("wz", [128, 8 * 128], BF16)
            p.dpool.emit(lambda e: e.dma_start(
                out=wz[:].rearrange("p (dc j) -> p dc j", dc=8),
                in_=winZ_d.rearrange("dc p j -> p dc j")), [], [wz_b])
            z_s, z_s_b = sbl("zs", [128, 2 * 512], F32)
            kst, kst_b = sbl("kst", [128, TPC], BF16)
            with ExitStack() as zs_scope:
                z_ps = zs_scope.enter_context(
                    nc.psum_tensor("p2zps", [128, 2 * 512], F32))
                z_ps_b = Buf("p2zps")
                for g in range(4):
                    psl = z_ps[:, (g % 2) * 512:(g % 2 + 1) * 512]
                    zsl = z_s[:, (g % 2) * 512:(g % 2 + 1) * 512]
                    for dc in range(NDC):
                        PE.emit(lambda e, g=g, dc=dc, psl=psl: e.matmul(
                            psl, lhsT=wz[:, dc * 128:(dc + 1) * 128],
                            rhs=xn2[:, dc * TPC + g * 512: dc * TPC + (g + 1) * 512],
                            start=(dc == 0), stop=(dc == NDC - 1)),
                            [wz_b, xn2_b], [z_ps_b], inc=(dc == NDC - 1))
                    A.emit(lambda e, g=g, psl=psl, zsl=zsl: e.activation(
                        out=zsl, in_=psl, func=AF.Silu,
                        bias=bz[:]), [z_ps_b, bz_b], [z_s_b])
                    V.emit(lambda e, g=g, zsl=zsl: e.tensor_scalar(
                        out=zq_sb[:, g * 512:(g + 1) * 512], in0=zsl, scalar1=gqv[:],
                        scalar2=bqv[:], op0=ALU.mult, op1=ALU.add),
                        [z_s_b, gqv_b, bqv_b], [zq_sb_b])
                    V.emit(lambda e, g=g, zsl=zsl: e.tensor_scalar(
                        out=kst[:, g * 512:(g + 1) * 512], in0=zsl, scalar1=gkv[:],
                        scalar2=bkv[:], op0=ALU.mult, op1=ALU.add),
                        [z_s_b, gkv_b, bkv_b], [kst_b])
            p.compute_fence()
            p.dact.emit(lambda e: e.dma_start(
                out=vkown_d[HALF:HALF + 128, :],
                in_=kst[:]), [kst_b], [vown_B[1]])
            p.cc.emit(lambda e: e.collective_compute(
                "AllGather", ALU.bypass,
                replica_groups=[[0, 1, 2, 3], [4, 5, 6, 7]],
                ins=[vkown_d.opt()], outs=[vall[1].opt()]),
                [vown_B[1]], [cc_v_done[1], vall_B[1]])

            for tt in range(NQT):
                uv_tile(tt, wuvU, wuvU_b, bu_bf, bu_bf_b, u_dram,
                        ud_B)
        p.barrier()

        if stop_after == "p2":
            p.barrier()
            block = ctx.enter_context(nc.Block())
            p.replay(block)
            return nc
        with ExitStack() as p4:
            sbl = lambda nm, sh, dtt: sb_in(p4, "p4" + nm, sh, dtt)
            zk_sb, zk_sb_b = sbl("zk", [128, SEQ], BF16)
            kview = vall[1].rearrange("(r x) e -> r x e", x=HALF + 128)
            p.dsync.emit(lambda e: e.dma_start(
                out=zk_sb[:].rearrange("p (r t) -> p r t", r=4),
                in_=kview[:, HALF:HALF + 128, :].rearrange("r p e -> p r e")),
                [cc_v_done[1]], [zk_sb_b])
            wo_sb, wo_sb_b = sbl("wo", [128, 16384], BF16)
            p.dpool.emit(lambda e: e.dma_start(out=wo_sb[:], in_=wout_d),
                         [], [wo_sb_b])
            acc = [sbl(f"acc{i}", [128, 2048], F32) for i in range(4)]
            at_cache = [sbl(f"at{i}", [128, 8 * 512], BF16) for i in range(2)]
            vb = [sbl(f"vb{i}", [128, 8 * 2048], BF16) for i in range(2)]
            usl0 = sbl("usl0", [128, 2048], BF16)
            usl = [usl0, usl0]
            x1t, x1t_b = sbl("x1t", [128, 2048], F32)
            rsb, rsb_b = sbl("rs", [128, 512], F32)
            ptok, ptok_b = sbl("ptok", [128, 2048], BF16)
            pfm, pfm_b = sbl("pfm", [128, 4096], BF16)
            r1_row, r1_row_b = sbl("r1row", [1, TPC], F32)
            n1_row, n1_row_b = sbl("n1row", [1, TPC], BF16)
            p.dsync.emit(lambda e: e.dma_start(out=r1_row[:], in_=r1f_d[0:1, :]),
                         [rowsd_B], [r1_row_b])
            p.dsync.emit(lambda e: e.dma_start(out=n1_row[:], in_=rows_d[1:2, :]),
                         [rowsd_B], [n1_row_b])
            sc_ps = p4.enter_context(nc.psum_tensor("p4sc", [128, 512], F32))
            sc_b = Buf("p4sc")
            av_ps = [p4.enter_context(nc.psum_tensor(f"p4av{i}", [128, 512], F32))
                     for i in range(4)]
            av_b = [Buf(f"p4av{i}") for i in range(4)]
            tr_ps = p4.enter_context(nc.psum_tensor("p4tr", [128, 1024], BF16))
            tr_b = Buf("p4tr")
            wout_ps = p4.enter_context(nc.psum_tensor("p4wout", [128, 512], F32))
            wout_b = Buf("p4wout")
            lnbc_ps = p4.enter_context(nc.psum_tensor("p4lnbc", [128, 512], F32))
            lnbc_b = Buf("p4lnbc")
            Xd_r = Xd.rearrange("(dc p) t -> p dc t", p=128)

            def emit_A(qg, kb):
                atc, atc_b = at_cache[kb % 2]
                for k8 in range(8):
                    PE.emit(lambda e, qg=qg, kb=kb, k8=k8: e.matmul(
                        sc_ps[:],
                        lhsT=zk_sb[:, (kb * 8 + k8) * 128:(kb * 8 + k8 + 1) * 128],
                        rhs=zq_sb[:, qg * 512:(qg + 1) * 512],
                        start=True, stop=True),
                        [zk_sb_b, zq_sb_b], [sc_b], inc=True)
                    A.emit(lambda e: e.activation(out=rsb[:], in_=sc_ps[:],
                                                  func=AF.Relu, scale=INV_L),
                           [sc_b], [rsb_b])
                    V.emit(lambda e, k8=k8, atc=atc: e.tensor_mul(
                        out=atc[:, k8 * 512:(k8 + 1) * 512], in0=rsb[:], in1=rsb[:]),
                        [rsb_b], [atc_b])

            def load_vblock(kb):
                buf, buf_b = vb[kb % 2]
                r = kb // 2
                src = vall[kb % 2]
                nkt = 8 if kb % 2 == 0 else 9
                srcr = src.rearrange("(r kt p) ee -> p r kt ee", p=128, kt=nkt)
                p.dsync.emit(lambda e, buf=buf, r=r, srcr=srcr: e.dma_start(
                    out=buf[:].rearrange("p (kt ee) -> p kt ee", kt=8),
                    in_=srcr[:, r, 0:8]), [cc_v_done[kb % 2]], [buf_b])

            def emit_B(qg, kb, first):
                atc, atc_b = at_cache[kb % 2]
                buf, buf_b = vb[kb % 2]
                for qt4 in range(4):
                    a4, a4_b = acc[qt4]
                    for ns in range(4):
                        for k8 in range(8):
                            PE.emit(lambda e, qt4=qt4, ns=ns, k8=k8, atc=atc,
                                    buf=buf: e.matmul(
                                av_ps[ns][:],
                                lhsT=atc[:, k8 * 512 + qt4 * 128:
                                         k8 * 512 + (qt4 + 1) * 128],
                                rhs=buf[:, k8 * 2048 + ns * 512:
                                        k8 * 2048 + (ns + 1) * 512],
                                start=(k8 == 0), stop=(k8 == 7)),
                                [atc_b, buf_b], [av_b[ns]], inc=(k8 == 7))
                        if first:
                            V.emit(lambda e, a4=a4, ns=ns: e.tensor_copy(
                                out=a4[:, ns * 512:(ns + 1) * 512],
                                in_=av_ps[ns][:]), [av_b[ns]], [a4_b])
                        else:
                            V.emit(lambda e, a4=a4, ns=ns: e.tensor_add(
                                out=a4[:, ns * 512:(ns + 1) * 512],
                                in0=a4[:, ns * 512:(ns + 1) * 512],
                                in1=av_ps[ns][:]), [av_b[ns], a4_b], [a4_b])

            load_vblock(0)
            load_vblock(1)
            emit_A(0, 0)
            for qg in range(4):
                for kb in range(8):
                    if kb + 1 < 8:
                        emit_A(qg, kb + 1)
                    emit_B(qg, kb, first=(kb == 0))
                    if kb + 2 < 8:
                        load_vblock(kb + 2)
                if qg + 1 < 4:
                    load_vblock(0)
                    load_vblock(1)
                    emit_A(qg + 1, 0)
                for qp in range(2):
                    for sub2 in range(2):
                        qt4 = qp * 2 + sub2
                        qt = qg * 4 + qt4
                        a4, a4_b = acc[qt4]
                        u_t, u_tb = usl[qt % 2]
                        p.dsync.emit(lambda e, u_t=u_t, qt=qt: e.dma_start(
                            out=u_t[:], in_=u_dram[qt * 128:(qt + 1) * 128, :]),
                            [ud_B], [u_tb])
                        V.emit(lambda e, u_t=u_t, a4=a4: e.tensor_mul(
                            out=ptok[:], in0=a4[:], in1=u_t[:]),
                            [a4_b, u_tb], [ptok_b])
                        for half8 in range(2):
                            for j in range(8):
                                ec = half8 * 8 + j
                                PE.emit(lambda e, ec=ec, j=j: e.transpose(
                                    tr_ps[:, j * 128:(j + 1) * 128],
                                    in_=ptok[:, ec * 128:(ec + 1) * 128],
                                    identity=ident[:]),
                                    [ptok_b, ident_b], [tr_b], inc=(j == 7))
                            V.emit(lambda e, half8=half8, sub2=sub2: e.tensor_copy(
                                out=pfm[:].rearrange("p (ec q) -> p ec q", ec=16)
                                [:, half8 * 8:(half8 + 1) * 8,
                                 sub2 * 128:(sub2 + 1) * 128],
                                in_=tr_ps[:].rearrange("p (j q) -> p j q", j=8)),
                                [tr_b], [pfm_b])
                    qt0 = qg * 4 + qp * 2
                    p.dsync.emit(lambda e, qt0=qt0: e.dma_start(
                        out=x1t[:].rearrange("p (dc t) -> p dc t", dc=NDC),
                        in_=Xd_r[:, :, qt0 * 128:(qt0 + 2) * 128]),
                        [Xd_B], [x1t_b])
                    PE.emit(lambda e, qt0=qt0: e.matmul(
                        lnbc_ps[:, 0:256], lhsT=ones1f[:],
                        rhs=r1_row[0:1, qt0 * 128:(qt0 + 2) * 128],
                        start=True, stop=True),
                        [r1_row_b, ones1f_b], [lnbc_b], inc=False)
                    PE.emit(lambda e, qt0=qt0: e.matmul(
                        lnbc_ps[:, 256:512], lhsT=ones1b[:],
                        rhs=n1_row[0:1, qt0 * 128:(qt0 + 2) * 128],
                        start=True, stop=True),
                        [n1_row_b, ones1b_b], [lnbc_b], inc=True)
                    V.emit(lambda e: e.tensor_tensor(
                        out=x1t[:].rearrange("p (dc t) -> p dc t", dc=NDC),
                        in0=x1t[:].rearrange("p (dc t) -> p dc t", dc=NDC),
                        in1=lnbc_ps[:, 0:256].unsqueeze(1)
                        .to_broadcast([128, 8, 256]),
                        op=ALU.mult), [lnbc_b], [x1t_b])
                    V.emit(lambda e: e.tensor_tensor(
                        out=x1t[:].rearrange("p (dc t) -> p dc t", dc=NDC),
                        in0=x1t[:].rearrange("p (dc t) -> p dc t", dc=NDC),
                        in1=lnbc_ps[:, 256:512].unsqueeze(1)
                        .to_broadcast([128, 8, 256]),
                        op=ALU.add), [lnbc_b], [x1t_b])
                    V.emit(lambda e: e.tensor_tensor(
                        out=x1t[:].rearrange("p (dc t) -> p dc t", dc=NDC),
                        in0=x1t[:].rearrange("p (dc t) -> p dc t", dc=NDC),
                        in1=bout[:].unsqueeze(2).to_broadcast([128, 8, 256]),
                        op=ALU.add), [bout_b], [x1t_b])
                    for dcb in range(4):
                        for dc2 in range(2):
                            dc = dcb * 2 + dc2
                            for ec in range(16):
                                PE.emit(lambda e, dc=dc, dc2=dc2, ec=ec: e.matmul(
                                    wout_ps[:, dc2 * 256:(dc2 + 1) * 256],
                                    lhsT=wo_sb[:, (ec * 8 + dc) * 128:
                                               (ec * 8 + dc + 1) * 128],
                                    rhs=pfm[:, ec * 256:(ec + 1) * 256],
                                    start=(ec == 0), stop=(ec == 15)),
                                    [wo_sb_b, pfm_b], [wout_b],
                                    inc=(dc2 == 1 and ec == 15))
                        V.emit(lambda e, dcb=dcb: e.tensor_add(
                            out=x1t[:, dcb * 512:(dcb + 1) * 512],
                            in0=x1t[:, dcb * 512:(dcb + 1) * 512],
                            in1=wout_ps[:]), [wout_b, x1t_b], [x1t_b])
                    p.dact.emit(lambda e, qt0=qt0: e.dma_start(
                        out=Xd_r[:, :, qt0 * 128:(qt0 + 2) * 128],
                        in_=x1t[:].rearrange("p (dc t) -> p dc t", dc=NDC)),
                        [x1t_b], [Xd_B])
        p.barrier()

        if stop_after == "attn":
            ob = Buf("odbg")
            p.dsync.emit(lambda e: e.dma_start(out=out_d, in_=Xd), [Xd_B], [ob])
            p.barrier()
            block = ctx.enter_context(nc.Block())
            p.replay(block)
            return nc
        ffn_pass("f2", Xd, Xd_B, out_d, out_B)

        p.barrier()
        block = ctx.enter_context(nc.Block())
        p.replay(block)
    return nc


STOP_AFTER = None


def kernel(**inputs):
    x = np.asarray(inputs["L0_attn_x"])
    W_in = np.asarray(inputs["L0_attn_W_in"])
    b_in = np.asarray(inputs["L0_attn_b_in"])
    W_out = np.asarray(inputs["L0_attn_W_out"])
    b_out = np.asarray(inputs["L0_attn_b_out"])
    ln_g = np.asarray(inputs["L0_attn_ln_g"])
    ln_b = np.asarray(inputs["L0_attn_ln_b"])
    gq = np.asarray(inputs["L0_attn_gamma_q"])
    bq = np.asarray(inputs["L0_attn_beta_q"])
    gk = np.asarray(inputs["L0_attn_gamma_k"])
    bk = np.asarray(inputs["L0_attn_beta_k"])
    Wr = np.asarray(inputs["L0_ffn_router_weight"])
    rbias = np.asarray(inputs["L0_ffn_router_bias"])
    w_gate = np.asarray(inputs["L0_ffn_w_gate"])
    w_up = np.asarray(inputs["L0_ffn_w_up"])
    w_down = np.asarray(inputs["L0_ffn_w_down"])

    expw = np.empty((NEXP, 128, 12288), np.float32)
    for e in range(NEXP):
        WgT = w_gate[e].T
        WuT = w_up[e].T
        WdT = w_down[e].T
        expw[e, :, 0:4096] = WgT.reshape(8, 128, 512).transpose(1, 0, 2).reshape(128, 4096)
        expw[e, :, 4096:8192] = WuT.reshape(8, 128, 512).transpose(1, 0, 2).reshape(128, 4096)
        expw[e, :, 8192:12288] = WdT.reshape(4, 128, 1024).transpose(1, 0, 2).reshape(128, 4096)
    shared = dict(
        expw=expw,
        winU=np.ascontiguousarray(W_in[:, 0:2048].reshape(8, 128, 2048)),
        winV=np.ascontiguousarray(W_in[:, 2048:4096].reshape(8, 128, 2048)),
        winZ=np.ascontiguousarray(W_in[:, 4096:4224].reshape(8, 128, 128)),
        wout=np.ascontiguousarray(
            W_out.reshape(16, 128, 8, 128).transpose(1, 0, 2, 3).reshape(128, 16384)),
        wr=np.ascontiguousarray(Wr.T.reshape(8, 128, 16)),
        bu=np.ascontiguousarray(b_in[None, 0:2048].astype(np.float32)),
        bv=np.ascontiguousarray(b_in[None, 2048:4096].astype(np.float32)),
        bz=np.ascontiguousarray(b_in[4096:4224].reshape(128, 1)),
        lng=np.ascontiguousarray(ln_g.reshape(8, 128).T),
        lnb=np.ascontiguousarray(ln_b.reshape(8, 128).T),
        bout=np.ascontiguousarray(b_out.reshape(8, 128).T),
        gq=np.ascontiguousarray(gq.reshape(128, 1)),
        bq=np.ascontiguousarray(bq.reshape(128, 1)),
        gk=np.ascontiguousarray(gk.reshape(128, 1)),
        bk=np.ascontiguousarray(bk.reshape(128, 1)),
        rb=np.ascontiguousarray(rbias[None, :].astype(np.float32)),
        ident=np.eye(128, dtype=ml_dtypes.bfloat16),
        onesc=np.ones((128, 1), dtype=ml_dtypes.bfloat16),
        ones1b=np.ones((1, 128), dtype=ml_dtypes.bfloat16),
        ones1f=np.ones((1, 128), dtype=np.float32),
        ones11=np.ones((1, 1), dtype=np.float32),
    )
    in_maps = []
    for c in range(8):
        b = c // 4
        t0 = (c % 4) * TPC
        m = dict(shared)
        m["x"] = np.ascontiguousarray(x[b, t0:t0 + TPC, :].T)
        in_maps.append(m)

    nc = build_nc(stop_after=STOP_AFTER)
    res = run_bass_kernel_spmd(nc, in_maps, core_ids=list(range(8)))
    out = np.empty((2, 8192, 1024), np.float32)
    for c in range(8):
        b = c // 4
        t0 = (c % 4) * TPC
        out[b, t0:t0 + TPC, :] = res.results[c]["out"].T
    return out
